# version 02 (final)
# baseline (speedup 1.0000x reference)
import math
import numpy as np
from contextlib import ExitStack

import concourse.bass as bass
import concourse.mybir as mybir
from concourse.bass_utils import run_bass_kernel_spmd
from concourse.bass_core_rust import IndirectOffsetOnAxis

F32 = mybir.dt.float32
BF16 = mybir.dt.bfloat16
U32 = mybir.dt.uint32
I32 = mybir.dt.int32
AF = mybir.ActivationFunctionType
ALU = mybir.AluOpType
AX = mybir.AxisListType

NCORES = 8
T = 2048
D = 1024
NCH = D // 128
TT = 512
NTT = T // TT

ENGINES = ("sync", "vector", "scalar", "tensor", "gpsimd")


def ceil_div(a, b):
    return (a + b - 1) // b


class Sem:
    def __init__(self, plan, name):
        self.h = plan.stack.enter_context(plan.nc.semaphore(name))
        self.n = 0

    def bump(self, k):
        self.n += k
        return self.n


class Plan:

    def __init__(self, nc):
        self.nc = nc
        self.stack = ExitStack()
        self.ops = {e: [] for e in ENGINES}
        self.sems = {}
        for name in ("ld", "st", "pe", "dv", "ac", "gp", "cc", "bar",
                     "d0", "d1", "d2", "d3", "d4", "d5"):
            self.sems[name] = Sem(self, name)
        self._bar_count = 0
        self.gp_bregs = {}

    def sem(self, name):
        return self.sems[name]

    def emit(self, engine, fn):
        self.ops[engine].append(fn)

    def do(self, engine, meth, *args, wait=None, inc=None, drain_inc=None, **kw):
        waits = tuple((s, (s.n if v is None else v)) for s, v in (wait or ()))
        if inc is not None:
            inc = (inc[0], inc[1], inc[0].bump(inc[1]))
        if drain_inc is not None:
            drain_inc = (drain_inc[0], drain_inc[1], drain_inc[0].bump(drain_inc[1]))

        def f(eng, meth=meth, args=args, kw=kw, waits=waits, inc=inc, drain_inc=drain_inc,
              engine=engine):
            for s, v in waits:
                eng.wait_ge(s.h, v)
            if DEBUG_DRAIN and engine in ("vector", "scalar", "gpsimd") and meth != "dma_start":
                eng.drain()
            kw = {k: (x() if callable(x) else x) for k, x in kw.items()}
            try:
                ins = getattr(eng, meth)(*args, **kw)
            except Exception:
                import sys as _s
                print("EMIT FAIL:", meth, {k: str(v)[:200] for k, v in kw.items()}, file=_s.stderr)
                raise
            if inc is not None:
                ins.then_inc(inc[0].h, inc[1])
            if drain_inc is not None:
                eng.drain().then_inc(drain_inc[0].h, drain_inc[1])
        self.emit(engine, f)

    def bound_reg(self, v):
        h = self.gp_bregs.get(v)
        if h is None:
            h = {}
            self.gp_bregs[v] = h

            def f(eng, v=v, h=h):
                r = eng.alloc_register()
                eng.reg_mov(r, v)
                h["r"] = r
            self.emit("gpsimd", f)
        return lambda: h["r"]

    def wait(self, engine, sem, val=None):
        v = sem.n if val is None else val
        self.emit(engine, lambda eng, s=sem, v=v: eng.wait_ge(s.h, v))

    def barrier(self):
        b = self.sems["bar"]
        self._bar_count += 1
        target = 5 * self._bar_count
        for e in ENGINES:
            if e == "sync":
                def f(eng, b=b, t=target):
                    eng.sem_inc(b.h, 1)
                    eng.wait_ge(b.h, t)
            else:
                def f(eng, b=b, t=target):
                    eng.drain()
                    eng.sem_inc(b.h, 1)
                    eng.wait_ge(b.h, t)
            self.emit(e, f)
        b.n = target

    def sbuf(self, stack, name, shape, dtype, side=None):
        return stack.enter_context(self.nc.sbuf_tensor(name, shape, dtype, side=side))



def ln_feature_major(p, G, r_chunks, out_f32, out_bf16, tmp, ready=()):
    psA, psB = G["psA"], G["psB"]
    ones, onesrow = G["ones"], G["onesrow"]
    rowmu, rowrs, eps1 = G["rowmu"], G["rowrs"], G["eps1"]
    pe, dv, ac = p.sem("pe"), p.sem("dv"), p.sem("ac")
    for s, v in ready:
        p.wait("tensor", s, v)
        p.wait("scalar", s, v)
        p.wait("vector", s, v)

    for tt in range(NTT):
        sl = slice(tt * TT, (tt + 1) * TT)
        for c in range(NCH):
            p.do("tensor", "matmul", psA[0:1, sl], ones[:], r_chunks[c][:, sl],
                 start=(c == 0), stop=(c == NCH - 1))
    for c in range(NCH):
        p.do("scalar", "activation", tmp[:], r_chunks[c], AF.Square,
             drain_inc=(ac, 1))
        for tt in range(NTT):
            sl = slice(tt * TT, (tt + 1) * TT)
            p.do("tensor", "matmul", psA[32:33, sl], ones[:], tmp[:, sl],
                 start=(c == 0), stop=(c == NCH - 1),
                 wait=[(ac, None)] if tt == 0 else None,
                 inc=(pe, 1) if tt == NTT - 1 else None)
        if c < NCH - 1:
            p.wait("scalar", pe, pe.n)
    p.do("vector", "tensor_scalar_mul", rowmu[0:1, :], psA[0:1, 0:2048], 1.0 / D,
         wait=[(pe, None)])
    p.do("vector", "tensor_mul", rowrs[0:1, :], rowmu[0:1, :], rowmu[0:1, :])
    p.do("vector", "scalar_tensor_tensor", rowrs[0:1, :], psA[32:33, 0:2048],
         1.0 / D, rowrs[0:1, :], ALU.mult, ALU.subtract)
    p.do("vector", "tensor_single_scalar", rowrs[0:1, :], rowrs[0:1, :],
         eps1[0:1, 0:1], ALU.add, drain_inc=(dv, 1))
    p.do("scalar", "activation", rowrs[0:1, :], rowrs[0:1, :], AF.Ln,
         wait=[(dv, None)], drain_inc=(ac, 1))
    p.do("scalar", "activation", rowrs[0:1, :], rowrs[0:1, :], AF.Exp,
         scale=-0.5, drain_inc=(ac, 1))
    p.wait("tensor", ac, ac.n)
    for tt in range(NTT):
        sl = slice(tt * TT, (tt + 1) * TT)
        p.do("tensor", "matmul", psA[:, sl], onesrow[0:1, :], rowmu[0:1, sl],
             start=True, stop=True)
        p.do("tensor", "matmul", psB[:, sl], onesrow[0:1, :], rowrs[0:1, sl],
             start=True, stop=True,
             inc=(pe, 1) if tt == NTT - 1 else None)
    p.wait("vector", pe, pe.n)
    for c in range(NCH):
        p.do("vector", "tensor_sub", tmp[:], r_chunks[c], psA[:, 0:2048])
        p.do("vector", "tensor_mul", out_f32[c], tmp[:], psB[:, 0:2048],
             inc=(dv, 1))
        p.do("scalar", "activation", out_bf16[c], out_f32[c], AF.Copy,
             wait=[(dv, None)], inc=(ac, 1))
    p.wait("tensor", ac, ac.n)



def emit_weight_load(p, stk, name, wT_ap, n_in=NCH, n_out=1024, wf_stk=None,
                     ld_sem=None):
    ld, ac = ld_sem or p.sem("ld"), p.sem("ac")
    wf = p.sbuf(wf_stk or stk, name + "_f", [128, n_out], F32,
                side="right" if wf_stk is not None else None)
    wb = p.sbuf(stk, name + "_b", [128, n_in * n_out], BF16)
    pe, dv = p.sem("pe"), p.sem("dv")
    for ci in range(n_in):
        w = [(ac, None)] if ci > 0 else [(pe, None), (dv, None)]
        p.do("sync", "dma_start", out=wf[:], in_=wT_ap[ci * 128:(ci + 1) * 128, :],
             wait=w, inc=(ld, 16))
        p.do("scalar", "activation", wb[:, ci * n_out:(ci + 1) * n_out], wf[:],
             AF.Copy, wait=[(ld, None)], drain_inc=(ac, 1))
    return wb


def emit_proj_fm(p, G, wb, dst, dst_is_dram, stage=None, rhs=None, n_out=NCH,
                 out_bf=True, ksq_to=None, onesb=None, ksq_tmp=None,
                 out_scale=None, w_stride=1024):
    pe, dv, ac = p.sem("pe"), p.sem("dv"), p.sem("ac")
    dsem = [p.sem("d0"), p.sem("d1")]
    rhs = rhs or G["xb"]
    evac_dv = {}
    dv0 = dv.n
    for co in range(n_out):
        ps = G["psA"] if co % 2 == 0 else G["psB"]
        for tt in range(NTT):
            sl = slice(tt * TT, (tt + 1) * TT)
            w = []
            if tt == 0:
                w.append((dv, evac_dv[co - 2] if co >= 2 else dv0))
            for ci in range(NCH):
                p.do("tensor", "matmul", ps[:, sl],
                     wb[:, ci * w_stride + co * 128: ci * w_stride + (co + 1) * 128],
                     rhs[ci][:, sl], start=(ci == 0), stop=(ci == NCH - 1),
                     wait=w if ci == 0 else None,
                     inc=(pe, 1) if (tt == NTT - 1 and ci == NCH - 1) else None)
        if dst_is_dram:
            sg = stage[co % 2]
            w = [(pe, None), (dsem[co % 2], None)]
            if out_scale is None:
                p.do("vector", "tensor_copy", wait=w, inc=(dv, 1),
                     out=sg[:], in_=ps[:, 0:T])
            else:
                p.do("vector", "tensor_scalar_mul", sg[:], ps[:, 0:T], out_scale,
                     wait=w, inc=(dv, 1))
            p.do("sync", "dma_start", out=dst[co * 128:(co + 1) * 128, :],
                 in_=sg[:], wait=[(dv, None)], inc=(dsem[co % 2], 16))
        else:
            p.do("vector", "tensor_copy", wait=[(pe, None)], inc=(dv, 1),
                 out=dst[co], in_=ps[:, 0:T])
        evac_dv[co] = dv.n
        if ksq_to is not None:
            kmrow, hbase = ksq_to
            src = stage[co % 2][:] if dst_is_dram else dst[co]
            p.do("vector", "tensor_mul", ksq_tmp[:], src, src, inc=(dv, 1))
            for hh in range(2):
                pslice = slice(hh * 64, hh * 64 + 64)
                for tt in range(NTT):
                    sl = slice(tt * TT, (tt + 1) * TT)
                    p.do("tensor", "matmul", ps[0:1, sl], onesb[pslice, 0:1],
                         ksq_tmp[pslice, sl], start=True, stop=True,
                         wait=[(dv, None)] if tt == 0 else None,
                         inc=(pe, 1) if tt == NTT - 1 else None)
                h = hbase + 2 * co + hh
                p.do("vector", "reduce_max", kmrow[0:1, h:h + 1], ps[0:1, 0:T],
                     AX.X, wait=[(pe, None)], inc=(dv, 1))
            evac_dv[co] = dv.n
    return evac_dv[n_out - 2], evac_dv[n_out - 1]



L0A_PHASE = 99
L0A_SUB = "qkv"
L1A_PHASE = 99
L3A_PHASE = 99
L3A_SUB = "vks"
L3A_NC = 16
L1A_SUB = "qkvg"
L1A_GCUT = 9
DEBUG_DRAIN = False
RACE_DETECT = True
DEBUG_TAPS = False
TAPS_SEL = "ABCDEF"


def build_l0_attn(p, G):
    nc = p.nc
    ld, st_s, pe, dv, ac, gp, cc = [p.sem(s) for s in
                                    ("ld", "st", "pe", "dv", "ac", "gp", "cc")]
    psA, psB = G["psA"], G["psB"]
    NH, HD = 16, 64
    q_dr = nc.dram_tensor("l0a_q", [D, T], BF16).ap()
    kin = nc.dram_tensor("l0a_kin", [D, T], BF16).ap()
    kout = nc.dram_tensor("l0a_kout", [4 * D, T], BF16).ap()
    vin = nc.dram_tensor("l0a_vin", [T, D], BF16).ap()
    vout = nc.dram_tensor("l0a_vout", [4 * T, D], BF16).ap()
    ksq_in = nc.dram_tensor("l0a_ksqi", [1, 16], F32).ap()
    ksq_out = nc.dram_tensor("l0a_ksqo", [8, 16], F32).ap()
    GROUPS2 = [[0, 1, 2, 3], [4, 5, 6, 7]]
    GROUPS8 = [[0, 1, 2, 3, 4, 5, 6, 7]]

    with ExitStack() as stk0:
        O_fm = [p.sbuf(stk0, f"l0a_O{c}", [128, T], BF16) for c in range(NCH)]
        onesb = p.sbuf(stk0, "l0a_onesb", [128, 1], BF16)
        kmrow = p.sbuf(stk0, "l0a_km", [1, 16], F32)
        p.do("vector", "memset", onesb[:], 1.0)

        with ExitStack() as stk:
            stage = [p.sbuf(stk, f"l0a_stg{i}", [128, T], BF16) for i in range(2)]
            k2 = p.sbuf(stk, "l0a_k2", [128, T], BF16)
            kml = p.sbuf(stk, "l0a_kml", [1, 16], F32)
            wlist = []
            if "q" in L0A_SUB:
                wlist.append(("L0a_wqT", q_dr))
            if "k" in L0A_SUB:
                wlist.append(("L0a_wkT", kin))
            for wname, target in wlist:
                with ExitStack() as wstk:
                    wb = emit_weight_load(p, wstk, wname, G["inp"][wname])
                    p.wait("tensor", p.sem("ac"), None)
                    if wname == "L0a_wkT":
                        emit_proj_fm(p, G, wb, target, True, stage=stage,
                                     ksq_to=(kml[:], 0), onesb=onesb[:], ksq_tmp=k2[:])
                    else:
                        emit_proj_fm(p, G, wb, target, True, stage=stage,
                                     out_scale=0.125)
            with ExitStack() as wstk:
                wvb = None if "v" not in L0A_SUB else emit_weight_load(p, wstk, "L0a_wvT", G["inp"]["L0a_wvT"])
                p.wait("tensor", p.sem("ac"), None)
                dsem = [p.sem("d0"), p.sem("d1")]
                evac_v = {}
                for tch in range(16 if wvb is not None else 0):
                    ps = psA if tch % 2 == 0 else psB
                    sg = stage[tch % 2]
                    for vt in range(2):
                        sl = slice(vt * 512, (vt + 1) * 512)
                        w = []
                        if tch >= 2 and vt == 0:
                            w.append((dv, evac_v[tch - 2]))
                        for ci in range(NCH):
                            p.do("tensor", "matmul", ps[:, sl],
                                 G["xb"][ci][:, tch * 128:(tch + 1) * 128],
                                 wvb[:, ci * 1024 + vt * 512: ci * 1024 + (vt + 1) * 512],
                                 start=(ci == 0), stop=(ci == NCH - 1),
                                 wait=w if ci == 0 else None,
                                 inc=(pe, 1) if (vt == 1 and ci == NCH - 1) else None)
                    p.do("vector", "tensor_copy", out=sg[:, 0:1024], in_=ps[:, 0:1024],
                         wait=[(pe, None), (dsem[tch % 2], None)], inc=(dv, 1))
                    evac_v[tch] = dv.n
                    p.do("sync", "dma_start", out=vin[tch * 128:(tch + 1) * 128, :],
                         in_=sg[:, 0:1024], wait=[(dv, None)], inc=(dsem[tch % 2], 16))
            p.do("scalar", "dma_start", out=ksq_in[:, :], in_=kml[:],
                 wait=[(dv, None)], inc=(st_s, 16))
        if L0A_PHASE < 2:
            p.wait("sync", st_s, None)
            p.barrier()
            return
        p.wait("gpsimd", st_s, None)
        p.wait("gpsimd", p.sem("d0"), None)
        p.wait("gpsimd", p.sem("d1"), None)
        p.do("gpsimd", "collective_compute", "AllGather", ALU.bypass,
             replica_groups=GROUPS2, ins=[kin.opt()], outs=[kout.opt()],
             inc=(cc, 1))
        p.do("gpsimd", "collective_compute", "AllGather", ALU.bypass,
             replica_groups=GROUPS2, ins=[vin.opt()], outs=[vout.opt()],
             inc=(cc, 1))
        p.do("gpsimd", "collective_compute", "AllGather", ALU.bypass,
             replica_groups=GROUPS8, ins=[ksq_in.opt()], outs=[ksq_out.opt()],
             inc=(cc, 1))

        if L0A_PHASE < 3:
            p.wait("gpsimd", cc, None)
            p.barrier()
            return
        with ExitStack() as stk:
            sq32 = p.sbuf(stk, "l0a_sq32", [32, 64], F32)
            p.do("vector", "memset", sq32[:], 0.0, wait=[(cc, None)])
            p.do("sync", "dma_start", out=sq32[0:8, 0:16], in_=ksq_out[:, :],
                 wait=[(cc, None)], inc=(ld, 16))
            p.do("vector", "transpose", sq32[:, 32:64], sq32[:, 0:32],
                 wait=[(ld, None)])
            p.do("vector", "reduce_max", sq32[:, 0:1], sq32[:, 32:64], AX.X)
            p.do("vector", "tensor_copy", sq32[:, 32:64], sq32[:, 0:32])
            p.do("vector", "transpose", sq32[:, 0:32], sq32[:, 32:64])
            p.do("vector", "tensor_copy", kmrow[0:1, 0:16], sq32[0:1, 0:16],
                 drain_inc=(dv, 1))
        km_ready = dv.n
        if L0A_PHASE < 4:
            p.wait("vector", dv, None)
            p.barrier()
            return

        with ExitStack() as stk:
            Kx = [p.sbuf(stk, f"l0a_Kx{i}", [65, 4 * T], BF16) for i in range(2)]
            Vx = [p.sbuf(stk, f"l0a_Vx{i}", [128, 64 * 65], BF16) for i in range(2)]
            Qx0 = p.sbuf(stk, "l0a_Qx", [65, T], BF16)
            Qx = [Qx0, Qx0]
            Pt = [p.sbuf(stk, f"l0a_Pt{i}", [128, T], BF16) for i in range(2)]
            rowm = G["rowmu"]
            rowm2 = G["rowrs"]
            rl = G["rowmu"]
            for i in range(2):
                p.do("vector", "memset", Kx[i][64:65, :], 1.0)
                p.do("vector", "memset",
                     Vx[i][:].rearrange("p (k e) -> p k e", e=65)[:, :, 64:65], 1.0,
                     drain_inc=(dv, 1) if i == 1 else None)
            ones_ready = dv.n

            ksem = [p.sem("d2"), p.sem("d3")]
            vsem = [p.sem("gp"), p.sem("d5")]
            qsem = p.sem("d4")

            def prefetch(h):
                pi = h % 2
                w = []
                if h < 2:
                    w = [(cc, cc.n), (dv, ones_ready)]
                else:
                    w = [(pe, head_mm_done[h - 2])]
                for r in range(4):
                    p.do("sync", "dma_start",
                         out=Kx[pi][0:64, r * T:(r + 1) * T],
                         in_=kout[r * D + HD * h: r * D + HD * h + HD, :],
                         wait=w if r == 0 else None, inc=(ksem[pi], 16))
                vx_ap = Vx[pi][:].rearrange("p (k e) -> p k e", e=65)
                for r in range(4):
                    for s in range(4):
                        kcg = r * 16 + s * 4
                        src = vout[r * T + s * 512: r * T + (s + 1) * 512,
                                   HD * h: HD * h + HD]
                        p.do("gpsimd", "dma_start",
                             out=vx_ap[:, kcg:kcg + 4, 0:64],
                             in_=src.rearrange("(c q) d -> q c d", q=128),
                             wait=w if (r == 0 and s == 0) else None,
                             inc=(vsem[pi], 16))
                return ksem[pi].n, vsem[pi].n

            head_mm_done = {}
            head_ld = {}
            q_ld = {}

            def q_load(h, wait_pe):
                w = [(pe, wait_pe)] if wait_pe is not None else [(cc, cc.n)]
                p.do("sync", "dma_start", out=Qx[0][0:64, :],
                     in_=q_dr[HD * h: HD * h + HD, :], wait=w, inc=(qsem, 16))
                q_ld[h] = qsem.n

            prev_exp_ac = None
            prev_evac_dv = None
            for h in range(NH):
                if h == 0:
                    head_ld[0] = prefetch(0)
                    q_load(0, None)
                if h + 1 < NH:
                    head_ld[h + 1] = prefetch(h + 1)
                pi = h % 2
                ldn, gpn = head_ld[h]
                q2 = Pt[0][0:64, :]
                wq = [(ksem[pi], ldn), (qsem, q_ld[h]), (dv, km_ready)]
                if prev_exp_ac is not None:
                    wq.append((ac, prev_exp_ac))
                if h > 0:
                    wq.append((pe, head_mm_done[h - 1]))
                p.do("vector", "tensor_mul", q2, Qx[pi][0:64, :], Qx[pi][0:64, :],
                     wait=wq, drain_inc=(dv, 1))
                wpsA = [(dv, None)]
                if prev_evac_dv is not None:
                    wpsA.append((dv, prev_evac_dv))
                for tt in range(NTT):
                    sl = slice(tt * TT, (tt + 1) * TT)
                    p.do("tensor", "matmul", psA[0:1, sl], onesb[0:64, 0:1],
                         q2[:, sl], start=True, stop=True,
                         wait=wpsA if tt == 0 else None,
                         inc=(pe, 1) if tt == NTT - 1 else None)
                p.do("vector", "tensor_single_scalar", rowm[0:1, 0:T],
                     psA[0:1, 0:T], kmrow[0:1, h:h + 1], ALU.mult,
                     wait=[(pe, None)], drain_inc=(dv, 1))
                p.do("scalar", "activation", rowm2[0:1, 0:T], rowm[0:1, 0:T],
                     AF.Sqrt, wait=[(dv, None)],
                     drain_inc=(ac, 1))
                p.do("vector", "tensor_single_scalar", rowm[0:1, 0:T],
                     rowm2[0:1, 0:T], -1.0, ALU.mult, wait=[(ac, None)])
                p.do("vector", "tensor_single_scalar", rowm[0:1, 0:T],
                     rowm[0:1, 0:T], -0.5, ALU.add)
                p.do("vector", "tensor_copy", Qx[pi][64:65, :], rowm[0:1, 0:T],
                     drain_inc=(dv, 1))
                mhat_dv = dv.n
                exp_ac_half = {}
                pv_pe = {}
                s_pe = {}
                for kc in range(64):
                    for half in range(2):
                        qsl = slice(half * 1024, (half + 1) * 1024)
                        ps_half = psA[:, qsl]
                        w = [(dv, mhat_dv)] if kc == 0 else [(ac, exp_ac_half[(kc - 1, half)])]
                        if kc == 0 and half == 0 and prev_evac_dv is not None:
                            w.append((dv, prev_evac_dv))
                        for j in range(2):
                            ssl = slice(half * 1024 + j * 512, half * 1024 + (j + 1) * 512)
                            p.do("tensor", "matmul", psA[:, ssl],
                                 Kx[pi][0:65, kc * 128:(kc + 1) * 128],
                                 Qx[pi][0:65, ssl], start=True, stop=True,
                                 wait=w if j == 0 else None,
                                 inc=(pe, 1) if j == 1 else None)
                        s_pe[(kc, half)] = pe.n
                        we = [(pe, s_pe[(kc, half)])]
                        if kc >= 2:
                            we.append((pe, pv_pe[kc - 2]))
                        if kc == 1 and prev_evac_dv is not None:
                            we.append((dv, prev_evac_dv))
                        p.do("scalar", "activation", Pt[kc % 2][:, qsl], ps_half,
                             AF.Exp, wait=we, inc=(ac, 1))
                        exp_ac_half[(kc, half)] = ac.n
                    wv = [(ac, exp_ac_half[(kc, 1)])]
                    if kc == 0:
                        wv.append((vsem[pi], gpn))
                        if prev_evac_dv is not None:
                            wv.append((dv, prev_evac_dv))
                    for tt in range(NTT):
                        sl = slice(tt * TT, (tt + 1) * TT)
                        p.do("tensor", "matmul", psB[0:65, sl],
                             Vx[pi][:, kc * 65:(kc + 1) * 65],
                             Pt[kc % 2][:, sl], start=(kc == 0), stop=(kc == 63),
                             wait=wv if tt == 0 else None,
                             inc=(pe, 1) if tt == NTT - 1 else None)
                    pv_pe[kc] = pe.n
                head_mm_done[h] = pe.n
                if h + 1 < NH:
                    q_load(h + 1, head_mm_done[h])
                p.do("vector", "reciprocal", rl[0:1, 0:T], psB[64:65, 0:T],
                     wait=[(pe, None)], drain_inc=(dv, 1))
                for tt in range(NTT):
                    sl = slice(tt * TT, (tt + 1) * TT)
                    p.do("tensor", "matmul", psA[0:64, sl], G["onesrow"][0:1, 0:64],
                         rl[0:1, sl], start=True, stop=True,
                         wait=[(dv, None), (ac, exp_ac_half[(63, 1)])] if tt == 0 else None,
                         inc=(pe, 1) if tt == NTT - 1 else None)
                tmpO = Pt[1][0:64, :].bitcast(F32)
                for hf in range(2):
                    hs = slice(hf * (T // 2), (hf + 1) * (T // 2))
                    p.do("vector", "tensor_copy", tmpO, psB[0:64, hs],
                         wait=[(pe, None)] if hf == 0 else None)
                    p.do("vector", "tensor_mul",
                         O_fm[h // 2][(h % 2) * 64:(h % 2) * 64 + 64, hs],
                         tmpO, psA[0:64, hs], inc=(dv, 1) if hf == 1 else None)
                prev_evac_dv = dv.n
                prev_exp_ac = exp_ac_half[(63, 1)]

        if L0A_PHASE < 5:
            p.barrier()
            return
        if DEBUG_TAPS:
            dbgO = nc.dram_tensor("dbgO", [D, T], BF16, kind="ExternalOutput").ap()
            st_d = p.sem("st")
            for c in range(NCH):
                p.do("sync", "dma_start", out=dbgO[c * 128:(c + 1) * 128, :],
                     in_=O_fm[c][:], wait=[(dv, prev_evac_dv)], inc=(st_d, 16))
        with ExitStack() as stk:
            tmp = p.sbuf(stk, "l0a_tmp", [128, T], F32)
            with ExitStack() as wstk:
                wob = emit_weight_load(p, wstk, "L0a_woT", G["inp"]["L0a_woT"])
                p.wait("tensor", p.sem("ac"), None)
                p.wait("tensor", dv, prev_evac_dv)
                evac_dv = {}
                for co in range(NCH):
                    ps = psA if co % 2 == 0 else psB
                    for tt in range(NTT):
                        sl = slice(tt * TT, (tt + 1) * TT)
                        w = [(dv, evac_dv[co - 2])] if (co >= 2 and tt == 0) else None
                        for ci in range(NCH):
                            p.do("tensor", "matmul", ps[:, sl],
                                 wob[:, ci * 1024 + co * 128: ci * 1024 + (co + 1) * 128],
                                 O_fm[ci][:, sl], start=(ci == 0), stop=(ci == NCH - 1),
                                 wait=w if ci == 0 else None,
                                 inc=(pe, 1) if (tt == NTT - 1 and ci == NCH - 1) else None)
                    p.do("vector", "tensor_add", G["xs"][co], G["xs"][co],
                         ps[:, 0:T], wait=[(pe, None)], inc=(dv, 1))
                    evac_dv[co] = dv.n
            ln_feature_major(p, G, G["xs"], G["xs"], G["xb"], tmp[:],
                             ready=[(dv, dv.n)])
    p.barrier()



class _SkipRest(Exception):
    pass


def build_l1_attn(p, G):
    nc = p.nc
    ld, st_s, pe, dv, ac, gp, cc = [p.sem(s) for s in
                                    ("ld", "st", "pe", "dv", "ac", "gp", "cc")]
    psA, psB = G["psA"], G["psB"]
    H, DK, DV = 4, 128, 256
    NC = 16
    vdr = nc.dram_tensor("l1a_v", [T, D], BF16).ap()
    gdr = nc.dram_tensor("l1a_gd", [T, 512], BF16).ap()
    sin = nc.dram_tensor("l1a_sin", [128, 1028], F32).ap()
    sout = nc.dram_tensor("l1a_sout", [8 * 128, 1028], F32).ap()
    GROUPS8 = [[0, 1, 2, 3, 4, 5, 6, 7]]
    inp = G["inp"]
    with ExitStack() as stk0:
        qkstk = ExitStack()
        q_fm = p.sbuf(qkstk, "l1a_q", [128, H * T], BF16)
        k_fm = p.sbuf(qkstk, "l1a_k", [128, H * T], BF16)
        gt_buf = [p.sbuf(qkstk, f"l1a_gt{i}", [128, 512], BF16) for i in range(2)]
        Sst = p.sbuf(qkstk, "l1a_S", [128, H * DV], F32)
        Sbf = p.sbuf(qkstk, "l1a_Sbf", [128, DV], BF16)
        S0 = p.sbuf(qkstk, "l1a_S0", [128, H * DV + 4], F32)
        Atot = p.sbuf(qkstk, "l1a_At", [128, 8], F32)
        idt = G["idt"]
        u128 = G["u128"]

        with ExitStack() as stk:
            stage = [p.sbuf(stk, f"l1a_stg{i}", [128, T], BF16) for i in range(2)]
            wl = []
            if "q" in L1A_SUB:
                wl.append(("l1a_wq", q_fm))
            if "k" in L1A_SUB:
                wl.append(("l1a_wk", k_fm))
            for wname, dstt in wl:
                with ExitStack() as wstk:
                    wb = emit_weight_load(p, wstk, wname, inp[wname], n_out=512)
                    p.wait("tensor", ac, None)
                    dst = [dstt[:, co * T:(co + 1) * T] for co in range(4)]
                    emit_proj_fm(p, G, wb, dst, False, n_out=4, w_stride=512)
            with ExitStack() as wstk:
                wvb = emit_weight_load(p, wstk, "l1a_wv", inp["l1a_wv"]) if "v" in L1A_SUB else None
                p.wait("tensor", ac, None)
                dsem = [p.sem("d0"), p.sem("d1")]
                evac_v = {}
                for tch in range(16 if wvb is not None else 0):
                    ps = psA if tch % 2 == 0 else psB
                    sg = stage[tch % 2]
                    for vt in range(2):
                        sl = slice(vt * 512, (vt + 1) * 512)
                        w = [(dv, None)] if tch < 2 and vt == 0 else []
                        if tch >= 2 and vt == 0:
                            w.append((dv, evac_v[tch - 2]))
                        for ci in range(NCH):
                            p.do("tensor", "matmul", ps[:, sl],
                                 G["xb"][ci][:, tch * 128:(tch + 1) * 128],
                                 wvb[:, ci * 1024 + vt * 512: ci * 1024 + (vt + 1) * 512],
                                 start=(ci == 0), stop=(ci == NCH - 1),
                                 wait=w if ci == 0 else None,
                                 inc=(pe, 1) if (vt == 1 and ci == NCH - 1) else None)
                    p.do("vector", "tensor_copy", out=sg[:, 0:1024], in_=ps[:, 0:1024],
                         wait=[(pe, None), (dsem[tch % 2], None)], inc=(dv, 1))
                    evac_v[tch] = dv.n
                    p.do("sync", "dma_start", out=vdr[tch * 128:(tch + 1) * 128, :],
                         in_=sg[:, 0:1024], wait=[(dv, None)], inc=(dsem[tch % 2], 16))
            with ExitStack() as wstk:
                w1f = p.sbuf(wstk, "l1a_w1f", [128, 16], F32)
                w1b = p.sbuf(wstk, "l1a_w1b", [128, NCH * 16], BF16)
                w2 = p.sbuf(wstk, "l1a_w2", [16, 512], BF16)
                w2f = p.sbuf(wstk, "l1a_w2f", [16, 512], F32)
                bg = p.sbuf(wstk, "l1a_bg", [1, 512], F32)
                bgb = p.sbuf(wstk, "l1a_bgb", [1, 512], BF16)
                zt = p.sbuf(wstk, "l1a_z", [16, T], BF16)
                sc1 = p.sbuf(wstk, "l1a_sc1", [128, T], F32)
                sc2 = p.sbuf(wstk, "l1a_sc2", [128, T], F32)
                for ci in range(NCH if "g" in L1A_SUB else 0):
                    p.do("sync", "dma_start", out=w1f[:],
                         in_=inp["l1a_wgk1"][ci * 128:(ci + 1) * 128, :],
                         wait=[(ac, None)] if ci > 0 else [(pe, None), (dv, None)],
                         inc=(ld, 16))
                    p.do("scalar", "activation", w1b[:, ci * 16:(ci + 1) * 16], w1f[:],
                         AF.Copy, wait=[(ld, None)], drain_inc=(ac, 1))
                if "g" in L1A_SUB:
                    p.do("sync", "dma_start", out=w2f[:], in_=inp["l1a_wgk2"][:, :],
                         wait=[(ac, None)], inc=(ld, 16))
                    p.do("sync", "dma_start", out=bg[:], in_=inp["l1a_bgk2"][:, :],
                         inc=(ld, 16))
                    p.do("scalar", "activation", w2[:], w2f[:], AF.Copy,
                         wait=[(ld, None)])
                    p.do("scalar", "activation", bgb[:], bg[:], AF.Copy,
                         drain_inc=(ac, 1))
                    p.wait("tensor", ac, None)
                    p.wait("tensor", dv, None)
                for tt in range(NTT if "g" in L1A_SUB else 0):
                    sl = slice(tt * TT, (tt + 1) * TT)
                    for ci in range(NCH):
                        p.do("tensor", "matmul", psA[0:16, sl],
                             w1b[:, ci * 16:(ci + 1) * 16], G["xb"][ci][:, sl],
                             start=(ci == 0), stop=(ci == NCH - 1),
                             inc=(pe, 1) if ci == NCH - 1 else None)
                if "g" in L1A_SUB:
                    p.do("vector", "tensor_copy", zt[:], psA[0:16, 0:T],
                         wait=[(pe, None)], drain_inc=(dv, 1))
                for grp in range((4 if "g" in L1A_SUB else 0) if L1A_GCUT >= 2 else 0):
                    w = [(dv, None)]
                    for j in range(4):
                        tch = grp * 4 + j
                        sl = slice(j * 512, (j + 1) * 512)
                        p.do("tensor", "matmul", psB[:, sl],
                             zt[:, tch * 128:(tch + 1) * 128], w2[:],
                             start=True, stop=False, wait=w if j == 0 else None)
                        p.do("tensor", "matmul", psB[:, sl],
                             G["onesrowb"][0:1, :], bgb[:],
                             start=False, stop=True,
                             inc=(pe, 1) if j == 3 else None)
                    p.do("scalar", "activation", sc2[:], psB[:, 0:T], AF.Abs,
                         wait=[(pe, None)], drain_inc=(ac, 1))
                    p.do("vector", "tensor_scalar_min", sc1[:], psB[:, 0:T], 0.0,
                         wait=[(ac, None)])
                    p.do("scalar", "activation", sc2[:], sc2[:], AF.Exp,
                         scale=-1.0, drain_inc=(ac, 1))
                    p.do("scalar", "activation", sc2[:], sc2[:], AF.Ln,
                         bias=G["ones"], drain_inc=(ac, 1))
                    p.do("vector", "tensor_sub", sc1[:], sc1[:], sc2[:],
                         wait=[(ac, None)], drain_inc=(dv, 1))
                    p.do("vector", "tensor_scalar_mul", sc1[:], sc1[:], 1.0 / 16.0)
                    p.do("vector", "tensor_scalar_max", sc2[:].bitcast(BF16)[:, 0:T],
                         sc1[:], -3.0, drain_inc=(dv, 1))
                    if L1A_GCUT >= 4:
                        p.do("sync", "dma_start",
                             out=gdr.rearrange("(c q) k -> q c k", q=128)[:, grp * 4:(grp + 1) * 4, :],
                             in_=sc2[:].bitcast(BF16)[:, 0:T].rearrange("q (c k) -> q c k", c=4),
                             wait=[(dv, None)], inc=(p.sem("d4"), 16))
        proj_dv = dv.n
        proj_pe = pe.n
        if L1A_PHASE < 2:
            qkstk.close()
            p.barrier()
            return
        O_fm = [p.sbuf(stk0, f"l1a_O{c}", [128, T], BF16, side="right") for c in range(NCH)]
        rmsw = p.sbuf(qkstk, "l1a_rmswb", [128, DV], BF16)
        rw = p.sbuf(qkstk, "l1a_rw", [1, 256], F32)
        eA = p.sbuf(qkstk, "l1a_eA", [128, 512], BF16)
        emA = p.sbuf(qkstk, "l1a_emA", [128, 512], BF16)
        eAl = p.sbuf(qkstk, "l1a_eAl", [128, 4], F32)
        qt_t = p.sbuf(qkstk, "l1a_qt", [128, 512], BF16)
        kt_t = p.sbuf(qkstk, "l1a_kt", [128, 512], BF16)
        kh_t = p.sbuf(qkstk, "l1a_kh", [128, 512], BF16)
        Pt = p.sbuf(qkstk, "l1a_Pt", [128, 128], BF16)
        on = p.sbuf(qkstk, "l1a_on", [128, 256], BF16)
        sq = p.sbuf(qkstk, "l1a_sq", [128, 256], F32)
        ms = p.sbuf(qkstk, "l1a_ms", [128, 2], F32)
        vt_buf = [p.sbuf(qkstk, f"l1a_vt{i}", [128, D], BF16) for i in range(2)]

        vsem = [p.sem("d2"), p.sem("d3")]
        vt_state = {}
        gt_state = {}

        def vload(c, wait=None):
            p.do("sync", "dma_start", out=vt_buf[c % 2][:],
                 in_=vdr[c * 128:(c + 1) * 128, :], wait=wait,
                 inc=(vsem[c % 2], 16))
            vt_state[c] = vsem[c % 2].n
            p.do("sync", "dma_start", out=gt_buf[c % 2][:],
                 in_=gdr[c * 128:(c + 1) * 128, :], wait=wait,
                 inc=(vsem[c % 2], 16))
            gt_state[c] = vsem[c % 2].n

        def decay_tiles(c, atot):
            for h in range(H):
                p.do("tensor", "matmul", psA[:, h * 128:(h + 1) * 128],
                     gt_buf[c % 2][:, h * 128:(h + 1) * 128], u128,
                     start=True, stop=True,
                     wait=[(dv, None), (ac, None), (vsem[c % 2], gt_state[c])]
                     if h == 0 else None,
                     inc=(pe, 1) if h == 3 else None)
            p.do("scalar", "activation", eA[:], psA[:, 0:512], AF.Exp,
                 wait=[(pe, None)], drain_inc=(ac, 1))
            p.do("scalar", "activation", emA[:], psA[:, 0:512], AF.Exp, scale=-1.0,
                 drain_inc=(ac, 1))
            ap_last = psA[:, 0:512].rearrange("p (h t) -> p h t", h=4)[:, :, 127:128].squeeze(2)
            p.do("scalar", "activation", eAl[:], ap_last, AF.Exp, drain_inc=(ac, 1))
            if atot:
                p.do("vector", "tensor_copy", Atot[:, 4:8], ap_last,
                     wait=[(ac, None)])
                p.do("vector", "tensor_add", Atot[:, 0:4], Atot[:, 0:4],
                     Atot[:, 4:8], drain_inc=(dv, 1))
            return ac.n

        def state_update(c, h, ex):
            p.do("vector", "tensor_mul", kt_t[:, h * 128:(h + 1) * 128],
                 k_fm[:, h * T + c * 128: h * T + (c + 1) * 128],
                 emA[:, h * 128:(h + 1) * 128], wait=[(ac, ex)])
            p.do("vector", "tensor_single_scalar", kh_t[:, h * 128:(h + 1) * 128],
                 kt_t[:, h * 128:(h + 1) * 128], eAl[:, h:h + 1], ALU.mult,
                 drain_inc=(dv, 1))
            tps = psA[:, 704:768].bitcast(BF16)
            p.do("tensor", "transpose", tps, kh_t[:, h * 128:(h + 1) * 128], idt,
                 wait=[(dv, None)], inc=(pe, 1))
            p.do("vector", "tensor_copy", kh_t[:, h * 128:(h + 1) * 128], tps,
                 wait=[(pe, None)], drain_inc=(dv, 1))
            ps_kv = psA[:, 1024 + h * 256:1024 + (h + 1) * 256]
            p.do("tensor", "matmul", ps_kv,
                 kh_t[:, h * 128:(h + 1) * 128],
                 vt_buf[c % 2][:, h * 256:(h + 1) * 256],
                 start=True, stop=True, wait=[(dv, None)], inc=(pe, 1))
            p.do("vector", "scalar_tensor_tensor", Sst[:, h * 256:(h + 1) * 256],
                 Sst[:, h * 256:(h + 1) * 256], eAl[:, h:h + 1], ps_kv,
                 ALU.mult, ALU.add, wait=[(pe, None)], drain_inc=(dv, 1))

        p.do("vector", "memset", Sst[:], 0.0)
        p.do("vector", "memset", Atot[:, 0:4], 0.0, drain_inc=(dv, 1))
        vload(0, wait=[(dv, proj_dv), (pe, proj_pe), (p.sem("d4"), None)])
        for c in range(NC):
            if c + 1 < NC:
                vload(c + 1, wait=[(pe, None)])
            ex = decay_tiles(c, atot=True)
            p.wait("tensor", vsem[c % 2], vt_state[c])
            for h in range(H):
                state_update(c, h, ex)
        p.do("vector", "tensor_copy", S0[:, 0:H * DV], Sst[:], wait=[(dv, None)])
        p.do("vector", "tensor_copy", S0[:, H * DV:H * DV + 4], Atot[:, 0:4],
             drain_inc=(dv, 1))
        p.do("sync", "dma_start", out=sin[:, :], in_=S0[:, :], wait=[(dv, None)],
             inc=(st_s, 16))
        if L1A_PHASE < 3:
            p.wait("sync", st_s, None)
            qkstk.close()
            p.barrier()
            return
        p.wait("gpsimd", st_s, None)
        p.do("gpsimd", "collective_compute", "AllGather", ALU.bypass,
             replica_groups=GROUPS8, ins=[sin.opt()], outs=[sout.opt()],
             inc=(cc, 1))
        pidx = inp["l1a_pidx"]
        with ExitStack() as stk:
            pred = p.sbuf(stk, "l1a_pred", [128, 1028], F32)
            pidx_s = p.sbuf(stk, "l1a_pidxs", [128, 3], U32)
            eap = p.sbuf(stk, "l1a_eap", [128, 4], F32)
            p.do("sync", "dma_start", out=pidx_s[:], in_=pidx[:, :], inc=(ld, 16))
            p.do("vector", "memset", S0[:], 0.0, drain_inc=(dv, 1))
            for j in range(2, -1, -1):
                p.do("gpsimd", "memset", pred[:], 0.0,
                     wait=[(cc, None), (ld, None)] if j == 2 else [(dv, None)])
                p.do("gpsimd", "indirect_dma_start", out=pred[:], out_offset=None,
                     in_=sout[:, :],
                     in_offset=IndirectOffsetOnAxis(ap=pidx_s[:, j:j + 1], axis=0),
                     bounds_check=p.bound_reg(8 * 128 - 1), oob_is_err=False, inc=(gp, 16))
                p.do("scalar", "activation", eap[:], pred[:, H * DV:H * DV + 4],
                     AF.Exp, wait=[(gp, None)], drain_inc=(ac, 1))
                for h in range(H):
                    p.do("vector", "scalar_tensor_tensor",
                         S0[:, h * 256:(h + 1) * 256], S0[:, h * 256:(h + 1) * 256],
                         eap[:, h:h + 1], pred[:, h * 256:(h + 1) * 256],
                         ALU.mult, ALU.add,
                         wait=[(ac, None), (dv, None)] if h == 0 else None,
                         drain_inc=(dv, 1) if h == 3 else None)
            p.do("vector", "tensor_copy", Sst[:], S0[:, 0:H * DV],
                 drain_inc=(dv, 1))
        if L1A_PHASE < 4:
            qkstk.close()
            p.barrier()
            return
        vload(0, wait=[(pe, None), (dv, None)])
        p.do("sync", "dma_start", out=rw[:], in_=inp["l1a_rmsw"][:, :],
             wait=[(dv, None)], inc=(p.sem("d5"), 16))
        p.do("tensor", "matmul", psB[:, 0:256], G["onesrow"][0:1, :], rw[:],
             start=True, stop=True, wait=[(p.sem("d5"), None), (dv, None)], inc=(pe, 1))
        p.do("vector", "tensor_copy", rmsw[:], psB[:, 0:256], wait=[(pe, None)],
             drain_inc=(dv, 1))
        for c in range(NC):
            if c + 1 < NC:
                vload(c + 1, wait=[(pe, None)])
            ex = decay_tiles(c, atot=False)
            p.wait("tensor", vsem[c % 2], vt_state[c])
            for h in range(H):
                p.do("vector", "tensor_mul", qt_t[:, h * 128:(h + 1) * 128],
                     q_fm[:, h * T + c * 128: h * T + (c + 1) * 128],
                     eA[:, h * 128:(h + 1) * 128], wait=[(ac, ex)])
                p.do("vector", "tensor_copy", Sbf[:], Sst[:, h * 256:(h + 1) * 256])
                p.do("vector", "tensor_mul", kt_t[:, h * 128:(h + 1) * 128],
                     k_fm[:, h * T + c * 128: h * T + (c + 1) * 128],
                     emA[:, h * 128:(h + 1) * 128], drain_inc=(dv, 1))
                p.do("tensor", "matmul", psA[:, 512:640],
                     kt_t[:, h * 128:(h + 1) * 128], qt_t[:, h * 128:(h + 1) * 128],
                     start=True, stop=True, wait=[(dv, None)], inc=(pe, 1))
                p.do("vector", "tensor_mul", Pt[:], psA[:, 512:640],
                     u128, wait=[(pe, None)], drain_inc=(dv, 1))
                ps_o = psB[:, h * 512:h * 512 + 256]
                p.do("tensor", "matmul", ps_o,
                     Pt[:], vt_buf[c % 2][:, h * 256:(h + 1) * 256],
                     start=True, stop=False, wait=[(dv, None)])
                p.do("tensor", "matmul", ps_o,
                     qt_t[:, h * 128:(h + 1) * 128], Sbf[:],
                     start=False, stop=True, inc=(pe, 1))
                p.do("scalar", "activation", sq[:], ps_o, AF.Square,
                     accum_out=ms[:, 0:1], wait=[(pe, None)], drain_inc=(ac, 1))
                p.do("scalar", "activation", ms[:, 1:2], ms[:, 0:1], AF.Ln,
                     scale=1.0 / 256.0, bias=G["eps128"])
                p.do("scalar", "activation", ms[:, 0:1], ms[:, 1:2], AF.Exp,
                     scale=-0.5, drain_inc=(ac, 1))
                p.do("vector", "scalar_tensor_tensor", on[:], ps_o,
                     ms[:, 0:1], rmsw[:], ALU.mult, ALU.mult,
                     wait=[(ac, None)], drain_inc=(dv, 1))
                for half in range(2):
                    tps = psA[:, 640:704].bitcast(BF16)
                    p.do("tensor", "transpose", tps,
                         on[:, half * 128:(half + 1) * 128], idt,
                         wait=[(dv, None)], inc=(pe, 1))
                    p.do("vector", "tensor_copy",
                         O_fm[2 * h + half][:, c * 128:(c + 1) * 128], tps,
                         wait=[(pe, None)], drain_inc=(dv, 1))
                state_update(c, h, ex)

        qkstk.close()
        if L1A_PHASE < 5:
            p.barrier()
            return
        with ExitStack() as stk:
            sg2 = p.sbuf(stk, "l1a_sgate", [128, T], BF16)
            with ExitStack() as wstk:
                wgb = emit_weight_load(p, wstk, "l1a_wg", inp["l1a_wg"])
                p.wait("tensor", ac, None)
                gate_dv = {}
                for co in range(NCH):
                    ps = psA if co % 2 == 0 else psB
                    w = [(dv, None)] if co < 2 else [(dv, gate_dv[co - 2])]
                    for tt in range(NTT):
                        sl = slice(tt * TT, (tt + 1) * TT)
                        for ci in range(NCH):
                            p.do("tensor", "matmul", ps[:, sl],
                                 wgb[:, ci * 1024 + co * 128: ci * 1024 + (co + 1) * 128],
                                 G["xb"][ci][:, sl], start=(ci == 0),
                                 stop=(ci == NCH - 1),
                                 wait=w if (tt == 0 and ci == 0) else None,
                                 inc=(pe, 1) if (tt == NTT - 1 and ci == NCH - 1) else None)
                    p.do("scalar", "activation", sg2[:], ps[:, 0:T], AF.Silu,
                         wait=[(pe, None)], drain_inc=(ac, 1))
                    p.do("vector", "tensor_mul", O_fm[co][:], O_fm[co][:], sg2[:],
                         wait=[(ac, None)], drain_inc=(dv, 1))
                    gate_dv[co] = dv.n
        with ExitStack() as stk:
            tmp = p.sbuf(stk, "l1a_tmp", [128, T], F32)
            with ExitStack() as wstk:
                wob = emit_weight_load(p, wstk, "l1a_wout", inp["l1a_wout"])
                p.wait("tensor", ac, None)
                p.wait("tensor", dv, None)
                evac_dv = {}
                for co in range(NCH):
                    ps = psA if co % 2 == 0 else psB
                    for tt in range(NTT):
                        sl = slice(tt * TT, (tt + 1) * TT)
                        w = [(dv, evac_dv[co - 2])] if (co >= 2 and tt == 0) else None
                        for ci in range(NCH):
                            p.do("tensor", "matmul", ps[:, sl],
                                 wob[:, ci * 1024 + co * 128: ci * 1024 + (co + 1) * 128],
                                 O_fm[ci][:, sl], start=(ci == 0), stop=(ci == NCH - 1),
                                 wait=w if ci == 0 else None,
                                 inc=(pe, 1) if (tt == NTT - 1 and ci == NCH - 1) else None)
                    p.do("vector", "tensor_add", G["xs"][co], G["xs"][co],
                         ps[:, 0:T], wait=[(pe, None)], inc=(dv, 1))
                    evac_dv[co] = dv.n
            ln_feature_major(p, G, G["xs"], G["xs"], G["xb"], tmp[:],
                             ready=[(dv, dv.n)])
    p.barrier()


def build_moe(p, G, pref, E, CAP, FF, act_relu, has_bias, inner_resid, ne_local):
    nc = p.nc
    ld, st_s, pe, dv, ac, gp, cc = [p.sem(s) for s in
                                    ("ld", "st", "pe", "dv", "ac", "gp", "cc")]
    psA, psB = G["psA"], G["psB"]
    inp = G["inp"]
    NSLOT = E * CAP
    LSLOT = NSLOT // 8
    NG = LSLOT // 128
    FCH = FF // 128
    GROUPS8 = [[0, 1, 2, 3, 4, 5, 6, 7]]
    xga_in = nc.dram_tensor(pref + "xin", [T, D], BF16).ap()
    xga = nc.dram_tensor(pref + "xga", [8 * T, D], BF16).ap()
    cnt_in = nc.dram_tensor(pref + "ci", [1, 128], F32).ap()
    cnt_out = nc.dram_tensor(pref + "co", [8, 128], F32).ap()
    inv_in = nc.dram_tensor(pref + "ii", [NSLOT, 2], I32).ap()
    inv_rs = nc.dram_tensor(pref + "ir", [LSLOT, 2], I32).ap()
    buf_dr = nc.dram_tensor(pref + "bd", [LSLOT, D], BF16).ap()
    hdr = nc.dram_tensor(pref + "hd", [FCH * 128, CAP], BF16).ap()
    w2b = nc.dram_tensor(pref + "w2b", [ne_local * FF, D], BF16).ap()
    y_in = nc.dram_tensor(pref + "yi", [LSLOT, D], BF16).ap()
    y_out = nc.dram_tensor(pref + "yo", [NSLOT, D], BF16).ap()
    slot_dr = nc.dram_tensor(pref + "sd", [T, 1], U32).ap()
    BIG = float(1 << 22)
    dw0, dw1 = p.sem("d2"), p.sem("d3")
    dsA = [p.sem("d0"), p.sem("d1")]

    with ExitStack() as stk0:
        with ExitStack() as stk:
            xstg = p.sbuf(stk, pref + "xstg", [128, D], BF16)
            for tch in range(16):
                for c in range(NCH):
                    tps = psA[:, (c % 4) * 512:(c % 4) * 512 + 64].bitcast(BF16)
                    p.do("tensor", "transpose", tps,
                         G["xb"][c][:, tch * 128:(tch + 1) * 128], G["idt"],
                         wait=[(dv, None)] if c % 4 == 0 else None, inc=(pe, 1))
                    p.do("vector", "tensor_copy", xstg[:, c * 128:(c + 1) * 128],
                         tps, wait=[(pe, None)], drain_inc=(dv, 1))
                p.do("sync", "dma_start", out=xga_in[tch * 128:(tch + 1) * 128, :],
                     in_=xstg[:], wait=[(dv, None)], inc=(st_s, 16))
                p.wait("tensor", st_s, None)
        p.wait("gpsimd", st_s, None)
        p.do("gpsimd", "collective_compute", "AllGather", ALU.bypass,
             replica_groups=GROUPS8, ins=[xga_in.opt()], outs=[xga.opt()],
             inc=(cc, 1))
        with ExitStack() as stk:
            wf2 = p.sbuf(stk, pref + "wf2", [128, D], F32)
            wb2 = p.sbuf(stk, pref + "wb2", [128, D], BF16)
            for e in range(ne_local):
                for fi in range(FCH):
                    p.do("sync", "dma_start", out=wf2[:],
                         in_=inp[pref + "w2"][e, fi * 128:(fi + 1) * 128, :],
                         wait=[(st_s, None)], inc=(ld, 16))
                    p.do("scalar", "activation", wb2[:], wf2[:], AF.Copy,
                         wait=[(ld, None)], drain_inc=(ac, 1))
                    p.do("sync", "dma_start",
                         out=w2b[(e * FF) + fi * 128:(e * FF) + (fi + 1) * 128, :],
                         in_=wb2[:], wait=[(ac, None)], inc=(st_s, 16))
        p.wait("vector", p.sem("st"), None)
        p.wait("scalar", p.sem("st"), None)
        p.wait("tensor", p.sem("st"), None)
        gate_t = p.sbuf(stk0, pref + "gate", [128, 16], F32)
        ohT = p.sbuf(stk0, pref + "ohT", [32, T], F32)
        posT = p.sbuf(stk0, pref + "posT", [32, T], F32)
        srow = p.sbuf(stk0, pref + "srow", [1, T], F32)
        su = posT[0:1, :].bitcast(U32) if False else None
        with ExitStack() as stk:
            wrf = p.sbuf(stk, pref + "wrf_s", [128, 32], F32)
            wr = p.sbuf(stk, pref + "wr_s", [128, NCH * 32], F32)
            grb = p.sbuf(stk, pref + "grb_s", [1, 32], F32)
            prob = p.sbuf(stk, pref + "prob", [128, 32], F32)
            oh1 = p.sbuf(stk, pref + "oh1", [128, 32], BF16)
            mx = p.sbuf(stk, pref + "mx", [128, 2], F32)
            s32 = p.sbuf(stk, pref + "s32", [32, 64], F32)
            onecol = p.sbuf(stk, pref + "onecol", [32, T], F32)
            ecap = p.sbuf(stk, pref + "ecap_s", [32, 1], F32)
            p.do("vector", "memset", onecol[:], 1.0)
            for ci in range(NCH):
                p.do("sync", "dma_start", out=wrf[:],
                     in_=inp[pref + "wr"][ci * 128:(ci + 1) * 128, :],
                     wait=[(dv, None)], inc=(ld, 16))
                p.do("vector", "tensor_copy", wr[:, ci * 32:(ci + 1) * 32], wrf[:],
                     wait=[(ld, None)], drain_inc=(dv, 1))
            if has_bias:
                p.do("sync", "dma_start", out=grb[:], in_=inp[pref + "grb"][:, :],
                     inc=(ld, 16))
                p.wait("tensor", ld, None)
            lgt = p.sbuf(stk, pref + "lgt", [128, 32], F32)
            if DEBUG_TAPS:
                dbgl_s = p.sbuf(stk, pref + "dbgl_s", [128, 512], F32)
                dbgl = nc.dram_tensor(pref + "dbgl", [128, 512], F32,
                                      kind="ExternalOutput").ap()
                dbge_s = p.sbuf(stk, pref + "dbge_s", [128, 512], F32)
                dbge = nc.dram_tensor(pref + "dbge", [128, 512], F32,
                                      kind="ExternalOutput").ap()
                dbgp_s = p.sbuf(stk, pref + "dbgp_s", [128, 512], F32)
                dbgp = nc.dram_tensor(pref + "dbgp", [128, 512], F32,
                                      kind="ExternalOutput").ap()
                dbgm_s = p.sbuf(stk, pref + "dbgm_s", [128, 32], F32)
                dbgm = nc.dram_tensor(pref + "dbgm", [128, 32], F32,
                                      kind="ExternalOutput").ap()
            xlo = p.sbuf(stk, pref + "xlo", [128, NCH * 128], BF16)
            wrh = p.sbuf(stk, pref + "wrh", [128, NCH * 32], BF16)
            wrl = p.sbuf(stk, pref + "wrl", [128, NCH * 32], BF16)
            grbb = p.sbuf(stk, pref + "grbb", [1, 32], BF16)
            p.do("vector", "tensor_copy", wrh[:, 0:NCH * 32],
                 wr[:, 0:NCH * 32], wait=[(ld, None)])
            p.do("vector", "tensor_sub", wrl[:, 0:NCH * 32].bitcast(F32)[:, 0:NCH * 16]
                 if False else wr[:, 0:NCH * 32], wr[:, 0:NCH * 32],
                 wrh[:, 0:NCH * 32])
            p.do("vector", "tensor_copy", wrl[:, 0:NCH * 32], wr[:, 0:NCH * 32],
                 drain_inc=(dv, 1))
            if has_bias:
                p.do("scalar", "activation", grbb[:], grb[:], AF.Copy,
                     wait=[(ld, None)], drain_inc=(ac, 1))
                p.wait("tensor", ac, None)
            for tch in range(16):
                for ci in range(NCH):
                    p.do("vector", "tensor_sub", xlo[:, ci * 128:(ci + 1) * 128],
                         G["xs"][ci][:, tch * 128:(tch + 1) * 128],
                         G["xb"][ci][:, tch * 128:(tch + 1) * 128],
                         wait=[(dv, None), (pe, None)] if ci == 0 else None,
                         drain_inc=(dv, 1) if ci == NCH - 1 else None)
                terms = []
                for ci in range(NCH):
                    terms += [
                        (G["xb"][ci][:, tch * 128:(tch + 1) * 128],
                         wrh[:, ci * 32:ci * 32 + E]),
                        (G["xb"][ci][:, tch * 128:(tch + 1) * 128],
                         wrl[:, ci * 32:ci * 32 + E]),
                        (xlo[:, ci * 128:(ci + 1) * 128],
                         wrh[:, ci * 32:ci * 32 + E]),
                        (xlo[:, ci * 128:(ci + 1) * 128],
                         wrl[:, ci * 32:ci * 32 + E]),
                    ]
                if has_bias:
                    terms.append((G["onesrowb"][0:1, :], grbb[0:1, 0:E]))
                rb = (tch % 4) * 512
                for ti2, (lt, rt) in enumerate(terms):
                    p.do("tensor", "matmul", psA[0:128, rb:rb + E],
                         lt, rt, start=(ti2 == 0), stop=(ti2 == len(terms) - 1),
                         wait=[(dv, None)] if ti2 == 0 else None,
                         inc=(pe, 1) if ti2 == len(terms) - 1 else None)
                p.do("scalar", "activation", lgt[:, 0:E], psA[:, rb:rb + E],
                     AF.Copy, wait=[(pe, None)], drain_inc=(ac, 1))
                p.wait("vector", ac, None)
                if DEBUG_TAPS:
                    p.do("vector", "tensor_copy",
                         dbgl_s[:, tch * 32:tch * 32 + E], lgt[:, 0:E])
                p.do("scalar", "activation", prob[:, 0:E], lgt[:, 0:E],
                     AF.Exp, drain_inc=(ac, 1))
                if DEBUG_TAPS:
                    p.do("vector", "tensor_copy",
                         dbge_s[:, tch * 32:tch * 32 + E], prob[:, 0:E],
                         wait=[(ac, None)])
                p.do("vector", "reduce_sum", mx[:, 0:1], prob[:, 0:E], AX.X,
                     wait=[(ac, None)], drain_inc=(dv, 1))
                p.do("vector", "reciprocal", mx[:, 1:2], mx[:, 0:1],
                     drain_inc=(dv, 1))
                if DEBUG_TAPS:
                    p.do("vector", "tensor_copy",
                         dbgm_s[:, 2 * tch:2 * tch + 2], mx[:, 0:2])
                p.do("scalar", "activation", prob[:, 0:E], prob[:, 0:E],
                     AF.Copy, scale=mx[:, 1:2],
                     wait=[(dv, None)], drain_inc=(ac, 1))
                if DEBUG_TAPS:
                    p.do("vector", "tensor_copy",
                         dbgp_s[:, tch * 32:tch * 32 + E], prob[:, 0:E],
                         wait=[(ac, None)])
                p.do("vector", "reduce_max", gate_t[:, tch:tch + 1], prob[:, 0:E],
                     AX.X, wait=[(ac, None)], drain_inc=(dv, 1))
                p.do("gpsimd", "tensor_single_scalar", oh1[:, 0:E], prob[:, 0:E],
                     gate_t[:, tch:tch + 1], ALU.is_ge, wait=[(dv, None)],
                     drain_inc=(gp, 1))
                p.wait("tensor", gp, None)
                p.wait("vector", gp, None)
                tps = psB[:, 1536:1600].bitcast(BF16)
                p.do("tensor", "transpose", tps[0:E, :], oh1[:, 0:E], G["idt"],
                     wait=[(dv, None)], inc=(pe, 1))
                p.do("vector", "tensor_copy", ohT[0:E, tch * 128:(tch + 1) * 128],
                     tps[0:E, :], wait=[(pe, None)], drain_inc=(dv, 1))
            p.do("vector", "tensor_tensor_scan", posT[0:E, :], onecol[0:E, :],
                 ohT[0:E, :], 0.0, ALU.mult, ALU.add, drain_inc=(dv, 1))
            p.do("vector", "tensor_copy", s32[0:E, 0:1], posT[0:E, T - 1:T],
                 drain_inc=(dv, 1))
            cnt_in_c = cnt_in.rearrange("a (b c) -> (a b) c", c=1)
            p.do("sync", "dma_start", out=cnt_in_c[0:E, :], in_=s32[0:E, 0:1],
                 wait=[(dv, None)], inc=(st_s, 16))
            if DEBUG_TAPS:
                dbgci = nc.dram_tensor(pref + "dbgci", [1, 32], F32,
                                       kind="ExternalOutput").ap()
                dbgsr = nc.dram_tensor(pref + "dbgsr", [1, 32], F32,
                                       kind="ExternalOutput").ap()
                p.do("sync", "dma_start", out=dbgci[:, :], in_=cnt_in[0:1, 0:32],
                     wait=[(st_s, None)], inc=(st_s, 16))
                p.do("sync", "dma_start", out=dbgsr[:, :], in_=s32[0:1, 32:64],
                     inc=(st_s, 16))
                dbgs0 = nc.dram_tensor(pref + "dbgs0", [32, 1], F32,
                                       kind="ExternalOutput").ap()
                p.do("sync", "dma_start", out=dbgs0[:, :], in_=s32[0:32, 0:1],
                     inc=(st_s, 16))
            p.wait("gpsimd", st_s, None)
            p.wait("gpsimd", cc, None)
            p.do("gpsimd", "collective_compute", "AllGather", ALU.bypass,
                 replica_groups=GROUPS8, ins=[cnt_in.opt()], outs=[cnt_out.opt()],
                 inc=(cc, 1))
            cnt8 = p.sbuf(stk, pref + "cnt8", [8, 32], F32)
            mkc = p.sbuf(stk, pref + "mkc", [8, 1], F32)
            p.do("sync", "dma_start", out=cnt8[:], in_=cnt_out[:, 0:32],
                 wait=[(cc, None)], inc=(ld, 16))
            p.do("sync", "dma_start", out=mkc[:], in_=inp[pref + "mask"][:, :],
                 inc=(ld, 16))
            p.do("sync", "dma_start", out=ecap[:], in_=inp[pref + "ecap"][:, :],
                 inc=(ld, 16))
            p.do("tensor", "matmul", psA[0:1, 1536:1568], mkc[:], cnt8[:],
                 start=True, stop=True, wait=[(ld, None)], inc=(pe, 1))
            p.do("scalar", "activation", s32[0:1, 0:32], psA[0:1, 1536:1568],
                 AF.Copy, wait=[(pe, None)], drain_inc=(ac, 1))
            off_r = nc.dram_tensor(pref + "od", [1, 32], F32).ap()
            off_c = off_r.rearrange("a (b c) -> (a b) c", c=1)
            p.do("sync", "dma_start", out=off_r[0:1, :], in_=s32[0:1, 0:32],
                 wait=[(ac, None)], inc=(st_s, 16))
            p.do("sync", "dma_start", out=s32[0:32, 32:33], in_=off_c[0:32, :],
                 wait=[(st_s, None)], inc=(ld, 16))
            p.do("vector", "tensor_single_scalar", s32[:, 0:1], s32[:, 32:33],
                 -1.0, ALU.add, wait=[(ld, None)], drain_inc=(dv, 1))
            if DEBUG_TAPS:
                dbgc = nc.dram_tensor(pref + "dbgc", [8, 33], F32,
                                      kind="ExternalOutput").ap()
                dbgk = nc.dram_tensor(pref + "dbgk", [8, 1], F32,
                                      kind="ExternalOutput").ap()
                dbgo = nc.dram_tensor(pref + "dbgo", [32, 2], F32,
                                      kind="ExternalOutput").ap()
                p.do("sync", "dma_start", out=dbgc[:, 0:32], in_=cnt8[:, :],
                     wait=[(dv, None)], inc=(st_s, 16))
                p.do("sync", "dma_start", out=dbgk[:, :], in_=mkc[:, :],
                     inc=(st_s, 16))
                p.do("sync", "dma_start", out=dbgo[:, :], in_=s32[:, 0:2],
                     inc=(st_s, 16))
            p.do("vector", "tensor_single_scalar", posT[0:E, :], posT[0:E, :],
                 s32[0:E, 0:1], ALU.add)
            p.do("vector", "tensor_single_scalar", onecol[0:E, :], posT[0:E, :],
                 float(CAP), ALU.is_lt)
            p.do("vector", "tensor_mul", onecol[0:E, :], onecol[0:E, :],
                 ohT[0:E, :], drain_inc=(dv, 1))
            p.do("vector", "tensor_single_scalar", posT[0:E, :], posT[0:E, :],
                 ecap[0:E, 0:1], ALU.add)
            p.do("vector", "tensor_mul", posT[0:E, :], posT[0:E, :],
                 onecol[0:E, :], drain_inc=(dv, 1))
            for q in range(4):
                p.do("tensor", "matmul", psA[0:1, q * 512:(q + 1) * 512],
                     G["ones"][0:E, 0:1], posT[0:E, q * 512:(q + 1) * 512],
                     start=True, stop=True,
                     wait=[(dv, None)] if q == 0 else None)
            for q in range(4):
                p.do("tensor", "matmul", psB[0:1, q * 512:(q + 1) * 512],
                     G["ones"][0:E, 0:1], onecol[0:E, q * 512:(q + 1) * 512],
                     start=True, stop=True,
                     inc=(pe, 1) if q == 3 else None)
            p.do("vector", "tensor_copy", srow[:], psA[0:1, 0:T],
                 wait=[(pe, None)])
            p.do("vector", "scalar_tensor_tensor", srow[:], psB[0:1, 0:T],
                 -BIG, srow[:], ALU.mult, ALU.add)
            p.do("vector", "tensor_single_scalar", srow[:], srow[:], BIG,
                 ALU.add)
            su = posT[0:1, :].bitcast(U32)[0:1, 0:T]
            p.do("vector", "tensor_copy", su, srow[:], drain_inc=(dv, 1))
            p.do("sync", "dma_start", out=slot_dr[:, :], in_=su,
                 wait=[(dv, None)], inc=(st_s, 16))
        if DEBUG_TAPS:
            p.do("sync", "dma_start", out=dbgl[:, :], in_=dbgl_s[:],
                 wait=[(dv, None)], inc=(st_s, 16))
            p.do("sync", "dma_start", out=dbge[:, :], in_=dbge_s[:],
                 inc=(st_s, 16))
            p.do("sync", "dma_start", out=dbgp[:, :], in_=dbgp_s[:],
                 inc=(st_s, 16))
            p.do("sync", "dma_start", out=dbgm[:, :], in_=dbgm_s[:],
                 inc=(st_s, 16))
            dbgs = nc.dram_tensor(pref + "dbgs", [T, 1], U32, kind="ExternalOutput").ap()
            p.do("sync", "dma_start", out=dbgs[:, :], in_=su,
                 wait=[(dv, None)], inc=(st_s, 16))
            dbgg = nc.dram_tensor(pref + "dbgg", [128, 16], F32, kind="ExternalOutput").ap()
            p.do("sync", "dma_start", out=dbgg[:, :], in_=gate_t[:],
                 inc=(st_s, 16))
        with ExitStack() as stk:
            bigt = p.sbuf(stk, pref + "bigt", [128, NSLOT * 2 // 128], I32)
            p.do("vector", "memset", bigt[:], 0x3FFFFFFF, drain_inc=(dv, 1))
            p.do("sync", "dma_start",
                 out=inv_in.rearrange("(q c) j -> q (c j)", q=128),
                 in_=bigt[:], wait=[(dv, None)], inc=(st_s, 16))
            slot_u = p.sbuf(stk, pref + "slotu", [128, 16], U32)
            for tch in range(16):
                p.do("sync", "dma_start", out=slot_u[:, tch:tch + 1],
                     in_=slot_dr[tch * 128:(tch + 1) * 128, :],
                     wait=[(st_s, None)] if tch == 0 else None, inc=(ld, 16))
            tk2 = p.sbuf(stk, pref + "tk2", [128, 32], I32)
            tkp = p.sbuf(stk, pref + "tkp", [128, 2], I32)
            p.do("sync", "dma_start", out=tk2[:], in_=inp[pref + "tok2"][:, :],
                 inc=(ld, 16))
            p.wait("gpsimd", ld, None)
            p.wait("gpsimd", st_s, None)
            p.wait("vector", ld, None)
            for tch in range(16):
                p.do("vector", "tensor_copy", tkp[:], tk2[:, 2 * tch:2 * tch + 2],
                     wait=[(gp, None)] if tch > 0 else None, drain_inc=(dv, 1))
                p.do("gpsimd", "indirect_dma_start", out=inv_in[:, :],
                     out_offset=IndirectOffsetOnAxis(ap=slot_u[:, tch:tch + 1], axis=0),
                     in_=tkp[:], in_offset=None, wait=[(dv, None)],
                     bounds_check=p.bound_reg(NSLOT - 1), oob_is_err=False, inc=(gp, 16))
            p.wait("gpsimd", gp, None)
            p.do("gpsimd", "collective_compute", "ReduceScatter", ALU.min,
                 replica_groups=GROUPS8, ins=[inv_in.opt()], outs=[inv_rs.opt()],
                 inc=(cc, 1))
        invs_i = p.sbuf(stk0, pref + "invs", [128, NG * 2], I32)
        invs = invs_i[:].bitcast(U32)
        p.do("sync", "dma_start",
             out=invs_i[:].rearrange("q (g j) -> q g j", j=2),
             in_=inv_rs.rearrange("(g q) j -> q g j", q=128),
             wait=[(cc, None)], inc=(ld, 16))
        with ExitStack() as stk:
            gbuf = p.sbuf(stk, pref + "gbuf", [128, D], BF16)
            p.wait("gpsimd", ld, None)
            p.wait("gpsimd", cc, None)
            for g in range(NG):
                p.do("gpsimd", "memset", gbuf[:], 0.0,
                     wait=[(st_s, None)] if g > 0 else None)
                p.do("gpsimd", "indirect_dma_start", out=gbuf[:], out_offset=None,
                     in_=xga[:, :],
                     in_offset=IndirectOffsetOnAxis(ap=invs[:, 2 * g:2 * g + 1], axis=0),
                     bounds_check=p.bound_reg(8 * T - 1), oob_is_err=False, inc=(gp, 16))
                p.do("sync", "dma_start", out=buf_dr[g * 128:(g + 1) * 128, :],
                     in_=gbuf[:], wait=[(gp, None)], inc=(st_s, 16))
        NTS = ceil_div(CAP, 512)
        tiles = [(i * 512, min(512, CAP - i * 512)) for i in range(NTS)]
        with ExitStack() as stk:
            buf_e = p.sbuf(stk, pref + "bufe", [128, NCH * CAP], BF16)
            hstg = p.sbuf(stk, pref + "hstg", [128, CAP], BF16)
            wgb = [p.sbuf(stk, pref + f"wgb{i}", [128, NCH * 128], BF16) for i in range(2)]
            wub = [p.sbuf(stk, pref + f"wub{i}", [128, NCH * 128], BF16) for i in range(2)] \
                if not act_relu else [None, None]
            wdb = [p.sbuf(stk, pref + f"wdb{i}", [128, D], BF16) for i in range(2)]
            hld = [p.sbuf(stk, pref + f"hld{i}", [128, 512], BF16) for i in range(2)]
            wf = p.sbuf(stk, pref + "wf", [128, NCH * 128], F32)
            sgb = (p.sbuf(stk, pref + "sgb", [128, 512], F32)
                   if not act_relu else None)
            ybk = p.sbuf(stk, pref + "ybk", [128, NCH * 512], BF16)
            ystg = p.sbuf(stk, pref + "ystg", [128, D], BF16)
            b1t = p.sbuf(stk, pref + "b1t_s", [128, FCH], F32)
            b2t = p.sbuf(stk, pref + "b2t_s", [128, NCH], F32)
            for e in range(ne_local):
                for c in range(NCH):
                    p.do("scalar", "dma_start",
                         out=buf_e[:, c * CAP:(c + 1) * CAP],
                         in_=buf_dr[e * CAP:(e + 1) * CAP, c * 128:(c + 1) * 128],
                         transpose=True,
                         wait=[(st_s, None), (pe, None)] if c == 0 else None,
                         inc=(p.sem("d4"), 16))
                if has_bias:
                    p.do("sync", "dma_start", out=b1t[:],
                         in_=inp[pref + "b1t"][e, :, :], inc=(ld, 16))
                    p.do("sync", "dma_start", out=b2t[:],
                         in_=inp[pref + "b2t"][e, :, :], inc=(ld, 16))
                    p.wait("scalar", ld, None)
                    p.wait("vector", ld, None)
                p.wait("tensor", p.sem("d4"), None)
                p.wait("tensor", ac, None)
                wsem = p.sem("d5")

                def w1load(fo, wait=None):
                    pi = fo % 2
                    w1s = inp[pref + "w1"][e, :, fo * 128:(fo + 1) * 128] \
                        .rearrange("(c p) f -> p c f", p=128)
                    p.do("sync", "dma_start",
                         out=wf[:].rearrange("p (c f) -> p c f", c=NCH),
                         in_=w1s,
                         wait=(wait if wait else [(ac, None)]),
                         inc=(wsem, 16))
                    p.do("scalar", "activation", wgb[pi][:], wf[:], AF.Copy,
                         wait=[(wsem, None)], drain_inc=(ac, 1))
                    if not act_relu:
                        wus = inp[pref + "wu"][e, :, fo * 128:(fo + 1) * 128] \
                            .rearrange("(c p) f -> p c f", p=128)
                        p.do("sync", "dma_start",
                             out=wf[:].rearrange("p (c f) -> p c f", c=NCH),
                             in_=wus,
                             wait=[(ac, None)], inc=(wsem, 16))
                        p.do("scalar", "activation", wub[pi][:], wf[:],
                             AF.Copy, wait=[(wsem, None)], drain_inc=(ac, 1))
                    return ac.n
                wready = {0: w1load(0)}
                for fo in range(FCH):
                    if fo + 1 < FCH:
                        wready[fo + 1] = w1load(fo + 1,
                                                wait=[(pe, None), (ac, None)])
                    pi = fo % 2
                    for ti, (t0, tw) in enumerate(tiles):
                        pA = psA[:, (ti % 2) * 512:(ti % 2) * 512 + tw]
                        pB = psB[:, (ti % 2) * 512:(ti % 2) * 512 + tw]
                        for ci in range(NCH):
                            p.do("tensor", "matmul", pA,
                                 wgb[pi][:, ci * 128:(ci + 1) * 128],
                                 buf_e[:, ci * CAP + t0: ci * CAP + t0 + tw],
                                 start=(ci == 0), stop=(ci == NCH - 1),
                                 wait=[(ac, wready[fo]), (dv, None)]
                                 if ci == 0 else None,
                                 inc=(pe, 1) if ci == NCH - 1 else None)
                        if not act_relu:
                            for ci in range(NCH):
                                p.do("tensor", "matmul", pB,
                                     wub[pi][:, ci * 128:(ci + 1) * 128],
                                     buf_e[:, ci * CAP + t0: ci * CAP + t0 + tw],
                                     start=(ci == 0), stop=(ci == NCH - 1),
                                     inc=(pe, 1) if ci == NCH - 1 else None)
                        if act_relu:
                            p.do("scalar", "activation", hstg[:, t0:t0 + tw],
                                 pA, AF.Relu, bias=b1t[:, fo:fo + 1],
                                 wait=[(pe, None)], drain_inc=(dv, 1))
                        else:
                            p.do("scalar", "activation",
                                 sgb[:].bitcast(BF16)[:, 0:tw], pA, AF.Silu,
                                 wait=[(pe, None)], drain_inc=(ac, 1))
                            p.do("vector", "tensor_mul", hstg[:, t0:t0 + tw],
                                 sgb[:].bitcast(BF16)[:, 0:tw], pB,
                                 wait=[(ac, None)], drain_inc=(dv, 1))
                    p.do("sync", "dma_start",
                         out=hdr[fo * 128:(fo + 1) * 128, :], in_=hstg[:, 0:CAP],
                         wait=[(dv, None)], inc=(st_s, 16))
                    p.wait("scalar", st_s, None)
                    p.wait("vector", st_s, None)
                for (t0, tw) in tiles:
                    def hload(fi, wait=None):
                        pi2 = fi % 2
                        dwp = dsA[pi2]
                        p.do("gpsimd", "dma_start", out=hld[pi2][:, 0:tw],
                             in_=hdr[fi * 128:(fi + 1) * 128, t0:t0 + tw],
                             wait=wait, inc=(dwp, 16))
                        return dwp.n
                    def wdload(fi, wait=None):
                        pi2 = fi % 2
                        p.do("gpsimd", "dma_start", out=wdb[pi2][:],
                             in_=w2b[e * FF + fi * 128:e * FF + (fi + 1) * 128, :],
                             wait=wait, inc=(dsA[pi2], 16))
                        return dsA[pi2].n
                    hw = {}
                    hload(0, wait=[(st_s, None)])
                    hw[0] = wdload(0)
                    for fi in range(FCH):
                        if fi + 1 < FCH:
                            hload(fi + 1, wait=[(pe, None)])
                            hw[fi + 1] = wdload(fi + 1)
                        pi2 = fi % 2
                        for co in range(NCH):
                            ps = (psA if co < 4 else psB)[:, (co % 4) * 512:(co % 4) * 512 + tw]
                            p.do("tensor", "matmul", ps,
                                 wdb[pi2][:, co * 128:(co + 1) * 128],
                                 hld[pi2][:, 0:tw],
                                 start=(fi == 0), stop=(fi == FCH - 1),
                                 wait=[(dsA[pi2], hw[fi]), (dv, None)]
                                 if co == 0 and fi == 0 else
                                 ([(dsA[pi2], hw[fi])] if co == 0 else None),
                                 inc=(pe, 1) if co == NCH - 1 else None)
                    p.do("tensor", "nop", inc=(pe, 1))
                    for co in range(NCH):
                        pss = (psA if co < 4 else psB)[:, (co % 4) * 512:(co % 4) * 512 + tw]
                        if has_bias:
                            p.do("vector", "tensor_single_scalar",
                                 ybk[:, co * 512:co * 512 + tw], pss,
                                 b2t[:, co:co + 1], ALU.add,
                                 wait=[(pe, None)] if co == 0 else None,
                                 drain_inc=(dv, 1) if co == NCH - 1 else None)
                        else:
                            p.do("vector", "tensor_copy",
                                 ybk[:, co * 512:co * 512 + tw], pss,
                                 wait=[(pe, None)] if co == 0 else None,
                                 drain_inc=(dv, 1) if co == NCH - 1 else None)
                    for k in range(ceil_div(tw, 128)):
                        for co in range(NCH):
                            tps = psA[:, 256:320].bitcast(BF16)
                            p.do("tensor", "transpose", tps,
                                 ybk[:, co * 512 + k * 128: co * 512 + (k + 1) * 128],
                                 G["idt"], wait=[(dv, None)], inc=(pe, 1))
                            p.do("vector", "tensor_copy",
                                 ystg[:, co * 128:(co + 1) * 128], tps,
                                 wait=[(pe, None)], drain_inc=(dv, 1))
                        p.do("sync", "dma_start",
                             out=y_in[e * CAP + t0 + k * 128: e * CAP + t0 + (k + 1) * 128, :],
                             in_=ystg[:], wait=[(dv, None)], inc=(st_s, 16))
                        p.wait("vector", st_s, None)
                        p.wait("tensor", st_s, None)
        p.wait("gpsimd", st_s, None)
        p.do("gpsimd", "collective_compute", "AllGather", ALU.bypass,
             replica_groups=GROUPS8, ins=[y_in.opt()], outs=[y_out.opt()],
             inc=(cc, 1))
        with ExitStack() as stk:
            ybuf = p.sbuf(stk, pref + "ybuf", [128, D], BF16)
            slot_u2 = p.sbuf(stk, pref + "slotu2", [128, 16], U32)
            tmp = p.sbuf(stk, pref + "tmpl", [128, T], F32)
            for tch in range(16):
                p.do("sync", "dma_start", out=slot_u2[:, tch:tch + 1],
                     in_=slot_dr[tch * 128:(tch + 1) * 128, :], inc=(ld, 16))
            p.wait("gpsimd", ld, None)
            p.wait("gpsimd", cc, None)
            for tch in range(16):
                p.do("gpsimd", "memset", ybuf[:], 0.0,
                     wait=[(pe, None)] if tch > 0 else None)
                p.do("gpsimd", "indirect_dma_start", out=ybuf[:], out_offset=None,
                     in_=y_out[:, :],
                     in_offset=IndirectOffsetOnAxis(ap=slot_u2[:, tch:tch + 1], axis=0),
                     bounds_check=p.bound_reg(NSLOT - 1), oob_is_err=False, inc=(gp, 16))
                p.do("vector", "tensor_single_scalar", ybuf[:], ybuf[:],
                     gate_t[:, tch:tch + 1], ALU.mult, wait=[(gp, None)],
                     drain_inc=(dv, 1))
                for c in range(NCH):
                    tps = psA[:, (c % 4) * 512 + 256:(c % 4) * 512 + 320].bitcast(BF16)
                    p.do("tensor", "transpose", tps,
                         ybuf[:, c * 128:(c + 1) * 128], G["idt"],
                         wait=[(dv, None)], inc=(pe, 1))
                    p.do("vector", "scalar_tensor_tensor",
                         G["xs"][c][:, tch * 128:(tch + 1) * 128],
                         G["xs"][c][:, tch * 128:(tch + 1) * 128],
                         2.0 if inner_resid else 1.0, tps,
                         ALU.mult, ALU.add, wait=[(pe, None)],
                         drain_inc=(dv, 1))
            ln_feature_major(p, G, G["xs"], G["xs"], G["xb"], tmp[:],
                             ready=[(dv, dv.n)])
    p.barrier()


def build_l1_ffn(p, G):
    build_moe(p, G, "l1f_", E=32, CAP=640, FF=1024, act_relu=False,
              has_bias=False, inner_resid=True, ne_local=4)


def build_l3_ffn(p, G):
    build_moe(p, G, "l3f_", E=8, CAP=2560, FF=4096, act_relu=True,
              has_bias=True, inner_resid=False, ne_local=1)


def ln_affine_resid(p, G, t1, gcol, bcol, tmp, ready=()):
    psA, psB = G["psA"], G["psB"]
    ones, onesrow, onescolb = G["ones"], G["onesrow"], G["onescolb"]
    rowmu, rowrs, eps1 = G["rowmu"], G["rowrs"], G["eps1"]
    pe, dv, ac = p.sem("pe"), p.sem("dv"), p.sem("ac")
    for s, v in ready:
        p.wait("tensor", s, v)
        p.wait("scalar", s, v)
        p.wait("vector", s, v)
    for tt in range(NTT):
        sl = slice(tt * TT, (tt + 1) * TT)
        for c in range(NCH):
            p.do("tensor", "matmul", psA[0:1, sl], onescolb, t1[c][:, sl],
                 start=(c == 0), stop=(c == NCH - 1))
    for c in range(NCH):
        p.do("scalar", "activation", tmp[:], t1[c][:, 0:T], AF.Square,
             drain_inc=(ac, 1))
        for tt in range(NTT):
            sl = slice(tt * TT, (tt + 1) * TT)
            p.do("tensor", "matmul", psA[32:33, sl], ones[:], tmp[:, sl],
                 start=(c == 0), stop=(c == NCH - 1),
                 wait=[(ac, None)] if tt == 0 else None,
                 inc=(pe, 1) if tt == NTT - 1 else None)
        if c < NCH - 1:
            p.wait("scalar", pe, pe.n)
    p.do("vector", "tensor_scalar_mul", rowmu[0:1, :], psA[0:1, 0:2048], 1.0 / D,
         wait=[(pe, None)])
    p.do("vector", "tensor_mul", rowrs[0:1, :], rowmu[0:1, :], rowmu[0:1, :])
    p.do("vector", "scalar_tensor_tensor", rowrs[0:1, :], psA[32:33, 0:2048],
         1.0 / D, rowrs[0:1, :], ALU.mult, ALU.subtract)
    p.do("vector", "tensor_single_scalar", rowrs[0:1, :], rowrs[0:1, :],
         eps1[0:1, 0:1], ALU.add, drain_inc=(dv, 1))
    p.do("scalar", "activation", rowrs[0:1, :], rowrs[0:1, :], AF.Ln,
         wait=[(dv, None)], drain_inc=(ac, 1))
    p.do("scalar", "activation", rowrs[0:1, :], rowrs[0:1, :], AF.Exp,
         scale=-0.5, drain_inc=(ac, 1))
    p.wait("tensor", ac, ac.n)
    for tt in range(NTT):
        sl = slice(tt * TT, (tt + 1) * TT)
        p.do("tensor", "matmul", psA[:, sl], onesrow[0:1, :], rowmu[0:1, sl],
             start=True, stop=True)
        p.do("tensor", "matmul", psB[:, sl], onesrow[0:1, :], rowrs[0:1, sl],
             start=True, stop=True,
             inc=(pe, 1) if tt == NTT - 1 else None)
    p.wait("vector", pe, pe.n)
    for c in range(NCH):
        p.do("vector", "tensor_sub", tmp[:], t1[c][:, 0:T], psA[:, 0:2048])
        p.do("vector", "tensor_mul", tmp[:], tmp[:], psB[:, 0:2048])
        p.do("vector", "tensor_single_scalar", tmp[:], tmp[:],
             gcol[:, c:c + 1], ALU.mult)
        p.do("vector", "tensor_single_scalar", tmp[:], tmp[:],
             bcol[:, c:c + 1], ALU.add)
        p.do("vector", "tensor_add", G["xs"][c], G["xs"][c], tmp[:],
             inc=(dv, 1))
    p.wait("tensor", dv, dv.n)


def build_l3_attn(p, G):
    nc = p.nc
    ld, st_s, pe, dv, ac, gp, cc = [p.sem(s) for s in
                                    ("ld", "st", "pe", "dv", "ac", "gp", "cc")]
    psA, psB = G["psA"], G["psB"]
    inp = G["inp"]
    H, HE, NC = 16, 64, 16
    EPS = 1e-4
    GROUPS8 = [[0, 1, 2, 3, 4, 5, 6, 7]]
    kdr = nc.dram_tensor("l3a_kd", [NC * 128, 1024], BF16).ap()
    qdr = nc.dram_tensor("l3a_qd", [NC * 128, 1024], BF16).ap()
    vdr = nc.dram_tensor("l3a_vd", [NC * 128, 2048], BF16).ap()
    sin = nc.dram_tensor("l3a_si", [128, 1024], F32).ap()
    sout = nc.dram_tensor("l3a_so", [1024, 1024], F32).ap()
    dsA = [p.sem("d0"), p.sem("d1")]

    with ExitStack() as stk0:
        kstk = ExitStack()
        epsc = p.sbuf(kstk, "l3a_epsc", [128, 1], F32)
        SSf = p.sbuf(kstk, "l3a_ssf", [128, 8 * 128], F32)
        SSz = p.sbuf(kstk, "l3a_ssz", [128, 16 * 128], BF16)
        bqc = p.sbuf(kstk, "l3a_bqc_s", [128, 8], F32)
        bkc = p.sbuf(kstk, "l3a_bkc_s", [128, 8], F32)
        brf = p.sbuf(kstk, "l3a_brf", [1, 2048], F32)
        brb = p.sbuf(kstk, "l3a_brb", [1, 2048], BF16)
        p.do("vector", "memset", epsc[:], EPS)
        p.do("vector", "memset", SSz[:], 0.0)
        p.do("vector", "memset", SSf[:], 0.0, drain_inc=(dv, 1))
        p.do("sync", "dma_start", out=bqc[:], in_=inp["l3a_bqc"][:, :],
             inc=(ld, 16))
        p.do("sync", "dma_start", out=bkc[:], in_=inp["l3a_bkc"][:, :],
             inc=(ld, 16))
        p.do("sync", "dma_start", out=brf[:], in_=inp["l3a_bkvr"][:, :],
             inc=(ld, 16))
        p.do("scalar", "activation", brb[:], brf[:], AF.Copy,
             wait=[(ld, None)], drain_inc=(ac, 1))
        p.wait("tensor", ac, None)
        p.wait("tensor", dv, None)
        with ExitStack() as wfs:
            wkb = emit_weight_load(p, kstk, "l3a_wk", inp["l3a_wkT"],
                                   wf_stk=wfs, ld_sem=p.sem("d4"))
            p.wait("vector", p.sem("ac"), None)
            p.wait("scalar", p.sem("ac"), None)
            p.wait("sync", p.sem("ac"), None)
        p.barrier()

        with ExitStack() as stk:
            vts = p.sbuf(stk, "l3a_vts", [128, 2048], BF16)
            kts = p.sbuf(stk, "l3a_kts", [128, 1024], BF16)
            ftmp = p.sbuf(stk, "l3a_ftmp", [128, 1024], F32)
            fte = p.sbuf(stk, "l3a_fte", [128, 512], F32)
            ftz = p.sbuf(stk, "l3a_ftza", [128, 512], F32)
            wvb = emit_weight_load(p, stk, "l3a_wv", inp["l3a_wvT"],
                                   ld_sem=p.sem("d5"))
            p.wait("tensor", ac, None)
            for c in range(NC):
                if c >= L3A_NC:
                    break
                csl = slice(c * 128, (c + 1) * 128)
                for fo in range(2):
                    reg = psA[:, fo * 512:(fo + 1) * 512]
                    terms = [(G["xb"][ci][:, csl],
                              wvb[:, ci * 1024 + fo * 512:ci * 1024 + (fo + 1) * 512])
                             for ci in range(NCH)]
                    terms.append((G["onesrowb"][0:1, 0:128],
                                  brb[0:1, 1024 + fo * 512:1024 + (fo + 1) * 512]))
                    for ti, (lt, rt) in enumerate(terms):
                        p.do("tensor", "matmul", reg, lt, rt,
                             start=(ti == 0), stop=(ti == len(terms) - 1),
                             wait=[(dv, None)] if (ti == 0 and fo == 0) else None,
                             inc=(pe, 1) if ti == len(terms) - 1 else None)
                p.do("vector", "memset", vts[:], 1.0,
                     wait=[(pe, None), (st_s, None)])
                for h in range(H):
                    p.do("vector", "tensor_copy", vts[:, h * 128:h * 128 + 64],
                         psA[:, h * 64:(h + 1) * 64],
                         drain_inc=(dv, 1) if h == H - 1 else None)
                p.do("sync", "dma_start", out=vdr[csl, :], in_=vts[:],
                     wait=[(dv, None)], inc=(st_s, 16))
                if "k" not in L3A_SUB:
                    continue
                for fo in range(2):
                    reg = psA[:, 1024 + fo * 512:1024 + (fo + 1) * 512]
                    terms = [(G["xb"][ci][:, csl],
                              wkb[:, ci * 1024 + fo * 512:ci * 1024 + (fo + 1) * 512])
                             for ci in range(NCH)]
                    terms.append((G["onesrowb"][0:1, 0:128],
                                  brb[0:1, fo * 512:(fo + 1) * 512]))
                    for ti, (lt, rt) in enumerate(terms):
                        p.do("tensor", "matmul", reg, lt, rt,
                             start=(ti == 0), stop=(ti == len(terms) - 1),
                             wait=[(dv, None)] if (ti == 0 and fo == 0) else None,
                             inc=(pe, 1) if ti == len(terms) - 1 else None)
                for fo in range(2):
                    z = psA[:, 1024 + fo * 512:1024 + (fo + 1) * 512]
                    t = ftmp[:, fo * 512:(fo + 1) * 512]
                    p.do("vector", "tensor_copy", ftz[:], z,
                         wait=[(pe, None)] if fo == 0 else None)
                    p.do("vector", "tensor_single_scalar", t, ftz[:], 0.0,
                         ALU.min, drain_inc=(dv, 1))
                    p.do("scalar", "activation", fte[:], t, AF.Exp,
                         wait=[(dv, None)], drain_inc=(ac, 1))
                    p.do("vector", "tensor_sub", t, ftz[:], t, wait=[(ac, None)])
                    p.do("vector", "tensor_add", t, t, fte[:])
                    p.do("vector", "tensor_single_scalar",
                         kts[:, fo * 512:(fo + 1) * 512], t, EPS, ALU.add,
                         wait=[(st_s, None)] if fo == 0 else None,
                         drain_inc=(dv, 1))
                p.do("sync", "dma_start", out=kdr[csl, :], in_=kts[:],
                     wait=[(dv, None)], inc=(st_s, 16))
                if DEBUG_TAPS and "A" in TAPS_SEL and c == 0:
                    dbgkts = nc.dram_tensor("l3a_dbgkts", [128, 1024], BF16,
                                            kind="ExternalOutput").ap()
                    p.do("sync", "dma_start", out=dbgkts[:, :], in_=kts[:],
                         inc=(st_s, 16))
                if "s" not in L3A_SUB:
                    continue
                for h in range(H):
                    hb, hg = (h % 2) * 64, h // 2
                    reg = psB[hb:hb + 64, (h % 4) * 512:(h % 4) * 512 + 128]
                    p.do("tensor", "matmul", reg,
                         kts[:, h * 64:(h + 1) * 64],
                         vts[:, h * 128:h * 128 + 128],
                         start=True, stop=True,
                         wait=[(dv, None)], inc=(pe, 1))
                    p.do("vector", "tensor_add",
                         SSf[hb:hb + 64, hg * 128:hg * 128 + 128],
                         SSf[hb:hb + 64, hg * 128:hg * 128 + 128],
                         reg,
                         wait=[(pe, None)], drain_inc=(dv, 1))
        p.do("sync", "dma_start", out=sin[:, :], in_=SSf[:], wait=[(dv, None)],
             inc=(st_s, 16))
        if L3A_PHASE < 2:
            p.wait("sync", st_s, None)
            kstk.close()
            p.barrier()
            return
        p.wait("gpsimd", st_s, None)
        p.wait("gpsimd", cc, None)
        p.do("gpsimd", "collective_compute", "AllGather", ALU.bypass,
             replica_groups=GROUPS8, ins=[sin.opt()], outs=[sout.opt()],
             inc=(cc, 1))

        for e_ in ("vector", "scalar", "tensor", "sync"):
            p.wait(e_, st_s, None)
        with ExitStack() as stk:
            ftmp = p.sbuf(stk, "l3a_ftq", [128, 512], F32)
            fte = p.sbuf(stk, "l3a_fteq", [128, 512], F32)
            ftz = p.sbuf(stk, "l3a_ftzq", [128, 512], F32)
            qsg = [p.sbuf(stk, f"l3a_qsg{i}", [128, 512], BF16) for i in range(2)]
            dq = [p.sem("d2"), p.sem("d3")]
            wqb = emit_weight_load(p, stk, "l3a_wq", inp["l3a_wqT"])
            p.wait("tensor", ac, None)
            evac = {}
            for idx in range(32):
                hg, tt = idx // 4, idx % 4
                ps = psA if idx % 2 == 0 else psB
                reg = ps[:, ((idx // 2) % 2) * 512:(((idx // 2) % 2) + 1) * 512]
                sl = slice(tt * TT, (tt + 1) * TT)
                for ci in range(NCH):
                    p.do("tensor", "matmul", reg,
                         wqb[:, ci * 1024 + hg * 128:ci * 1024 + (hg + 1) * 128],
                         G["xb"][ci][:, sl], start=(ci == 0), stop=(ci == NCH - 1),
                         wait=[(dv, evac.get(idx - 4))] if ci == 0 else None,
                         inc=(pe, 1) if ci == NCH - 1 else None)
                p.do("vector", "tensor_single_scalar", ftz[:], reg,
                     bqc[:, hg:hg + 1], ALU.add, wait=[(pe, None)])
                p.do("vector", "tensor_single_scalar", ftmp[:], ftz[:], 0.0,
                     ALU.min, drain_inc=(dv, 1))
                p.do("scalar", "activation", fte[:], ftmp[:], AF.Exp,
                     wait=[(dv, None)], drain_inc=(ac, 1))
                p.do("vector", "tensor_sub", ftmp[:], ftz[:], ftmp[:],
                     wait=[(ac, None)])
                p.do("vector", "tensor_add", ftmp[:], ftmp[:], fte[:])
                p.do("vector", "tensor_single_scalar", ftmp[:], ftmp[:], EPS,
                     ALU.add)
                p.do("vector", "tensor_single_scalar",
                     qsg[idx % 2][:], ftmp[:],
                     0.125, ALU.mult, wait=[(dq[idx % 2], None)],
                     drain_inc=(dv, 1))
                for j in range(4):
                    cc_ = tt * 4 + j
                    p.do("sync", "dma_start",
                         out=qdr[cc_ * 128:(cc_ + 1) * 128,
                                 hg * 128:(hg + 1) * 128],
                         in_=qsg[idx % 2][:, j * 128:(j + 1) * 128],
                         wait=[(dv, None)] if j == 0 else None,
                         inc=(dq[idx % 2], 16))
                evac[idx] = dv.n
            p.wait("sync", dq[0], None)
            p.wait("sync", dq[1], None)
        if L3A_PHASE < 3:
            kstk.close()
            p.barrier()
            return

        with ExitStack() as stk:
            pred = p.sbuf(stk, "l3a_pred", [128, 1024], F32)
            pidx_s = p.sbuf(stk, "l3a_pidxs", [128, 3], U32)
            p.do("sync", "dma_start", out=pidx_s[:], in_=inp["l1a_pidx"][:, :],
                 inc=(ld, 16))
            p.do("vector", "memset", SSf[:], 0.0, wait=[(st_s, None)],
                 drain_inc=(dv, 1))
            for j in range(3):
                p.do("gpsimd", "memset", pred[:], 0.0,
                     wait=[(cc, None), (ld, None)] if j == 0 else [(dv, None)])
                p.do("gpsimd", "indirect_dma_start", out=pred[:], out_offset=None,
                     in_=sout[:, :],
                     in_offset=IndirectOffsetOnAxis(ap=pidx_s[:, j:j + 1], axis=0),
                     bounds_check=p.bound_reg(1023), oob_is_err=False,
                     inc=(gp, 16))
                p.do("vector", "tensor_add", SSf[:], SSf[:], pred[:],
                     wait=[(gp, None)], drain_inc=(dv, 1))
            for h in range(16):
                hb, hg = (h % 2) * 64, h // 2
                p.do("vector", "tensor_copy",
                     SSz[hb:hb + 64, h * 128:h * 128 + 128],
                     SSf[hb:hb + 64, hg * 128:hg * 128 + 128],
                     drain_inc=(dv, 1) if h == 15 else None)
            if DEBUG_TAPS and "B" in TAPS_SEL:
                dbgssz = nc.dram_tensor("l3a_dbgssz", [128, 2048], BF16,
                                        kind="ExternalOutput").ap()
                dbgssf = nc.dram_tensor("l3a_dbgssf", [128, 1024], F32,
                                        kind="ExternalOutput").ap()
                p.do("sync", "dma_start", out=dbgssz[:, :], in_=SSz[:],
                     wait=[(dv, None)], inc=(st_s, 16))
                p.do("sync", "dma_start", out=dbgssf[:, :], in_=SSf[:],
                     inc=(st_s, 16))
                p.wait("vector", st_s, None)

        if L3A_PHASE < 4:
            kstk.close()
            p.barrier()
            return
        p.wait("sync", dv, None)
        p.wait("gpsimd", dv, None)
        for e_ in ("vector", "scalar", "tensor", "sync"):
            p.wait(e_, p.sem("d2"), None)
            p.wait(e_, p.sem("d3"), None)
        O_fm = [p.sbuf(stk0, f"l3a_ofm{c}", [128, T], BF16, side="right")
                for c in range(NCH)]
        with ExitStack() as stk:
            kbuf = [p.sbuf(stk, f"l3a_kb{i}", [128, 1024], BF16) for i in range(2)]
            vbuf = [p.sbuf(stk, f"l3a_vb{i}", [128, 2048], BF16) for i in range(2)]
            qbuf = [p.sbuf(stk, f"l3a_qb{i}", [128, 1024], BF16) for i in range(2)]
            kxs = p.sbuf(stk, "l3a_kxs", [128, 1024], BF16)
            Ps = p.sbuf(stk, "l3a_ps", [128, 128], BF16)
            otok = p.sbuf(stk, "l3a_otok", [128, 1024], BF16)
            ftz = p.sbuf(stk, "l3a_ftzk", [128, 128], F32)
            ftmp = p.sbuf(stk, "l3a_ftmk", [128, 128], F32)
            fte = p.sbuf(stk, "l3a_ftek", [128, 128], F32)
            dens = p.sbuf(stk, "l3a_dens", [128, 1], F32)
            recs = p.sbuf(stk, "l3a_recs", [128, 1], F32)
            p.wait("tensor", ac, None)

            def kvload(c, wait=None):
                par = c % 2
                csl = slice(c * 128, (c + 1) * 128)
                p.do("sync", "dma_start", out=kbuf[par][:], in_=kdr[csl, :],
                     wait=wait, inc=(dsA[par], 16))
                p.do("sync", "dma_start", out=vbuf[par][:], in_=vdr[csl, :],
                     inc=(dsA[par], 16))
                p.do("sync", "dma_start", out=qbuf[par][:], in_=qdr[csl, :],
                     inc=(dsA[par], 16))
                return dsA[par].n

            if DEBUG_TAPS and "C" in TAPS_SEL:
                dbgkd = nc.dram_tensor("l3a_dbgkd", [128, 1024], BF16,
                                       kind="ExternalOutput").ap()
                p.do("sync", "dma_start", out=dbgkd[:, :], in_=kdr[0:128, :],
                     wait=[(st_s, None)], inc=(st_s, 16))
            kvr = {0: kvload(0, wait=[(st_s, None)])}
            for c in range(NC):
                if c + 1 < NC:
                    kvr[c + 1] = kvload(c + 1, wait=[(pe, None)])
                par = c % 2
                csl = slice(c * 128, (c + 1) * 128)
                for rnd in range(2):
                    for hq in range(4):
                        hg = rnd * 4 + hq
                        ps = psA if hq % 2 == 0 else psB
                        reg = ps[:, (hq // 2) * 512:(hq // 2) * 512 + 128]
                        for ci in range(NCH):
                            p.do("tensor", "matmul", reg,
                                 wkb[:, ci * 1024 + hg * 128:ci * 1024 + (hg + 1) * 128],
                                 G["xb"][ci][:, csl],
                                 start=(ci == 0), stop=(ci == NCH - 1),
                                 wait=[(dv, None), (ac, None)]
                                 if (ci == 0 and hq == 0) else None,
                                 inc=(pe, 1) if ci == NCH - 1 else None)
                    for hq in range(4):
                        hg = rnd * 4 + hq
                        ps = psA if hq % 2 == 0 else psB
                        reg = ps[:, (hq // 2) * 512:(hq // 2) * 512 + 128]
                        t = ftz[:, 0:128]
                        m = ftmp[:, 0:128]
                        e = fte[:, 0:128]
                        p.do("vector", "tensor_single_scalar", t, reg,
                             bkc[:, hg:hg + 1], ALU.add,
                             wait=[(pe, None)] if hq == 0 else None)
                        p.do("vector", "tensor_single_scalar", m, t, 0.0,
                             ALU.min, drain_inc=(dv, 1))
                        p.do("scalar", "activation", e, m, AF.Exp,
                             wait=[(dv, None)], drain_inc=(ac, 1))
                        p.do("vector", "tensor_sub", m, t, m,
                             wait=[(ac, None)])
                        p.do("vector", "tensor_add", m, m, e)
                        p.do("vector", "tensor_single_scalar",
                             kxs[:, hg * 128:(hg + 1) * 128], m, EPS, ALU.add,
                             drain_inc=(dv, 1))
                if DEBUG_TAPS and "F" in TAPS_SEL and c == 1:
                    dbgsz1 = nc.dram_tensor("l3a_dbgsz1", [128, 2048], BF16,
                                            kind="ExternalOutput").ap()
                    p.do("sync", "dma_start", out=dbgsz1[:, :], in_=SSz[:],
                         wait=[(dv, None)], inc=(st_s, 16))
                    p.wait("vector", st_s, None)
                p.wait("tensor", dsA[par], kvr[c])
                for h in range(H):
                    hb, hg = (h % 2) * 64, h // 2
                    pP = psA[:, (h % 2) * 512:(h % 2) * 512 + 128]
                    pS = psA[hb:hb + 64, 1024 + (h % 2) * 512:1024 + (h % 2) * 512 + 128]
                    pO = psB[:, 1024 + (h % 2) * 512:1024 + (h % 2) * 512 + 128]
                    p.do("tensor", "matmul", pP,
                         kxs[hb:hb + 64, hg * 128:(hg + 1) * 128],
                         qbuf[par][hb:hb + 64, hg * 128:(hg + 1) * 128],
                         start=True, stop=True, wait=[(dv, None)], inc=(pe, 1))
                    p.do("vector", "tensor_mul", Ps[:], pP, G["u128"],
                         wait=[(pe, None)], drain_inc=(dv, 1))
                    p.do("tensor", "matmul", pO, Ps[:],
                         vbuf[par][:, h * 128:h * 128 + 128],
                         start=True, stop=False,
                         wait=[(dv, None), (ac, None)])
                    p.do("tensor", "matmul", pO,
                         qbuf[par][:, hg * 128:(hg + 1) * 128],
                         SSz[:, h * 128:h * 128 + 128],
                         start=False, stop=True, inc=(pe, 1))
                    if DEBUG_TAPS and "D" in TAPS_SEL and c == 0 and h == 0:
                        dbgpo = nc.dram_tensor("l3a_dbgpo", [128, 128], F32,
                                               kind="ExternalOutput").ap()
                        dbgps = nc.dram_tensor("l3a_dbgps", [128, 128], BF16,
                                               kind="ExternalOutput").ap()
                        pot = p.sbuf(stk, "l3a_pot", [128, 128], F32)
                        p.do("vector", "tensor_copy", pot[:], pO,
                             wait=[(pe, None)], drain_inc=(dv, 1))
                        p.do("sync", "dma_start", out=dbgpo[:, :], in_=pot[:],
                             wait=[(dv, None)], inc=(st_s, 16))
                        p.do("sync", "dma_start", out=dbgps[:, :], in_=Ps[:],
                             inc=(st_s, 16))
                        p.wait("vector", st_s, None)
                    p.do("vector", "tensor_single_scalar", dens[:, 0:1],
                         pO[:, 64:65], epsc[:, 0:1], ALU.add,
                         wait=[(pe, None), (ac, None)], drain_inc=(dv, 1))
                    p.do("vector", "reciprocal", recs[:, 0:1], dens[:, 0:1],
                         drain_inc=(dv, 1))
                    p.do("scalar", "activation", otok[:, h * 64:(h + 1) * 64],
                         pO[:, 0:64], AF.Copy, scale=recs[:, 0:1],
                         wait=[(dv, None)], drain_inc=(ac, 1))
                    p.do("tensor", "matmul", pS[:, 0:128],
                         kbuf[par][:, h * 64:(h + 1) * 64],
                         vbuf[par][:, h * 128:h * 128 + 128],
                         start=True, stop=True, wait=[(dv, None)],
                         inc=(pe, 1))
                    p.do("vector", "tensor_add",
                         SSf[hb:hb + 64, hg * 128:hg * 128 + 128],
                         SSf[hb:hb + 64, hg * 128:hg * 128 + 128], pS[:, 0:128],
                         wait=[(pe, None)])
                    p.do("vector", "tensor_copy",
                         SSz[hb:hb + 64, h * 128:h * 128 + 128],
                         SSf[hb:hb + 64, hg * 128:hg * 128 + 128],
                         drain_inc=(dv, 1))
                if DEBUG_TAPS and "F" in TAPS_SEL and c == 1:
                    dbgo1 = nc.dram_tensor("l3a_dbgo1", [128, 1024], BF16,
                                           kind="ExternalOutput").ap()
                    p.do("sync", "dma_start", out=dbgo1[:, :], in_=otok[:],
                         wait=[(ac, None), (dv, None)], inc=(st_s, 16))
                    p.wait("vector", st_s, None)
                    p.wait("scalar", st_s, None)
                if DEBUG_TAPS and "E" in TAPS_SEL and c == 0:
                    for nm, tile in (("dbgO", otok[:]), ("dbgkx", kxs[:]),
                                     ("dbgq", qbuf[0][:]), ("dbgk2", kbuf[0][:]),
                                     ("dbgv", vbuf[0][:])):
                        dt_ = nc.dram_tensor("l3a_" + nm, list(tile.shape), BF16,
                                             kind="ExternalOutput").ap()
                        p.do("sync", "dma_start", out=dt_[:, :], in_=tile,
                             wait=[(ac, None), (dv, None)], inc=(st_s, 16))
                    p.wait("vector", st_s, None)
                    p.wait("scalar", st_s, None)
                p.wait("tensor", ac, None)
                for ci in range(NCH):
                    tps = psA[:, (ci % 2) * 512:(ci % 2) * 512 + 64].bitcast(BF16)
                    p.do("tensor", "transpose", tps,
                         otok[:, ci * 128:(ci + 1) * 128], G["idt"],
                         wait=[(dv, None)], inc=(pe, 1))
                    p.do("vector", "tensor_copy", O_fm[ci][:, csl], tps,
                         wait=[(pe, None)], drain_inc=(dv, 1))

        kstk.close()
        if L3A_PHASE < 5:
            p.barrier()
            return
        with ExitStack() as stk:
            t1 = [p.sbuf(stk, f"l3a_t1{c}", [128, T], BF16) for c in range(NCH)]
            boc = p.sbuf(stk, "l3a_boc_s", [128, 8], F32)
            gma = p.sbuf(stk, "l3a_gma_s", [128, 8], F32)
            bta = p.sbuf(stk, "l3a_bta_s", [128, 8], F32)
            tmp = p.sbuf(stk, "l3a_tmp", [128, T], F32)
            p.do("sync", "dma_start", out=boc[:], in_=inp["l3a_boc"][:, :],
                 wait=[(pe, None), (dv, None)], inc=(ld, 16))
            p.do("sync", "dma_start", out=gma[:], in_=inp["l3a_gma"][:, :],
                 inc=(ld, 16))
            p.do("sync", "dma_start", out=bta[:], in_=inp["l3a_bta"][:, :],
                 inc=(ld, 16))
            p.wait("scalar", ld, None)
            with ExitStack() as wstk:
                wob = emit_weight_load(p, wstk, "l3a_wo", inp["l3a_woT"])
                p.wait("tensor", ac, None)
                p.wait("tensor", dv, None)
                evac_dv = {}
                for co in range(NCH):
                    ps = psA if co % 2 == 0 else psB
                    for tt in range(NTT):
                        sl = slice(tt * TT, (tt + 1) * TT)
                        w = [(dv, evac_dv[co - 2])] if (co >= 2 and tt == 0) else None
                        for ci in range(NCH):
                            p.do("tensor", "matmul", ps[:, sl],
                                 wob[:, ci * 1024 + co * 128:ci * 1024 + (co + 1) * 128],
                                 O_fm[ci][:, sl], start=(ci == 0),
                                 stop=(ci == NCH - 1),
                                 wait=w if ci == 0 else None,
                                 inc=(pe, 1) if (tt == NTT - 1 and ci == NCH - 1) else None)
                    p.do("vector", "tensor_single_scalar", t1[co][:, 0:T],
                         ps[:, 0:T], boc[:, co:co + 1], ALU.add,
                         wait=[(pe, None)], drain_inc=(dv, 1))
                    evac_dv[co] = dv.n
            ln_affine_resid(p, G, [t[:] for t in t1], gma[:], bta[:], tmp[:],
                            ready=[(dv, dv.n)])
            ln_feature_major(p, G, G["xs"], G["xs"], G["xb"], tmp[:],
                             ready=[(dv, dv.n)])
    p.barrier()


def ln_affine_out(p, G, r_chunks, gcol, bcol, out_b16, tmp, ready=()):
    psA, psB = G["psA"], G["psB"]
    ones, onesrow = G["ones"], G["onesrow"]
    rowmu, rowrs, eps1 = G["rowmu"], G["rowrs"], G["eps1"]
    pe, dv, ac = p.sem("pe"), p.sem("dv"), p.sem("ac")
    for s, v in ready:
        p.wait("tensor", s, v)
        p.wait("scalar", s, v)
        p.wait("vector", s, v)
    for tt in range(NTT):
        sl = slice(tt * TT, (tt + 1) * TT)
        for c in range(NCH):
            p.do("tensor", "matmul", psA[0:1, sl], ones[:], r_chunks[c][:, sl],
                 start=(c == 0), stop=(c == NCH - 1))
    for c in range(NCH):
        p.do("scalar", "activation", tmp[:], r_chunks[c][:, 0:T], AF.Square,
             drain_inc=(ac, 1))
        for tt in range(NTT):
            sl = slice(tt * TT, (tt + 1) * TT)
            p.do("tensor", "matmul", psA[32:33, sl], ones[:], tmp[:, sl],
                 start=(c == 0), stop=(c == NCH - 1),
                 wait=[(ac, None)] if tt == 0 else None,
                 inc=(pe, 1) if tt == NTT - 1 else None)
        if c < NCH - 1:
            p.wait("scalar", pe, pe.n)
    p.do("vector", "tensor_scalar_mul", rowmu[0:1, :], psA[0:1, 0:2048], 1.0 / D,
         wait=[(pe, None)])
    p.do("vector", "tensor_mul", rowrs[0:1, :], rowmu[0:1, :], rowmu[0:1, :])
    p.do("vector", "scalar_tensor_tensor", rowrs[0:1, :], psA[32:33, 0:2048],
         1.0 / D, rowrs[0:1, :], ALU.mult, ALU.subtract)
    p.do("vector", "tensor_single_scalar", rowrs[0:1, :], rowrs[0:1, :],
         eps1[0:1, 0:1], ALU.add, drain_inc=(dv, 1))
    p.do("scalar", "activation", rowrs[0:1, :], rowrs[0:1, :], AF.Ln,
         wait=[(dv, None)], drain_inc=(ac, 1))
    p.do("scalar", "activation", rowrs[0:1, :], rowrs[0:1, :], AF.Exp,
         scale=-0.5, drain_inc=(ac, 1))
    p.wait("tensor", ac, ac.n)
    for tt in range(NTT):
        sl = slice(tt * TT, (tt + 1) * TT)
        p.do("tensor", "matmul", psA[:, sl], onesrow[0:1, :], rowmu[0:1, sl],
             start=True, stop=True)
        p.do("tensor", "matmul", psB[:, sl], onesrow[0:1, :], rowrs[0:1, sl],
             start=True, stop=True,
             inc=(pe, 1) if tt == NTT - 1 else None)
    p.wait("vector", pe, pe.n)
    for c in range(NCH):
        p.do("vector", "tensor_sub", tmp[:], r_chunks[c][:, 0:T], psA[:, 0:2048])
        p.do("vector", "tensor_mul", tmp[:], tmp[:], psB[:, 0:2048])
        p.do("vector", "tensor_single_scalar", tmp[:], tmp[:],
             gcol[:, c:c + 1], ALU.mult)
        p.do("vector", "tensor_single_scalar", out_b16[c][:, 0:T], tmp[:],
             bcol[:, c:c + 1], ALU.add, inc=(dv, 1))
    p.wait("tensor", dv, dv.n)


def build_l2_ffn(p, G):
    nc = p.nc
    ld, st_s, pe, dv, ac, gp, cc = [p.sem(s) for s in
                                    ("ld", "st", "pe", "dv", "ac", "gp", "cc")]
    psA, psB = G["psA"], G["psB"]
    inp = G["inp"]
    GROUPS8 = [[0, 1, 2, 3, 4, 5, 6, 7]]
    H2 = 8
    bnd_in = nc.dram_tensor("l2f_bi", [1, 512], F32).ap()
    bnd_out = nc.dram_tensor("l2f_bo", [8, 512], F32).ap()
    bnd_rt = nc.dram_tensor("l2f_brt", [1, 128], F32).ap()
    kvx_in = nc.dram_tensor("l2f_ki", [512, 1024], BF16).ap()
    kvx_out = nc.dram_tensor("l2f_ko", [4096, 1024], BF16).ap()
    w1b = nc.dram_tensor("l2f_w1b", [1024, 5632], BF16).ap()
    w2b = nc.dram_tensor("l2f_w2b", [2816, 1024], BF16).ap()
    wqb_dr = nc.dram_tensor("l2f_wqb", [1024, 1536], BF16).ap()
    xodr = nc.dram_tensor("l2f_xod", [1024, T], BF16).ap()
    qdr2 = nc.dram_tensor("l2f_qd2", [2048, 512], BF16).ap()
    d0, d1 = p.sem("d0"), p.sem("d1")
    d2, d3 = p.sem("d2"), p.sem("d3")

    with ExitStack() as stk0:
        bidx = p.sbuf(stk0, "l2f_bidxs", [128, 1], U32)
        kvidx = p.sbuf(stk0, "l2f_kvidxs", [128, 4], U32)
        p.do("sync", "dma_start", out=bidx[:], in_=inp["l2f_bidx"][:, :],
             inc=(ld, 16))
        p.do("sync", "dma_start", out=kvidx[:], in_=inp["l2f_kvidx"][:, :],
             inc=(ld, 16))
        p.wait("gpsimd", ld, None)
        for c in range(NCH):
            p.do("sync", "dma_start", out=xodr[c * 128:(c + 1) * 128, :],
                 in_=G["xb"][c], inc=(st_s, 16))
        for l in range(2):
            lstk = ExitStack()
            with ExitStack() as stk:
                wst = p.sbuf(stk, f"l2f_wst{l}", [128, 5632], F32)
                wsb = p.sbuf(stk, f"l2f_wsb{l}", [128, 5632], BF16)
                pw = [(pe, None), (dv, None), (ac, None),
                      (d0, None), (d1, None), (d2, None), (d3, None)]
                for ci in range(NCH):
                    p.do("sync", "dma_start", out=wst[:],
                         in_=inp["l2f_w1p"][l, ci * 128:(ci + 1) * 128, :],
                         wait=(pw if ci == 0 else [(ac, None)]), inc=(ld, 16))
                    p.do("scalar", "activation", wsb[:], wst[:], AF.Copy,
                         wait=[(ld, None)], drain_inc=(ac, 1))
                    p.do("sync", "dma_start",
                         out=w1b[ci * 128:(ci + 1) * 128, :], in_=wsb[:],
                         wait=[(ac, None)], inc=(st_s, 16))
                for ci in range(22):
                    p.do("sync", "dma_start", out=wst[:, 0:1024],
                         in_=inp["l2f_w2p"][l, ci * 128:(ci + 1) * 128, :],
                         wait=[(ac, None)], inc=(ld, 16))
                    p.do("scalar", "activation", wsb[:, 0:1024], wst[:, 0:1024],
                         AF.Copy, wait=[(ld, None)], drain_inc=(ac, 1))
                    p.do("sync", "dma_start",
                         out=w2b[ci * 128:(ci + 1) * 128, :], in_=wsb[:, 0:1024],
                         wait=[(ac, None)], inc=(st_s, 16))
                for ci in range(NCH):
                    p.do("sync", "dma_start", out=wst[:, 0:1536],
                         in_=inp["l2f_wqkv"][l, ci * 128:(ci + 1) * 128, :],
                         wait=[(ac, None)], inc=(ld, 16))
                    p.do("scalar", "activation", wsb[:, 0:1536], wst[:, 0:1536],
                         AF.Copy, wait=[(ld, None)], drain_inc=(ac, 1))
                    p.do("sync", "dma_start",
                         out=wqb_dr[ci * 128:(ci + 1) * 128, :],
                         in_=wsb[:, 0:1536], wait=[(ac, None)], inc=(st_s, 16))
                p.wait("vector", ac, None)
                p.wait("sync", st_s, None)
            hb0 = [p.sbuf(lstk, f"l2f_hb0{j}_{l}", [128, 128], BF16)
                   for j in range(4)]
            with ExitStack() as stk:
                bg = p.sbuf(stk, f"l2f_bg{l}", [128, 512], F32)
                bcolt = p.sbuf(stk, f"l2f_bcol{l}", [128, 4], F32)
                bic = bnd_in.rearrange("a (b c) -> (a b) c", c=1)
                for j in range(4):
                    p.do("sync", "dma_start",
                         out=bic[j * 128:(j + 1) * 128, :],
                         in_=G["xs"][4 + j][:, T - 1:T],
                         wait=[(dv, None), (ac, None)] if j == 0 else None,
                         inc=(st_s, 16))
                p.wait("gpsimd", st_s, None)
                p.wait("gpsimd", cc, None)
                p.do("gpsimd", "collective_compute", "AllGather", ALU.bypass,
                     replica_groups=GROUPS8, ins=[bnd_in.opt()],
                     outs=[bnd_out.opt()], inc=(cc, 1))
                p.do("gpsimd", "memset", bg[:], 0.0, wait=[(cc, None)])
                p.do("gpsimd", "indirect_dma_start", out=bg[:], out_offset=None,
                     in_=bnd_out[:, :],
                     in_offset=IndirectOffsetOnAxis(ap=bidx[:, 0:1], axis=0),
                     bounds_check=p.bound_reg(7), oob_is_err=False,
                     inc=(gp, 16))
                brc = bnd_rt.rearrange("a (b c) -> (a b) c", c=1)
                for j in range(4):
                    p.do("sync", "dma_start", out=bnd_rt[0:1, :],
                         in_=bg[0:1, j * 128:(j + 1) * 128],
                         wait=[(gp, None)] if j == 0 else [(ld, None)],
                         inc=(st_s, 16))
                    p.do("sync", "dma_start", out=bcolt[:, j:j + 1],
                         in_=brc[0:128, :], wait=[(st_s, None)], inc=(ld, 16))
                for j in range(4):
                    p.do("vector", "tensor_copy", hb0[j][:, 1:128],
                         G["xb"][4 + j][:, 0:127], wait=[(ld, None)])
                    p.do("vector", "tensor_copy", hb0[j][:, 0:1],
                         bcolt[:, j:j + 1], drain_inc=(dv, 1))
            ostk = ExitStack()
            o_fm = [p.sbuf(ostk, f"l2f_ofm{i}_{l}", [128, T], BF16)
                    for i in range(4)]
            kvstk = ExitStack()
            k_fm = p.sbuf(kvstk, f"l2f_kfm{l}", [128, 4 * 2560], BF16)
            var_ = p.sbuf(kvstk, f"l2f_var{l}", [128, 20 * 512], BF16)
            with ExitStack() as stk:
                wq = p.sbuf(stk, f"l2f_wqs{l}", [128, 8 * 1536], BF16)
                for ci in range(NCH):
                    p.do("sync", "dma_start",
                         out=wq[:, ci * 1536:(ci + 1) * 1536],
                         in_=wqb_dr[ci * 128:(ci + 1) * 128, :],
                         wait=[(st_s, None)] if ci == 0 else None,
                         inc=(ld, 16))
                zs = p.sbuf(stk, f"l2f_zs{l}", [128, 512], F32)
                nrm = p.sbuf(stk, f"l2f_nrm{l}", [128, 8], F32)
                nrs = p.sbuf(stk, f"l2f_nrs{l}", [128, 8], F32)
                nrr = p.sbuf(stk, f"l2f_nrr{l}", [128, 8], F32)
                qn = p.sbuf(stk, f"l2f_qn{l}", [128, 512], F32)
                qtms = p.sbuf(stk, f"l2f_qtm{l}", [128, 512], BF16)
                ktms = p.sbuf(stk, f"l2f_ktm{l}", [128, 512], BF16)
                qstg = [p.sbuf(stk, f"l2f_qs{i}_{l}", [128, 512], BF16)
                        for i in range(2)]
                rot = [p.sbuf(stk, f"l2f_rot{i}_{l}", [128, 1024], F32)
                       for i in range(1)]
                scl = p.sbuf(stk, f"l2f_scl{l}", [128, 128], F32)
                sclr = p.sbuf(stk, f"l2f_sclr{l}", [1, 128], F32)
                p.do("sync", "dma_start", out=sclr[:, 0:64],
                     in_=inp["l2f_qs"][l:l + 1, :], inc=(ld, 16))
                p.do("sync", "dma_start", out=sclr[:, 64:128],
                     in_=inp["l2f_ks"][l:l + 1, :], inc=(ld, 16))
                p.do("vector", "tensor_scalar_mul", sclr[0:1, 0:64],
                     sclr[0:1, 0:64], 8.0, wait=[(ld, None)], drain_inc=(dv, 1))
                p.do("tensor", "matmul", psB[:, 1024:1152],
                     G["onesrow"][0:1, :], sclr[0:1, :], start=True, stop=True,
                     wait=[(dv, None)], inc=(pe, 1))
                p.do("vector", "tensor_copy", scl[:], psB[:, 1024:1152],
                     wait=[(pe, None)], drain_inc=(dv, 1))
                p.wait("tensor", ld, None)
                p.wait("tensor", dv, None)
                p.wait("tensor", ac, None)

                def rotload(tch, qk):
                    p.do("sync", "dma_start", out=rot[0][:],
                         in_=inp["l2f_rotc"][tch * 128:(tch + 1) * 128,
                                             qk * 1024:(qk + 1) * 1024],
                         wait=[(dv, None)], inc=(d0, 16))
                    return d0.n

                rr = {}
                for tch in range(16):
                    par = tch % 2
                    dse = d0
                    csl = slice(tch * 128, (tch + 1) * 128)
                    for ri, (reg, fo0) in enumerate((
                            (psA[:, 0:512], 0), (psA[:, 512:1024], 512),
                            (psB[:, 0:512], 1024))):
                        for ci in range(NCH):
                            lt = (hb0[ci - 4][:] if (tch == 0 and ci >= 4)
                                  else (G["xb"][ci][:, csl] if ci < 4 else
                                        G["xb"][ci][:, tch * 128 - 1:
                                                     (tch + 1) * 128 - 1]))
                            p.do("tensor", "matmul", reg, lt,
                                 wq[:, ci * 1536 + fo0:ci * 1536 + fo0 + 512],
                                 start=(ci == 0), stop=(ci == NCH - 1),
                                 wait=[(dv, None), (ac, None)]
                                 if (ci == 0 and ri == 0) else None,
                                 inc=(pe, 1) if ci == NCH - 1 else None)
                    p.do("vector", "tensor_copy",
                         var_[:, (4 + tch) * 512:(5 + tch) * 512],
                         psB[:, 0:512], wait=[(pe, None)], drain_inc=(dv, 1))
                    for qk in range(2):
                        zreg = psA[:, 0:512] if qk == 0 else psA[:, 512:1024]
                        p.do("vector", "tensor_copy", zs[:], zreg,
                             wait=[(pe, None)] if qk == 0 else None,
                             drain_inc=(dv, 1))
                        p.do("scalar", "activation", qn[:], zs[:], AF.Square,
                             wait=[(dv, None)], drain_inc=(ac, 1))
                        for h in range(H2):
                            p.do("vector", "reduce_sum", nrm[:, h:h + 1],
                                 qn[:, h * 64:(h + 1) * 64], AX.X,
                                 wait=[(ac, None)] if h == 0 else None,
                                 drain_inc=(dv, 1) if h == H2 - 1 else None)
                        p.do("scalar", "activation", nrs[:], nrm[:], AF.Sqrt,
                             wait=[(dv, None)], drain_inc=(ac, 1))
                        p.do("vector", "tensor_single_scalar", nrs[:], nrs[:],
                             1e-12, ALU.max, wait=[(ac, None)])
                        p.do("vector", "reciprocal", nrr[:], nrs[:],
                             drain_inc=(dv, 1))
                        for h in range(H2):
                            p.do("vector", "tensor_single_scalar",
                                 qn[:, h * 64:(h + 1) * 64],
                                 zs[:, h * 64:(h + 1) * 64],
                                 nrr[:, h:h + 1], ALU.mult,
                                 wait=[(dv, None)] if h == 0 else None)
                        for h in range(H2):
                            p.do("vector", "tensor_mul",
                                 qn[:, h * 64:(h + 1) * 64],
                                 qn[:, h * 64:(h + 1) * 64],
                                 scl[:, qk * 64:qk * 64 + 64])
                        rcur = rotload(tch, qk)
                        ro = rot[0][:, 0:1024]
                        for h in range(H2):
                            p.do("vector", "tensor_mul",
                                 zs[:, h * 64:h * 64 + 32],
                                 qn[:, h * 64 + 32:h * 64 + 64],
                                 ro[:, 512 + h * 64:512 + h * 64 + 32],
                                 wait=[(d0, rcur)] if h == 0 else None)
                            p.do("vector", "tensor_mul",
                                 zs[:, h * 64 + 32:h * 64 + 64],
                                 qn[:, h * 64:h * 64 + 32],
                                 ro[:, 512 + h * 64 + 32:512 + h * 64 + 64])
                        p.do("vector", "tensor_mul", qn[:], qn[:],
                             ro[:, 0:512])
                        p.do("vector", "tensor_add",
                             (qtms[:] if qk == 0 else ktms[:]), qn[:], zs[:],
                             drain_inc=(dv, 1))
                        for hg in range(4):
                            tps = psB[:, 1024 + (hg % 2) * 512:
                                      1024 + (hg % 2) * 512 + 64].bitcast(BF16)
                            src = (qtms if qk == 0 else
                                   ktms)[:, hg * 128:(hg + 1) * 128]
                            p.do("tensor", "transpose", tps, src, G["idt"],
                                 wait=[(dv, None)], inc=(pe, 1))
                            if qk == 0:
                                dcol = qstg[par][:, hg * 128:(hg + 1) * 128]
                            else:
                                dcol = k_fm[:, hg * 2560 + 512 + tch * 128:
                                            hg * 2560 + 512 + (tch + 1) * 128]
                            p.do("vector", "tensor_copy", dcol, tps,
                                 wait=[(pe, None)], drain_inc=(dv, 1))
                    p.do("sync", "dma_start",
                         out=qdr2[tch * 128:(tch + 1) * 128, :],
                         in_=qstg[par][:], wait=[(dv, None)], inc=(dse, 16))
                    if tch >= 12:
                        p.do("sync", "dma_start",
                             out=kvx_in[(tch - 12) * 128:(tch - 11) * 128,
                                        0:512],
                             in_=ktms[:], inc=(st_s, 16))
                        p.do("sync", "dma_start",
                             out=kvx_in[(tch - 12) * 128:(tch - 11) * 128,
                                        512:1024],
                             in_=var_[:, (4 + tch) * 512:(5 + tch) * 512],
                             inc=(st_s, 16))
                        p.wait("vector", st_s, None)
                p.wait("vector", d0, None)
            with ExitStack() as stk:
                gk = p.sbuf(stk, f"l2f_gk{l}", [128, 1024], BF16)
                p.wait("gpsimd", st_s, None)
                p.wait("gpsimd", cc, None)
                p.do("gpsimd", "collective_compute", "AllGather", ALU.bypass,
                     replica_groups=GROUPS8, ins=[kvx_in.opt()],
                     outs=[kvx_out.opt()], inc=(cc, 1))
                for j in range(4):
                    p.do("gpsimd", "memset", gk[:], 0.0,
                         wait=[(cc, None)] if j == 0 else [(dv, None)])
                    p.do("gpsimd", "indirect_dma_start", out=gk[:],
                         out_offset=None, in_=kvx_out[:, :],
                         in_offset=IndirectOffsetOnAxis(ap=kvidx[:, j:j + 1],
                                                        axis=0),
                         bounds_check=p.bound_reg(4095), oob_is_err=False,
                         inc=(gp, 16))
                    p.do("vector", "tensor_copy",
                         var_[:, j * 512:(j + 1) * 512], gk[:, 512:1024],
                         wait=[(gp, None)], drain_inc=(dv, 1))
                    for hg in range(4):
                        tps = psB[:, 1024 + (hg % 2) * 512:
                                  1024 + (hg % 2) * 512 + 64].bitcast(BF16)
                        p.do("tensor", "transpose", tps,
                             gk[:, hg * 128:(hg + 1) * 128], G["idt"],
                             wait=[(dv, None)], inc=(pe, 1))
                        p.do("vector", "tensor_copy",
                             k_fm[:, hg * 2560 + j * 128:
                                  hg * 2560 + (j + 1) * 128], tps,
                             wait=[(pe, None)], drain_inc=(dv, 1))
            with ExitStack() as stk:
                mt = [p.sbuf(stk, f"l2f_mt{i}_{l}", [128, 1024], F32)
                      for i in range(2)]
                qb2 = [p.sbuf(stk, f"l2f_qb2{i}_{l}", [128, 512], BF16)
                       for i in range(2)]
                se = p.sbuf(stk, f"l2f_se{l}", [128, 1024], F32)
                mx1a = p.sbuf(stk, f"l2f_mx1a{l}", [128, 1], F32)
                mx1b = p.sbuf(stk, f"l2f_mx1b{l}", [128, 1], F32)
                mx2 = p.sbuf(stk, f"l2f_mx2{l}", [128, 1], F32)
                nmx = p.sbuf(stk, f"l2f_nmx{l}", [128, 1], F32)
                sm1a = p.sbuf(stk, f"l2f_sm1a{l}", [128, 1], F32)
                sm1b = p.sbuf(stk, f"l2f_sm1b{l}", [128, 1], F32)
                smt = p.sbuf(stk, f"l2f_smt{l}", [128, 1], F32)
                rcp = p.sbuf(stk, f"l2f_rcp{l}", [128, 1], F32)
                pb = p.sbuf(stk, f"l2f_pb{l}", [128, 1024], BF16)
                ptb = p.sbuf(stk, f"l2f_ptb{l}", [128, 1024], BF16)
                otm = p.sbuf(stk, f"l2f_otm{l}", [128, 512], BF16)

                def mqload(qc, wait=None):
                    par = qc % 2
                    dse = d2 if par == 0 else d3
                    p.do("sync", "dma_start", out=mt[par][:],
                         in_=inp["l2f_amask"][qc * 128:(qc + 1) * 128, :],
                         wait=wait, inc=(dse, 16))
                    p.do("sync", "dma_start", out=qb2[par][:],
                         in_=qdr2[qc * 128:(qc + 1) * 128, :],
                         inc=(dse, 16))
                    return dse.n

                mr = {0: mqload(0, wait=[(d0, None), (d1, None)])}
                for qc in range(16):
                    if qc + 1 < 16:
                        mr[qc + 1] = mqload(qc + 1, wait=[(pe, None)])
                    par = qc % 2
                    dse = d2 if par == 0 else d3
                    w = qc // 4
                    p.wait("tensor", dse, mr[qc])
                    for h in range(H2):
                        hb_, hg = (h % 2) * 64, h // 2
                        qsl = qb2[par][hb_:hb_ + 64, hg * 128:(hg + 1) * 128]
                        for half in range(2):
                            p.do("tensor", "matmul",
                                 psA[:, 1024 + half * 512:1536 + half * 512],
                                 qsl,
                                 k_fm[hb_:hb_ + 64,
                                      hg * 2560 + w * 512 + half * 512:
                                      hg * 2560 + w * 512 + half * 512 + 512],
                                 start=True, stop=True,
                                 wait=[(dv, None)] if half == 0 else None,
                                 inc=(pe, 1) if half == 1 else None)
                        p.do("vector", "tensor_add", se[:, 0:512],
                             psA[:, 1024:1536], mt[par][:, 0:512],
                             wait=[(pe, None)])
                        p.do("vector", "tensor_add", se[:, 512:1024],
                             psA[:, 1536:2048], mt[par][:, 512:1024])
                        p.do("vector", "reduce_max", mx1a[:, 0:1],
                             se[:, 0:512], AX.X)
                        p.do("vector", "reduce_max", mx1b[:, 0:1],
                             se[:, 512:1024], AX.X, drain_inc=(dv, 1))
                        p.do("vector", "tensor_single_scalar", mx2[:, 0:1],
                             mx1a[:, 0:1], mx1b[:, 0:1], ALU.max,
                             wait=[(dv, None)], drain_inc=(dv, 1))
                        p.do("vector", "tensor_single_scalar", nmx[:, 0:1],
                             mx2[:, 0:1], -1.0, ALU.mult, drain_inc=(dv, 1))
                        p.do("scalar", "activation", pb[:, 0:512],
                             se[:, 0:512], AF.Exp, bias=nmx[:, 0:1],
                             wait=[(dv, None)], drain_inc=(ac, 1))
                        p.do("scalar", "activation", pb[:, 512:1024],
                             se[:, 512:1024], AF.Exp, bias=nmx[:, 0:1],
                             drain_inc=(ac, 1))
                        p.do("vector", "reduce_sum", sm1a[:, 0:1],
                             pb[:, 0:512], AX.X, wait=[(ac, None)])
                        p.do("vector", "reduce_sum", sm1b[:, 0:1],
                             pb[:, 512:1024], AX.X, drain_inc=(dv, 1))
                        p.do("vector", "tensor_single_scalar", smt[:, 0:1],
                             sm1a[:, 0:1], sm1b[:, 0:1], ALU.add,
                             wait=[(dv, None)], drain_inc=(dv, 1))
                        p.do("vector", "reciprocal", rcp[:, 0:1], smt[:, 0:1],
                             drain_inc=(dv, 1))
                        for kc in range(8):
                            tps = psB[:, 1024 + (kc % 2) * 512:
                                      1024 + (kc % 2) * 512 + 64].bitcast(BF16)
                            p.do("tensor", "transpose", tps,
                                 pb[:, kc * 128:(kc + 1) * 128], G["idt"],
                                 wait=[(dv, None)], inc=(pe, 1))
                            p.do("vector", "tensor_copy",
                                 ptb[:, kc * 128:(kc + 1) * 128], tps,
                                 wait=[(pe, None)], drain_inc=(dv, 1))
                        for kc in range(8):
                            p.do("tensor", "matmul", psB[:, 0:64],
                                 ptb[:, kc * 128:(kc + 1) * 128],
                                 var_[:, (4 * w + kc) * 512 + h * 64:
                                      (4 * w + kc) * 512 + h * 64 + 64],
                                 start=(kc == 0), stop=(kc == 7),
                                 wait=[(dv, None), (ac, None)]
                                 if kc == 0 else None,
                                 inc=(pe, 1) if kc == 7 else None)
                        p.do("scalar", "activation",
                             otm[:, h * 64:(h + 1) * 64], psB[:, 0:64],
                             AF.Copy, scale=rcp[:, 0:1],
                             wait=[(pe, None)], drain_inc=(ac, 1))
                    if DEBUG_TAPS and l == 0 and qc == 0:
                        for nm, tile in (("dbgotm", otm[:]),
                                         ("dbgqb", qb2[par][:]),
                                         ("dbgv0", var_[:, 0:4096]),
                                         ("dbgmt", mt[par][:]),
                                         ("dbgkf", k_fm[:, 0:2560])):
                            dt_ = nc.dram_tensor("l2f_" + nm, list(tile.shape),
                                                 BF16, kind="ExternalOutput").ap()
                            p.do("sync", "dma_start", out=dt_[:, :], in_=tile,
                                 wait=[(ac, None), (dv, None)], inc=(st_s, 16))
                        p.wait("vector", st_s, None)
                        p.wait("scalar", st_s, None)
                    for hg in range(4):
                        tps = psB[:, 1024 + (hg % 2) * 512:
                                  1024 + (hg % 2) * 512 + 64].bitcast(BF16)
                        p.do("tensor", "transpose", tps,
                             otm[:, hg * 128:(hg + 1) * 128], G["idt"],
                             wait=[(dv, None), (ac, None)], inc=(pe, 1))
                        p.do("vector", "tensor_copy",
                             o_fm[hg][:, qc * 128:(qc + 1) * 128], tps,
                             wait=[(pe, None)], drain_inc=(dv, 1))
            p.barrier()
            kvstk.close()
            with ExitStack() as stk:
                wo = emit_weight_load(p, stk, f"l2f_wo{l}", inp["l2f_wo"][l],
                                      n_in=4)
                p.wait("tensor", ac, None)
                p.wait("tensor", dv, None)
                evac = {}
                for co in range(NCH):
                    ps = psA if co % 2 == 0 else psB
                    for tt in range(NTT):
                        sl = slice(tt * TT, (tt + 1) * TT)
                        ww = [(dv, evac[co - 2])] if (co >= 2 and tt == 0)                             else None
                        for ci in range(4):
                            p.do("tensor", "matmul", ps[:, sl],
                                 wo[:, ci * 1024 + co * 128:
                                    ci * 1024 + (co + 1) * 128],
                                 o_fm[ci][:, sl], start=(ci == 0),
                                 stop=(ci == 3),
                                 wait=ww if ci == 0 else None,
                                 inc=(pe, 1) if (tt == NTT - 1 and ci == 3)
                                 else None)
                    p.do("vector", "tensor_add", G["xs"][co], G["xs"][co],
                         ps[:, 0:T], wait=[(pe, None)], inc=(dv, 1))
                    p.do("scalar", "activation", G["xb"][co], G["xs"][co],
                         AF.Copy, wait=[(dv, None)], inc=(ac, 1))
                    evac[co] = dv.n
                p.wait("tensor", ac, None)
            p.barrier()
            ostk.close()
            lnb = [p.sbuf(lstk, f"l2f_ln{c}_{l}", [128, T], BF16)
                   for c in range(NCH)]
            h2stk = ExitStack()
            h2f = [p.sbuf(h2stk, f"l2f_h2{j}_{l}", [128, T], F32)
                   for j in range(4)]
            with ExitStack() as stk:
                bg = p.sbuf(stk, f"l2f_bg2{l}", [128, 512], F32)
                bcolt = p.sbuf(stk, f"l2f_bcol2{l}", [128, 4], F32)
                gcol = p.sbuf(stk, f"l2f_gcol{l}", [128, 8], F32)
                bcl = p.sbuf(stk, f"l2f_bcl{l}", [128, 8], F32)
                tmp = p.sbuf(stk, f"l2f_tmpl{l}", [128, T], F32)
                p.do("sync", "dma_start", out=gcol[:],
                     in_=inp["l2f_lgc"][l, :, :], inc=(ld, 16))
                p.do("sync", "dma_start", out=bcl[:],
                     in_=inp["l2f_lbc"][l, :, :], inc=(ld, 16))
                bic = bnd_in.rearrange("a (b c) -> (a b) c", c=1)
                for j in range(4):
                    p.do("sync", "dma_start",
                         out=bic[j * 128:(j + 1) * 128, :],
                         in_=G["xs"][4 + j][:, T - 1:T],
                         wait=[(dv, None)] if j == 0 else None,
                         inc=(st_s, 16))
                p.wait("gpsimd", st_s, None)
                p.wait("gpsimd", cc, None)
                p.do("gpsimd", "collective_compute", "AllGather", ALU.bypass,
                     replica_groups=GROUPS8, ins=[bnd_in.opt()],
                     outs=[bnd_out.opt()], inc=(cc, 1))
                p.do("gpsimd", "memset", bg[:], 0.0, wait=[(cc, None)])
                p.do("gpsimd", "indirect_dma_start", out=bg[:], out_offset=None,
                     in_=bnd_out[:, :],
                     in_offset=IndirectOffsetOnAxis(ap=bidx[:, 0:1], axis=0),
                     bounds_check=p.bound_reg(7), oob_is_err=False,
                     inc=(gp, 16))
                brc = bnd_rt.rearrange("a (b c) -> (a b) c", c=1)
                for j in range(4):
                    p.do("sync", "dma_start", out=bnd_rt[0:1, :],
                         in_=bg[0:1, j * 128:(j + 1) * 128],
                         wait=[(gp, None)] if j == 0 else [(ld, None)],
                         inc=(st_s, 16))
                    p.do("sync", "dma_start", out=bcolt[:, j:j + 1],
                         in_=brc[0:128, :], wait=[(st_s, None)], inc=(ld, 16))
                for j in range(4):
                    p.do("vector", "tensor_copy", h2f[j][:, 1:T],
                         G["xs"][4 + j][:, 0:T - 1],
                         wait=[(ld, None)] if j == 0 else None)
                    p.do("vector", "tensor_copy", h2f[j][:, 0:1],
                         bcolt[:, j:j + 1], drain_inc=(dv, 1))
                r_chunks = [G["xs"][c] for c in range(4)] +                            [t[:] for t in h2f]
                p.wait("scalar", ld, None)
                ln_affine_out(p, G, r_chunks, gcol[:], bcl[:],
                              [t[:] for t in lnb], tmp[:],
                              ready=[(dv, dv.n)])
            p.barrier()
            h2stk.close()
            with ExitStack() as stk:
                w1t = [p.sbuf(stk, f"l2f_w1t{i}_{l}", [128, 8 * 512], BF16)
                       for i in range(1)]
                w2t = [p.sbuf(stk, f"l2f_w2t{i}_{l}", [128, 22 * 128], BF16)
                       for i in range(2)]
                h_fm = p.sbuf(stk, f"l2f_hfm{l}", [128, 22 * 512], BF16)
                gt = p.sbuf(stk, f"l2f_gt{l}", [128, 256], BF16)
                ht = p.sbuf(stk, f"l2f_ht{l}", [128, 256], BF16)
                b1s = p.sbuf(stk, f"l2f_b1s{l}", [1, 512], F32)
                b1b = p.sbuf(stk, f"l2f_b1b{l}", [1, 5632], BF16)
                b2c = p.sbuf(stk, f"l2f_b2cs{l}", [128, 8], F32)
                p.do("sync", "dma_start", out=b2c[:],
                     in_=inp["l2f_b2c"][l, :, :], inc=(ld, 16))
                for i in range(11):
                    p.do("sync", "dma_start", out=b1s[:],
                         in_=inp["l2f_b1p"][l, :, i * 512:(i + 1) * 512],
                         wait=[(ac, None)], inc=(ld, 16))
                    p.do("scalar", "activation",
                         b1b[0:1, i * 512:(i + 1) * 512], b1s[:], AF.Copy,
                         wait=[(ld, None)], drain_inc=(ac, 1))
                p.wait("tensor", ac, None)
                p.wait("vector", ld, None)

                def w1load(i, wait=None):
                    for ci in range(NCH):
                        p.do("sync", "dma_start",
                             out=w1t[0][:, ci * 512:(ci + 1) * 512],
                             in_=w1b[ci * 128:(ci + 1) * 128,
                                     i * 512:(i + 1) * 512],
                             wait=(wait if ci == 0 else None),
                             inc=(d0, 16))
                    return d0.n

                def w2load(co, wait=None):
                    par = co % 2
                    dse = d2 if par == 0 else d3
                    for t2 in range(22):
                        p.do("sync", "dma_start",
                             out=w2t[par][:, t2 * 128:(t2 + 1) * 128],
                             in_=w2b[t2 * 128:(t2 + 1) * 128,
                                     co * 128:(co + 1) * 128],
                             wait=(wait if t2 == 0 else None),
                             inc=(dse, 16))
                    return dse.n

                for sc in range(4):
                    wr = {}
                    for i in range(11):
                        wr[i] = w1load(i, wait=[(pe, None), (d2, None),
                                                (d3, None)])
                        par = 0
                        dse = d0
                        for t2 in range(4):
                            tch = sc * 4 + t2
                            csl = slice(tch * 128, (tch + 1) * 128)
                            rega = psA[:, (t2 % 2) * 1024:
                                       (t2 % 2) * 1024 + 256]
                            regg = psA[:, (t2 % 2) * 1024 + 512:
                                       (t2 % 2) * 1024 + 512 + 256]
                            for gi, reg in ((0, rega), (1, regg)):
                                terms = [(lnb[ci][:, csl],
                                          w1t[0][:, ci * 512 + gi * 256:
                                                  ci * 512 + gi * 256 + 256])
                                         for ci in range(NCH)]
                                terms.append(
                                    (G["onesrowb"][0:1, 0:128],
                                     b1b[0:1, i * 512 + gi * 256:
                                         i * 512 + gi * 256 + 256]))
                                for ti, (lt, rt) in enumerate(terms):
                                    p.do("tensor", "matmul", reg, lt, rt,
                                         start=(ti == 0),
                                         stop=(ti == len(terms) - 1),
                                         wait=[(dse, wr[i]), (dv, None),
                                               (ac, None)]
                                         if (ti == 0 and gi == 0 and t2 == 0)
                                         else ([(dse, wr[i])]
                                               if (ti == 0 and gi == 0)
                                               else None),
                                         inc=(pe, 1)
                                         if ti == len(terms) - 1 else None)
                            p.do("scalar", "activation", gt[:], regg, AF.Gelu,
                                 wait=[(pe, None)], drain_inc=(ac, 1))
                            p.do("vector", "tensor_mul", ht[:], rega, gt[:],
                                 wait=[(ac, None)], drain_inc=(dv, 1))
                            for jh in range(2):
                                tps = psB[:, 1024 + jh * 512:
                                          1024 + jh * 512 + 64].bitcast(BF16)
                                p.do("tensor", "transpose", tps,
                                     ht[:, jh * 128:(jh + 1) * 128], G["idt"],
                                     wait=[(dv, None)], inc=(pe, 1))
                                p.do("vector", "tensor_copy",
                                     h_fm[:, (2 * i + jh) * 512 + t2 * 128:
                                          (2 * i + jh) * 512 + (t2 + 1) * 128],
                                     tps, wait=[(pe, None)],
                                     drain_inc=(dv, 1))
                    w2r = {0: w2load(0, wait=[(pe, None)])}
                    for co in range(NCH):
                        if co + 1 < NCH:
                            w2r[co + 1] = w2load(co + 1, wait=[(pe, None)])
                        par = co % 2
                        dse = d2 if par == 0 else d3
                        reg = (psA if co % 2 == 0 else psB)[:, 0:512]
                        for t2 in range(22):
                            p.do("tensor", "matmul", reg,
                                 w2t[par][:, t2 * 128:(t2 + 1) * 128],
                                 h_fm[:, t2 * 512:(t2 + 1) * 512],
                                 start=(t2 == 0), stop=(t2 == 21),
                                 wait=[(dse, w2r[co]), (dv, None)]
                                 if t2 == 0 else None,
                                 inc=(pe, 1) if t2 == 21 else None)
                        p.do("vector", "scalar_tensor_tensor",
                             G["xs"][co][:, sc * 512:(sc + 1) * 512], reg,
                             1.0, G["xs"][co][:, sc * 512:(sc + 1) * 512],
                             ALU.mult, ALU.add, wait=[(pe, None)])
                        p.do("vector", "tensor_single_scalar",
                             G["xs"][co][:, sc * 512:(sc + 1) * 512],
                             G["xs"][co][:, sc * 512:(sc + 1) * 512],
                             b2c[:, co:co + 1], ALU.add, drain_inc=(dv, 1))
                for co in range(NCH):
                    p.do("scalar", "activation", G["xb"][co], G["xs"][co],
                         AF.Copy, wait=[(dv, None)] if co == 0 else None,
                         inc=(ac, 1))
                p.wait("tensor", ac, None)
                p.wait("vector", ac, None)
                p.wait("sync", dv, None)
            p.barrier()
            lstk.close()
        with ExitStack() as stk:
            xot = p.sbuf(stk, "l2f_xot", [128, T], BF16)
            tmp = p.sbuf(stk, "l2f_tmpf", [128, T], F32)
            for c in range(NCH):
                p.do("sync", "dma_start", out=xot[:],
                     in_=xodr[c * 128:(c + 1) * 128, :],
                     wait=[(st_s, None), (dv, None), (ac, None)]
                     if c == 0 else [(dv, None)], inc=(ld, 16))
                p.do("vector", "tensor_add", G["xs"][c], G["xs"][c],
                     xot[:], wait=[(ld, None)], drain_inc=(dv, 1))
            ln_feature_major(p, G, G["xs"], G["xs"], G["xb"], tmp[:],
                             ready=[(dv, dv.n)])
    p.barrier()


def build_l2_attn(p, G):
    nc = p.nc
    ld, st_s, pe, dv, ac, gp, cc = [p.sem(s) for s in
                                    ("ld", "st", "pe", "dv", "ac", "gp", "cc")]
    psA, psB = G["psA"], G["psB"]
    inp = G["inp"]
    GROUPS2 = [[0, 1, 2, 3], [4, 5, 6, 7]]
    NCA = 17
    xodr = nc.dram_tensor("l2a_xod", [1024, T], BF16).ap()
    xidr = nc.dram_tensor("l2a_xid", [T, 512], BF16).ap()
    xig = nc.dram_tensor("l2a_xig", [8192, 512], BF16).ap()
    odr = [nc.dram_tensor(f"l2a_od{b}", [NCA * 128, 256], BF16).ap()
           for b in range(2)]
    d0, d1 = p.sem("d0"), p.sem("d1")
    d2, d3 = p.sem("d2"), p.sem("d3")

    with ExitStack() as stk0:
        for c in range(NCH):
            p.do("sync", "dma_start", out=xodr[c * 128:(c + 1) * 128, :],
                 in_=G["xb"][c], inc=(st_s, 16))
        with ExitStack() as stk:
            xi = [p.sbuf(stk, f"l2a_xi{i}", [128, T], BF16) for i in range(4)]
            dbc = p.sbuf(stk, "l2a_dbc_s", [128, 4], F32)
            n1c = p.sbuf(stk, "l2a_n1c_s", [128, 4], F32)
            tmp = p.sbuf(stk, "l2a_tmp0", [128, T], F32)
            xstg = p.sbuf(stk, "l2a_xstg", [128, 512], BF16)
            p.do("sync", "dma_start", out=dbc[:], in_=inp["l2a_dbc"][:, :],
                 inc=(ld, 16))
            p.do("sync", "dma_start", out=n1c[:], in_=inp["l2a_n1c"][:, :],
                 inc=(ld, 16))
            with ExitStack() as wstk:
                wdb = emit_weight_load(p, wstk, "l2a_dw", inp["l2a_dwT"],
                                       n_out=512)
                p.wait("tensor", ac, None)
                p.wait("vector", ld, None)
                evac = {}
                for co in range(4):
                    ps = psA if co % 2 == 0 else psB
                    for tt in range(NTT):
                        sl = slice(tt * TT, (tt + 1) * TT)
                        ww = [(dv, evac[co - 2])] if (co >= 2 and tt == 0)                             else ([(dv, None)] if (co == 0 and tt == 0) else None)
                        for ci in range(NCH):
                            p.do("tensor", "matmul", ps[:, sl],
                                 wdb[:, ci * 512 + co * 128:
                                    ci * 512 + (co + 1) * 128],
                                 G["xb"][ci][:, sl], start=(ci == 0),
                                 stop=(ci == NCH - 1),
                                 wait=ww if ci == 0 else None,
                                 inc=(pe, 1) if (tt == NTT - 1 and ci == NCH - 1)
                                 else None)
                    p.do("vector", "tensor_single_scalar",
                         (tmp[:] if co == 0 else tmp[:]), ps[:, 0:T],
                         dbc[:, co:co + 1], ALU.add, wait=[(pe, None)],
                         drain_inc=(dv, 1))
                    p.do("scalar", "activation", xi[co][:], tmp[:], AF.Copy,
                         wait=[(dv, None)], drain_inc=(ac, 1))
                    evac[co] = dv.n
            p.wait("tensor", ac, None)
            for co in range(4):
                p.do("scalar", "activation", tmp[:], xi[co][:], AF.Square,
                     drain_inc=(ac, 1))
                for tt in range(NTT):
                    sl = slice(tt * TT, (tt + 1) * TT)
                    p.do("tensor", "matmul", psA[0:1, sl], G["ones"][:],
                         tmp[:, sl], start=(co == 0), stop=(co == 3),
                         wait=[(ac, None)] if tt == 0 else None,
                         inc=(pe, 1) if tt == NTT - 1 else None)
                if co < 3:
                    p.wait("scalar", pe, pe.n)
            rowrs = G["rowrs"]
            p.do("vector", "tensor_scalar_mul", rowrs[0:1, :],
                 psA[0:1, 0:2048], 1.0 / 512, wait=[(pe, None)])
            p.do("vector", "tensor_single_scalar", rowrs[0:1, :],
                 rowrs[0:1, :], 1e-6, ALU.add, drain_inc=(dv, 1))
            p.do("scalar", "activation", rowrs[0:1, :], rowrs[0:1, :], AF.Ln,
                 wait=[(dv, None)], drain_inc=(ac, 1))
            p.do("scalar", "activation", rowrs[0:1, :], rowrs[0:1, :], AF.Exp,
                 scale=-0.5, drain_inc=(ac, 1))
            p.wait("tensor", ac, None)
            for tt in range(NTT):
                sl = slice(tt * TT, (tt + 1) * TT)
                p.do("tensor", "matmul", psB[:, sl], G["onesrow"][0:1, :],
                     rowrs[0:1, sl], start=True, stop=True,
                     inc=(pe, 1) if tt == NTT - 1 else None)
            p.wait("vector", pe, pe.n)
            for co in range(4):
                p.do("vector", "tensor_mul", tmp[:], xi[co][:], psB[:, 0:2048])
                p.do("vector", "tensor_single_scalar", xi[co][:], tmp[:],
                     n1c[:, co:co + 1], ALU.mult, drain_inc=(dv, 1))
            for tch in range(16):
                for j in range(4):
                    tps = psA[:, (j % 2) * 512:(j % 2) * 512 + 64].bitcast(BF16)
                    p.do("tensor", "transpose", tps,
                         xi[j][:, tch * 128:(tch + 1) * 128], G["idt"],
                         wait=[(dv, None)], inc=(pe, 1))
                    p.do("vector", "tensor_copy",
                         xstg[:, j * 128:(j + 1) * 128], tps,
                         wait=[(pe, None)], drain_inc=(dv, 1))
                p.do("sync", "dma_start",
                     out=xidr[tch * 128:(tch + 1) * 128, :], in_=xstg[:],
                     wait=[(dv, None)], inc=(d2, 16))
                p.wait("vector", d2, None)
        p.wait("gpsimd", d2, None)
        p.wait("gpsimd", cc, None)
        p.do("gpsimd", "collective_compute", "AllGather", ALU.bypass,
             replica_groups=GROUPS2, ins=[xidr.opt()], outs=[xig.opt()],
             inc=(cc, 1))
        for b in range(2):
            HW = 256
            with ExitStack() as stk:
                gidx = p.sbuf(stk, f"l2a_gidxs{b}", [128, NCA], U32)
                q_fm = [p.sbuf(stk, f"l2a_qfm{i}_{b}", [128, NCA * 128], BF16)
                        for i in range(4)]
                k_fm = [p.sbuf(stk, f"l2a_kfm{i}_{b}", [128, NCA * 128], BF16)
                        for i in range(4)]
                vtm = p.sbuf(stk, f"l2a_vtm{b}", [128, NCA * 256], BF16)
                ga = p.sbuf(stk, f"l2a_ga{b}", [128, 512], BF16)
                otm = p.sbuf(stk, f"l2a_otm{b}", [128, 256], BF16)
                mt = [p.sbuf(stk, f"l2a_mt{i}_{b}", [128, 384], F32)
                      for i in range(2)]
                pb_ = p.sbuf(stk, f"l2a_pb{b}", [128, 384], BF16)
                ptb = p.sbuf(stk, f"l2a_ptb{b}", [128, 384], BF16)
                se = p.sbuf(stk, f"l2a_se{b}", [128, 384], F32)
                mx2 = p.sbuf(stk, f"l2a_mx2{b}", [128, 1], F32)
                nmx = p.sbuf(stk, f"l2a_nmx{b}", [128, 1], F32)
                smt = p.sbuf(stk, f"l2a_smt{b}", [128, 1], F32)
                rcp = p.sbuf(stk, f"l2a_rcp{b}", [128, 1], F32)
                qbc = p.sbuf(stk, f"l2a_qbcs{b}", [128, 10], F32)
                pbc = p.sbuf(stk, f"l2a_pbcs{b}", [128, 2], F32)
                wqv = emit_weight_load(p, stk, f"l2a_wqkv{b}",
                                       inp[f"l2a_qkvT{b}"], n_in=2, n_out=1280)
                wpv = emit_weight_load(p, stk, f"l2a_wp{b}",
                                       inp[f"l2a_pwT{b}"], n_in=2, n_out=256)
                p.do("sync", "dma_start", out=gidx[:],
                     in_=inp[f"l2a_gidx{b}"][:, :], inc=(ld, 16))
                p.do("sync", "dma_start", out=qbc[:],
                     in_=inp[f"l2a_qbc{b}"][:, :], inc=(ld, 16))
                p.do("sync", "dma_start", out=pbc[:],
                     in_=inp[f"l2a_pbc{b}"][:, :], inc=(ld, 16))
                p.wait("gpsimd", ld, None)
                p.wait("gpsimd", cc, None)
                p.wait("tensor", ac, None)
                for ci2 in range(NCA):
                    p.do("gpsimd", "memset", ga[:], 0.0,
                         wait=[(dv, None), (pe, None)])
                    p.do("gpsimd", "indirect_dma_start", out=ga[:],
                         out_offset=None, in_=xig[:, :],
                         in_offset=IndirectOffsetOnAxis(
                             ap=gidx[:, ci2:ci2 + 1], axis=0),
                         bounds_check=p.bound_reg(8191), oob_is_err=False,
                         inc=(gp, 16))
                    for j in range(2):
                        tps = psA[:, 1024 + j * 512:1024 + j * 512 + 64]                             .bitcast(BF16)
                        p.do("tensor", "transpose", tps,
                             ga[:, b * 256 + j * 128:b * 256 + (j + 1) * 128],
                             G["idt"], wait=[(gp, None)], inc=(pe, 1))
                        p.do("vector", "tensor_copy",
                             ga[:, 256 + j * 128:256 + (j + 1) * 128]
                             if b == 0 else ga[:, j * 128:(j + 1) * 128],
                             tps, wait=[(pe, None)], drain_inc=(dv, 1))
                    xfm = (ga[:, 256:512] if b == 0 else ga[:, 0:256])
                    for rnd in range(3):
                        fos = list(range(rnd * 4, min(rnd * 4 + 4, 10)))
                        for fi2, fo in enumerate(fos):
                            reg = (psA if fi2 % 2 == 0 else psB)[
                                :, (fi2 // 2) * 512:(fi2 // 2) * 512 + 128]
                            for ci in range(2):
                                p.do("tensor", "matmul", reg,
                                     wqv[:, ci * 1280 + fo * 128:
                                        ci * 1280 + (fo + 1) * 128],
                                     xfm[:, ci * 128:(ci + 1) * 128],
                                     start=(ci == 0), stop=(ci == 1),
                                     wait=[(dv, None)] if (ci == 0 and fi2 == 0)
                                     else None, inc=(pe, 1) if ci == 1 else None)
                        for fi2, fo in enumerate(fos):
                            reg = (psA if fi2 % 2 == 0 else psB)[
                                :, (fi2 // 2) * 512:(fi2 // 2) * 512 + 128]
                            if fo < 8:
                                dsts = q_fm[fo] if fo < 4 else k_fm[fo - 4]
                                p.do("vector", "tensor_single_scalar",
                                     dsts[:, ci2 * 128:(ci2 + 1) * 128],
                                     reg, qbc[:, fo:fo + 1], ALU.add,
                                     wait=[(pe, None)] if fi2 == 0 else None,
                                     drain_inc=(dv, 1))
                            else:
                                sub = fo - 8
                                p.do("vector", "tensor_single_scalar",
                                     se[:, 0:128], reg, qbc[:, fo:fo + 1],
                                     ALU.add,
                                     wait=[(pe, None)] if fi2 == 0 else None,
                                     drain_inc=(dv, 1))
                                p.do("scalar", "activation", pb_[:, 0:128],
                                     se[:, 0:128], AF.Copy,
                                     wait=[(dv, None)], drain_inc=(ac, 1))
                                tps = psB[:, 1536:1600].bitcast(BF16)
                                p.do("tensor", "transpose", tps, pb_[:, 0:128],
                                     G["idt"], wait=[(ac, None), (dv, None)],
                                     inc=(pe, 1))
                                p.do("vector", "tensor_copy",
                                     vtm[:, ci2 * 256 + sub * 128:
                                         ci2 * 256 + (sub + 1) * 128], tps,
                                     wait=[(pe, None)], drain_inc=(dv, 1))
                SCL = (1.0 / np.sqrt(32.0))

                def mload(ci2, h, wait=None):
                    par = (ci2 * 8 + h) % 2
                    dse = d0 if par == 0 else d1
                    p.do("sync", "dma_start", out=mt[par][:],
                         in_=inp[f"l2a_mt{b}"][ci2, h, :, :],
                         wait=wait, inc=(dse, 16))
                    return dse.n

                mr = {0: mload(0, 0, wait=[(dv, None)])}
                for ci2 in range(NCA):
                    for h in range(8):
                        idx2 = ci2 * 8 + h
                        if idx2 + 1 < NCA * 8:
                            mr[idx2 + 1] = mload((idx2 + 1) // 8,
                                                 (idx2 + 1) % 8,
                                                 wait=[(dv, None)])
                        par = idx2 % 2
                        dse = d0 if par == 0 else d1
                        hb_, ht = (h % 2) * 64, h // 2
                        kb0 = min(max(ci2 - 1, 0), NCA - 3)
                        qsl = q_fm[ht][hb_:hb_ + 32,
                                       ci2 * 128:(ci2 + 1) * 128]
                        p.do("tensor", "matmul", psA[:, 1024:1408],
                             qsl,
                             k_fm[ht][hb_:hb_ + 32,
                                      kb0 * 128:(kb0 + 3) * 128],
                             start=True, stop=True, wait=[(dv, None)],
                             inc=(pe, 1))
                        p.do("vector", "scalar_tensor_tensor", se[:],
                             psA[:, 1024:1408], SCL, mt[par][:],
                             ALU.mult, ALU.add,
                             wait=[(pe, None), (dse, mr[idx2])])
                        p.do("vector", "reduce_max", mx2[:, 0:1], se[:],
                             AX.X, drain_inc=(dv, 1))
                        p.do("vector", "tensor_single_scalar", nmx[:, 0:1],
                             mx2[:, 0:1], -1.0, ALU.mult, wait=[(dv, None)],
                             drain_inc=(dv, 1))
                        p.do("scalar", "activation", pb_[:], se[:], AF.Exp,
                             bias=nmx[:, 0:1], wait=[(dv, None)],
                             drain_inc=(ac, 1))
                        p.do("vector", "reduce_sum", smt[:, 0:1], pb_[:],
                             AX.X, wait=[(ac, None)], drain_inc=(dv, 1))
                        p.do("vector", "reciprocal", rcp[:, 0:1], smt[:, 0:1],
                             wait=[(dv, None)], drain_inc=(dv, 1))
                        for kc in range(3):
                            tps = psB[:, 1024 + (kc % 2) * 512:
                                      1024 + (kc % 2) * 512 + 64].bitcast(BF16)
                            p.do("tensor", "transpose", tps,
                                 pb_[:, kc * 128:(kc + 1) * 128], G["idt"],
                                 wait=[(dv, None)], inc=(pe, 1))
                            p.do("vector", "tensor_copy",
                                 ptb[:, kc * 128:(kc + 1) * 128], tps,
                                 wait=[(pe, None)], drain_inc=(dv, 1))
                        for kc in range(3):
                            p.do("tensor", "matmul", psB[:, 0:32],
                                 ptb[:, kc * 128:(kc + 1) * 128],
                                 vtm[:, (kb0 + kc) * 256 + h * 32:
                                     (kb0 + kc) * 256 + h * 32 + 32],
                                 start=(kc == 0), stop=(kc == 2),
                                 wait=[(dv, None), (ac, None)]
                                 if kc == 0 else None,
                                 inc=(pe, 1) if kc == 2 else None)
                        p.do("scalar", "activation",
                             otm[:, h * 32:(h + 1) * 32], psB[:, 0:32],
                             AF.Copy, scale=rcp[:, 0:1], wait=[(pe, None)],
                             drain_inc=(ac, 1))
                    for j in range(2):
                        tps = psA[:, 1536 + j * 256:1536 + j * 256 + 64]                             .bitcast(BF16)
                        p.do("tensor", "transpose", tps,
                             otm[:, j * 128:(j + 1) * 128], G["idt"],
                             wait=[(dv, None), (ac, None)], inc=(pe, 1))
                        p.do("vector", "tensor_copy",
                             se[:, 0:128].bitcast(BF16)[:, 0:128]
                             if False else pb_[:, j * 128:(j + 1) * 128],
                             tps, wait=[(pe, None)], drain_inc=(dv, 1))
                    for j in range(2):
                        reg = psB[:, 512:640] if j == 0 else psB[:, 1536:1664]
                        for ci in range(2):
                            p.do("tensor", "matmul", reg,
                                 wpv[:, ci * 256 + j * 128:
                                    ci * 256 + (j + 1) * 128],
                                 pb_[:, ci * 128:(ci + 1) * 128],
                                 start=(ci == 0), stop=(ci == 1),
                                 wait=[(dv, None)] if (ci == 0 and j == 0)
                                 else None,
                                 inc=(pe, 1) if ci == 1 else None)
                        p.do("vector", "tensor_single_scalar", se[:, 0:128],
                             reg, pbc[:, j:j + 1], ALU.add,
                             wait=[(pe, None)], drain_inc=(dv, 1))
                        p.do("scalar", "activation", ptb[:, j * 128:(j + 1) * 128],
                             se[:, 0:128], AF.Copy, wait=[(dv, None)],
                             drain_inc=(ac, 1))
                    for j in range(2):
                        tps = psA[:, 1536 + j * 256:1536 + j * 256 + 64]                             .bitcast(BF16)
                        p.do("tensor", "transpose", tps,
                             ptb[:, j * 128:(j + 1) * 128], G["idt"],
                             wait=[(ac, None), (dv, None)], inc=(pe, 1))
                        p.do("vector", "tensor_copy",
                             otm[:, j * 128:(j + 1) * 128], tps,
                             wait=[(pe, None)], drain_inc=(dv, 1))
                    p.do("sync", "dma_start",
                         out=odr[b][ci2 * 128:(ci2 + 1) * 128, :],
                         in_=otm[:, 0:256], wait=[(dv, None)], inc=(d2, 16))
                    p.wait("vector", d2, None)
                    p.wait("scalar", d2, None)
        with ExitStack() as stk:
            oidx = [p.sbuf(stk, f"l2a_oidxs{b}", [128, 16], U32)
                    for b in range(2)]
            gb = p.sbuf(stk, "l2a_gb", [128, 256], BF16)
            xc = p.sbuf(stk, "l2a_xc", [128, 512], BF16)
            xcf = [p.sbuf(stk, f"l2a_xcf{i}", [128, T], BF16) for i in range(4)]
            ubc = p.sbuf(stk, "l2a_ubc_s", [128, 8], F32)
            p.do("sync", "dma_start", out=ubc[:], in_=inp["l2a_ubc"][:, :],
                 inc=(ld, 16))
            for b in range(2):
                p.do("sync", "dma_start", out=oidx[b][:],
                     in_=inp[f"l2a_oidx{b}"][:, :], inc=(ld, 16))
            p.wait("gpsimd", ld, None)
            p.wait("gpsimd", d2, None)
            for tch in range(16):
                for b in range(2):
                    p.do("gpsimd", "indirect_dma_start", out=gb[:],
                         out_offset=None, in_=odr[b][:, :],
                         in_offset=IndirectOffsetOnAxis(
                             ap=oidx[b][:, tch:tch + 1], axis=0),
                         bounds_check=p.bound_reg(NCA * 128 - 1),
                         oob_is_err=False,
                         wait=[(dv, None)], inc=(gp, 16))
                    p.do("vector", "tensor_copy",
                         xc[:, b * 256:(b + 1) * 256], gb[:],
                         wait=[(gp, None)], drain_inc=(dv, 1))
                for j in range(4):
                    tps = psA[:, 1024 + (j % 2) * 512:
                              1024 + (j % 2) * 512 + 64].bitcast(BF16)
                    p.do("tensor", "transpose", tps,
                         xc[:, j * 128:(j + 1) * 128], G["idt"],
                         wait=[(dv, None)], inc=(pe, 1))
                    p.do("vector", "tensor_copy",
                         xcf[j][:, tch * 128:(tch + 1) * 128], tps,
                         wait=[(pe, None)], drain_inc=(dv, 1))
            with ExitStack() as wstk:
                wub = emit_weight_load(p, wstk, "l2a_up", inp["l2a_upT"],
                                       n_in=4)
                p.wait("tensor", ac, None)
                p.wait("tensor", dv, None)
                evac = {}
                for co in range(NCH):
                    ps = psA if co % 2 == 0 else psB
                    for tt in range(NTT):
                        sl = slice(tt * TT, (tt + 1) * TT)
                        ww = [(dv, evac[co - 2])] if (co >= 2 and tt == 0)                             else None
                        for ci in range(4):
                            p.do("tensor", "matmul", ps[:, sl],
                                 wub[:, ci * 1024 + co * 128:
                                    ci * 1024 + (co + 1) * 128],
                                 xcf[ci][:, sl], start=(ci == 0),
                                 stop=(ci == 3),
                                 wait=ww if ci == 0 else None,
                                 inc=(pe, 1) if (tt == NTT - 1 and ci == 3)
                                 else None)
                    p.do("vector", "tensor_single_scalar", ps[:, 0:T],
                         ps[:, 0:T], ubc[:, co:co + 1], ALU.add,
                         wait=[(pe, None)]) if False else None
                    p.do("vector", "scalar_tensor_tensor", G["xs"][co],
                         ps[:, 0:T], 1.0, G["xs"][co], ALU.mult, ALU.add,
                         wait=[(pe, None)])
                    p.do("vector", "tensor_single_scalar", G["xs"][co],
                         G["xs"][co], ubc[:, co:co + 1], ALU.add,
                         inc=(dv, 1))
                    p.do("scalar", "activation", G["xb"][co], G["xs"][co],
                         AF.Copy, wait=[(dv, None)], inc=(ac, 1))
                    evac[co] = dv.n
                p.wait("tensor", ac, None)
        p.barrier()
        with ExitStack() as stk:
            hfm = [p.sbuf(stk, f"l2a_hf{c}", [128, T], BF16)
                   for c in range(NCH)]
            hmid = [p.sbuf(stk, f"l2a_hm{i}", [128, T], BF16)
                    for i in range(2)]
            n2c = p.sbuf(stk, "l2a_n2c_s", [128, 8], F32)
            fb1 = p.sbuf(stk, "l2a_fb1_s", [128, 16], F32)
            fb2 = p.sbuf(stk, "l2a_fb2_s", [128, 8], F32)
            tmp = p.sbuf(stk, "l2a_tmp2", [128, T], F32)
            p.do("sync", "dma_start", out=n2c[:], in_=inp["l2a_n2c"][:, :],
                 inc=(ld, 16))
            p.do("sync", "dma_start", out=fb1[:], in_=inp["l2a_fb1c"][:, :],
                 inc=(ld, 16))
            p.do("sync", "dma_start", out=fb2[:], in_=inp["l2a_fb2c"][:, :],
                 inc=(ld, 16))
            p.wait("vector", ld, None)
            for co in range(NCH):
                p.do("scalar", "activation", tmp[:], G["xs"][co], AF.Square,
                     drain_inc=(ac, 1))
                for tt in range(NTT):
                    sl = slice(tt * TT, (tt + 1) * TT)
                    p.do("tensor", "matmul", psA[0:1, sl], G["ones"][:],
                         tmp[:, sl], start=(co == 0), stop=(co == NCH - 1),
                         wait=[(ac, None)] if tt == 0 else None,
                         inc=(pe, 1) if tt == NTT - 1 else None)
                if co < NCH - 1:
                    p.wait("scalar", pe, pe.n)
            rowrs = G["rowrs"]
            p.do("vector", "tensor_scalar_mul", rowrs[0:1, :],
                 psA[0:1, 0:2048], 1.0 / D, wait=[(pe, None)])
            p.do("vector", "tensor_single_scalar", rowrs[0:1, :],
                 rowrs[0:1, :], 1e-6, ALU.add, drain_inc=(dv, 1))
            p.do("scalar", "activation", rowrs[0:1, :], rowrs[0:1, :], AF.Ln,
                 wait=[(dv, None)], drain_inc=(ac, 1))
            p.do("scalar", "activation", rowrs[0:1, :], rowrs[0:1, :], AF.Exp,
                 scale=-0.5, drain_inc=(ac, 1))
            p.wait("tensor", ac, None)
            for tt in range(NTT):
                sl = slice(tt * TT, (tt + 1) * TT)
                p.do("tensor", "matmul", psB[:, sl], G["onesrow"][0:1, :],
                     rowrs[0:1, sl], start=True, stop=True,
                     inc=(pe, 1) if tt == NTT - 1 else None)
            p.wait("vector", pe, pe.n)
            for co in range(NCH):
                p.do("vector", "tensor_mul", tmp[:], G["xs"][co],
                     psB[:, 0:2048])
                p.do("vector", "tensor_single_scalar", hfm[co][:], tmp[:],
                     n2c[:, co:co + 1], ALU.mult, drain_inc=(dv, 1))
            w1d = nc.dram_tensor("l2a_w1d", [1024, 2048], BF16).ap()
            w2d = nc.dram_tensor("l2a_w2d", [2048, 1024], BF16).ap()
            fb1r = p.sbuf(stk, "l2a_fb1r_s", [1, 2048], F32)
            fb1b = p.sbuf(stk, "l2a_fb1b_s", [1, 2048], BF16)
            p.do("sync", "dma_start", out=fb1r[:], in_=inp["l2a_fb1r"][:, :],
                 inc=(ld, 16))
            p.do("scalar", "activation", fb1b[:], fb1r[:], AF.Copy,
                 wait=[(ld, None)], drain_inc=(ac, 1))
            with ExitStack() as wstk:
                wst = p.sbuf(wstk, "l2a_wst2", [128, 2048], F32)
                wsb = p.sbuf(wstk, "l2a_wsb2", [128, 2048], BF16)
                for ci in range(NCH):
                    p.do("sync", "dma_start", out=wst[:],
                         in_=inp["l2a_fw1"][ci * 128:(ci + 1) * 128, :],
                         wait=[(ac, None)], inc=(ld, 16))
                    p.do("scalar", "activation", wsb[:], wst[:], AF.Copy,
                         wait=[(ld, None)], drain_inc=(ac, 1))
                    p.do("sync", "dma_start",
                         out=w1d[ci * 128:(ci + 1) * 128, :], in_=wsb[:],
                         wait=[(ac, None)], inc=(d3, 16))
                for ci in range(16):
                    p.do("sync", "dma_start", out=wst[:, 0:1024],
                         in_=inp["l2a_fw2"][ci * 128:(ci + 1) * 128, :],
                         wait=[(ac, None)], inc=(ld, 16))
                    p.do("scalar", "activation", wsb[:, 0:1024],
                         wst[:, 0:1024], AF.Copy, wait=[(ld, None)],
                         drain_inc=(ac, 1))
                    p.do("sync", "dma_start",
                         out=w2d[ci * 128:(ci + 1) * 128, :],
                         in_=wsb[:, 0:1024], wait=[(ac, None)], inc=(d3, 16))
                p.wait("sync", d3, None)
                p.wait("vector", ac, None)
            with ExitStack() as fstk:
                w1t = p.sbuf(fstk, "l2a_w1t", [128, 8 * 512], BF16)
                w2t = p.sbuf(fstk, "l2a_w2t", [128, 16 * 128], BF16)
                h1 = p.sbuf(fstk, "l2a_h1", [128, 2048], BF16)
                h1f = p.sbuf(fstk, "l2a_h1f", [128, 16 * 128], BF16)
                for tch in range(16):
                    csl = slice(tch * 128, (tch + 1) * 128)
                    for rnd in range(4):
                        for ci in range(NCH):
                            p.do("sync", "dma_start",
                                 out=w1t[:, ci * 512:(ci + 1) * 512],
                                 in_=w1d[ci * 128:(ci + 1) * 128,
                                         rnd * 512:(rnd + 1) * 512],
                                 wait=[(pe, None)] if ci == 0 else None,
                                 inc=(d0, 16))
                        reg = psA[:, (rnd % 2) * 1024:(rnd % 2) * 1024 + 512]
                        terms = [(hfm[ci][:, csl],
                                  w1t[:, ci * 512:(ci + 1) * 512])
                                 for ci in range(NCH)]
                        terms.append((G["onesrowb"][0:1, 0:128],
                                      fb1b[0:1, rnd * 512:(rnd + 1) * 512]))
                        for ti, (lt, rt) in enumerate(terms):
                            p.do("tensor", "matmul", reg, lt, rt,
                                 start=(ti == 0), stop=(ti == len(terms) - 1),
                                 wait=[(d0, None), (dv, None), (ac, None)]
                                 if ti == 0 else None,
                                 inc=(pe, 1) if ti == len(terms) - 1 else None)
                        p.do("scalar", "activation",
                             h1[:, rnd * 512:(rnd + 1) * 512], reg, AF.Gelu,
                             wait=[(pe, None)], drain_inc=(ac, 1))
                    for jh in range(16):
                        tps = psB[:, 1024 + (jh % 2) * 512:
                                  1024 + (jh % 2) * 512 + 64].bitcast(BF16)
                        p.do("tensor", "transpose", tps,
                             h1[:, jh * 128:(jh + 1) * 128], G["idt"],
                             wait=[(ac, None), (dv, None)] if jh == 0
                             else [(dv, None)], inc=(pe, 1))
                        p.do("vector", "tensor_copy",
                             h1f[:, jh * 128:(jh + 1) * 128], tps,
                             wait=[(pe, None)], drain_inc=(dv, 1))
                    for co in range(NCH):
                        for t2 in range(16):
                            p.do("sync", "dma_start",
                                 out=w2t[:, t2 * 128:(t2 + 1) * 128],
                                 in_=w2d[t2 * 128:(t2 + 1) * 128,
                                         co * 128:(co + 1) * 128],
                                 wait=[(pe, None)] if t2 == 0 else None,
                                 inc=(d1, 16))
                        reg = (psA if co % 2 == 0 else psB)[:, 0:128]
                        for t2 in range(16):
                            p.do("tensor", "matmul", reg,
                                 w2t[:, t2 * 128:(t2 + 1) * 128],
                                 h1f[:, t2 * 128:(t2 + 1) * 128],
                                 start=(t2 == 0), stop=(t2 == 15),
                                 wait=[(d1, None), (dv, None)]
                                 if t2 == 0 else None,
                                 inc=(pe, 1) if t2 == 15 else None)
                        p.do("vector", "scalar_tensor_tensor",
                             G["xs"][co][:, csl], reg, 1.0,
                             G["xs"][co][:, csl], ALU.mult, ALU.add,
                             wait=[(pe, None)])
                        p.do("vector", "tensor_single_scalar",
                             G["xs"][co][:, csl], G["xs"][co][:, csl],
                             fb2[:, co:co + 1], ALU.add, drain_inc=(dv, 1))
        with ExitStack() as stk:
            xot = p.sbuf(stk, "l2a_xot", [128, T], BF16)
            tmp = p.sbuf(stk, "l2a_tmpf", [128, T], F32)
            for c in range(NCH):
                p.do("sync", "dma_start", out=xot[:],
                     in_=xodr[c * 128:(c + 1) * 128, :],
                     wait=[(st_s, None), (dv, None), (ac, None), (pe, None)]
                     if c == 0 else [(dv, None)], inc=(ld, 16))
                p.do("vector", "tensor_add", G["xs"][c], G["xs"][c],
                     xot[:], wait=[(ld, None)], drain_inc=(dv, 1))
            ln_feature_major(p, G, G["xs"], G["xs"], G["xb"], tmp[:],
                             ready=[(dv, dv.n)])
    p.barrier()


def build_l0_ffn(p, G):
    ld, pe, dv, ac = p.sem("ld"), p.sem("pe"), p.sem("dv"), p.sem("ac")
    psA, psB = G["psA"], G["psB"]
    with ExitStack() as stk:
        wtf = p.sbuf(stk, "l0f_wtf", [128, 1024], F32)
        wtb = p.sbuf(stk, "l0f_wtb", [128, NCH * 1024], BF16)
        tmp = p.sbuf(stk, "l0f_tmp", [128, T], F32)
        wt = G["inp"]["L0f_wt"]
        for ci in range(NCH):
            p.do("sync", "dma_start", out=wtf[:], in_=wt[ci * 128:(ci + 1) * 128, :],
                 wait=[(ac, None)] if ci > 0 else None, inc=(ld, 16))
            p.do("scalar", "sign", wtb[:, ci * 1024:(ci + 1) * 1024], wtf[:],
                 wait=[(ld, None)], drain_inc=(ac, 1))
        evac_dv = []
        for co in range(NCH):
            ps = psA if co % 2 == 0 else psB
            waits = [(ac, None)] if co == 0 else None
            if co >= 2:
                waits = [(dv, evac_dv[co - 2])]
            for tt in range(NTT):
                sl = slice(tt * TT, (tt + 1) * TT)
                for ci in range(NCH):
                    p.do("tensor", "matmul", ps[:, sl],
                         wtb[:, ci * 1024 + co * 128: ci * 1024 + (co + 1) * 128],
                         G["xb"][ci][:, sl],
                         start=(ci == 0), stop=(ci == NCH - 1),
                         wait=waits if (tt == 0 and ci == 0) else None,
                         inc=(pe, 1) if (tt == NTT - 1 and ci == NCH - 1) else None)
            p.do("vector", "tensor_add", G["xs"][co], G["xs"][co], ps[:, 0:2048],
                 wait=[(pe, None)], inc=(dv, 1))
            evac_dv.append(dv.n)
        ln_feature_major(p, G, G["xs"], G["xs"], G["xb"], tmp[:],
                         ready=[(dv, dv.n)])
    p.barrier()



def build_program(block_names):
    nc = bass.Bass("TRN2", target_bir_lowering=False,
                   detect_race_conditions=RACE_DETECT)
    p = Plan(nc)
    G = {"inp": {}, "p": p}

    xin = nc.dram_tensor("x_fm", [D, T], F32, kind="ExternalInput").ap()
    yout = nc.dram_tensor("y_fm", [D, T], F32, kind="ExternalOutput").ap()
    G["inp"]["x_fm"] = xin
    need = set()
    for b in block_names:
        need |= BLOCK_INPUTS[b]
    need.add("id128"); need.add("u128c")
    for name, spec in INPUT_SPECS.items():
        if name in need:
            shape, dt = (spec, F32) if not isinstance(spec[0], (list, tuple)) else (spec[0], spec[1])
            G["inp"][name] = nc.dram_tensor(name, list(shape), dt, kind="ExternalInput").ap()

    st = p.stack
    xs = [p.sbuf(st, f"xs{c}", [128, T], F32) for c in range(NCH)]
    xb = [p.sbuf(st, f"xb{c}", [128, T], BF16) for c in range(NCH)]
    G["xs"] = [t[:] for t in xs]
    G["xb"] = [t[:] for t in xb]
    ones = p.sbuf(st, "ones_f32", [128, 1], F32)
    onesrow = p.sbuf(st, "onesrow", [1, 128], F32)
    eps1 = p.sbuf(st, "eps1", [1, 1], F32)
    rowmu = p.sbuf(st, "rowmu", [1, 2048], F32)
    rowrs = p.sbuf(st, "rowrs", [1, 2048], F32)
    G["ones"], G["onesrow"], G["eps1"] = ones[:], onesrow[:], eps1[:]
    G["rowmu"] = rowmu[:]
    G["rowrs"] = rowrs[:]
    G["psA"] = st.enter_context(nc.psum_tensor("psA", [128, 2048], F32))
    G["psB"] = st.enter_context(nc.psum_tensor("psB", [128, 2048], F32))
    idt = p.sbuf(st, "idt", [128, 128], BF16)
    u128 = p.sbuf(st, "u128", [128, 128], BF16)
    onesrowb = p.sbuf(st, "onesrowb", [1, 128], BF16)
    onescolb = p.sbuf(st, "onescolb", [128, 1], BF16)
    G["onesrowb"] = onesrowb[:]
    G["onescolb"] = onescolb[:]
    eps128 = p.sbuf(st, "eps128", [128, 1], F32)
    G["idt"], G["u128"], G["eps128"] = idt[:], u128[:], eps128[:]

    dv, ld, st_sem, ac = p.sem("dv"), p.sem("ld"), p.sem("st"), p.sem("ac")

    p.do("vector", "memset", ones[:], 1.0)
    p.do("vector", "memset", onesrow[:], 1.0)
    p.do("vector", "memset", eps128[:], 1e-5)
    p.do("vector", "memset", onesrowb[:], 1.0)
    p.do("vector", "memset", onescolb[:], 1.0)
    p.do("vector", "memset", eps1[:], 1e-5, drain_inc=(dv, 1))
    p.do("sync", "dma_start", out=idt[:], in_=G["inp"]["id128"][:, :], inc=(ld, 16))
    p.do("sync", "dma_start", out=u128[:], in_=G["inp"]["u128c"][:, :], inc=(ld, 16))

    for c in range(NCH):
        p.do("sync", "dma_start", out=xs[c][:], in_=xin[c * 128:(c + 1) * 128, :],
             inc=(ld, 16))
    p.wait("scalar", ld, ld.n)
    for c in range(NCH):
        p.do("scalar", "activation", xb[c][:], xs[c][:], AF.Copy, inc=(ac, 1))
    p.barrier()

    for name in block_names:
        try:
            BLOCK_BUILDERS[name](p, G)
        except _SkipRest:
            p.barrier()

    for c in range(NCH):
        p.do("sync", "dma_start", out=yout[c * 128:(c + 1) * 128, :], in_=xs[c][:],
             inc=(st_sem, 16))
    p.wait("sync", st_sem, st_sem.n)

    with nc.Block() as block:
        @block.sync
        def _(eng):
            for f in p.ops["sync"]:
                f(eng)

        @block.vector
        def _(eng):
            for f in p.ops["vector"]:
                f(eng)

        @block.scalar
        def _(eng):
            for f in p.ops["scalar"]:
                f(eng)

        @block.tensor
        def _(eng):
            for f in p.ops["tensor"]:
                f(eng)

        @block.gpsimd
        def _(eng):
            for f in p.ops["gpsimd"]:
                f(eng)
    p.stack.close()
    return nc


BLOCK_BUILDERS = {
    "l0_attn": build_l0_attn,
    "l0_ffn": build_l0_ffn,
    "l1_attn": build_l1_attn,
    "l1_ffn": build_l1_ffn,
    "l3_ffn": build_l3_ffn,
    "l3_attn": build_l3_attn,
    "l2_ffn": build_l2_ffn,
    "l2_attn": build_l2_attn,
}
BLOCK_INPUTS = {
    "l0_attn": {"L0a_wqT", "L0a_wkT", "L0a_wvT", "L0a_woT"},
    "l0_ffn": {"L0f_wt"},
    "l1_attn": {"l1a_wq", "l1a_wk", "l1a_wv", "l1a_wg", "l1a_wgk1", "l1a_wgk2",
                "l1a_bgk2", "l1a_wout", "l1a_rmsw", "l1a_pidx"},
    "l1_ffn": {"l1f_wr", "l1f_mask", "l1f_ecap", "l1f_tok2", "l1f_w1",
               "l1f_wu", "l1f_w2"},
    "l3_ffn": {"l3f_wr", "l3f_grb", "l3f_mask", "l3f_ecap", "l3f_tok2",
               "l3f_w1", "l3f_w2", "l3f_b1t", "l3f_b2t"},
    "l3_attn": {"l3a_wqT", "l3a_wkT", "l3a_wvT", "l3a_woT", "l3a_bqc",
                "l3a_bkc", "l3a_bkvr", "l3a_boc", "l3a_gma", "l3a_bta",
                "l1a_pidx"},
    "l2_ffn": {"l2f_wqkv", "l2f_qs", "l2f_ks", "l2f_wo", "l2f_lgc", "l2f_lbc",
               "l2f_w1p", "l2f_w2p", "l2f_b1p", "l2f_b2c", "l2f_rotc",
               "l2f_amask", "l2f_bidx", "l2f_kvidx"},
    "l2_attn": {"l2a_dwT", "l2a_dbc", "l2a_n1c", "l2a_n2c", "l2a_qkvT0",
                "l2a_qkvT1", "l2a_qbc0", "l2a_qbc1", "l2a_pwT0", "l2a_pwT1",
                "l2a_pbc0", "l2a_pbc1", "l2a_mt0", "l2a_mt1", "l2a_gidx0",
                "l2a_gidx1", "l2a_oidx0", "l2a_oidx1", "l2a_upT", "l2a_ubc",
                "l2a_fw1", "l2a_fb1c", "l2a_fb1r", "l2a_fw2", "l2a_fb2c"},
}
INPUT_SPECS = {
    "L0a_wqT": (D, D),
    "L0a_wkT": (D, D),
    "L0a_wvT": (D, D),
    "L0a_woT": (D, D),
    "L0f_wt": (D, D),
    "id128": ((128, 128), BF16),
    "u128c": ((128, 128), BF16),
    "l1a_wq": (D, 512),
    "l1a_wk": (D, 512),
    "l1a_wv": (D, D),
    "l1a_wg": (D, D),
    "l1a_wgk1": (D, 16),
    "l1a_wgk2": (16, 512),
    "l1a_bgk2": (1, 512),
    "l1a_wout": (D, D),
    "l1a_rmsw": (1, 256),
    "l1a_pidx": ((128, 3), U32),
    "l1f_wr": (D, 32),
    "l1f_mask": (8, 1),
    "l1f_ecap": (32, 1),
    "l1f_tok2": ((128, 32), I32),
    "l1f_w1": (4, D, D),
    "l1f_wu": (4, D, D),
    "l1f_w2": (4, D, D),
    "l3f_wr": (D, 32),
    "l3f_grb": (1, 32),
    "l3f_mask": (8, 1),
    "l3f_ecap": (32, 1),
    "l3f_tok2": ((128, 32), I32),
    "l3f_w1": (1, D, 4096),
    "l3f_w2": (1, 4096, D),
    "l3f_b1t": (1, 128, 32),
    "l3f_b2t": (1, 128, 8),
    "l3a_wqT": (D, D),
    "l3a_wkT": (D, D),
    "l3a_wvT": (D, D),
    "l3a_woT": (D, D),
    "l3a_bqc": (128, 8),
    "l3a_bkc": (128, 8),
    "l3a_bkvr": (1, 2048),
    "l3a_boc": (128, 8),
    "l3a_gma": (128, 8),
    "l3a_bta": (128, 8),
    "l2f_wqkv": (2, 1024, 1536),
    "l2f_qs": (2, 64),
    "l2f_ks": (2, 64),
    "l2f_wo": (2, 512, 1024),
    "l2f_lgc": (2, 128, 8),
    "l2f_lbc": (2, 128, 8),
    "l2f_w1p": (2, 1024, 5632),
    "l2f_w2p": (2, 2816, 1024),
    "l2f_b1p": (2, 1, 5632),
    "l2f_b2c": (2, 128, 8),
    "l2f_rotc": ((2048, 2048), F32),
    "l2f_amask": ((2048, 1024), F32),
    "l2f_bidx": ((128, 1), U32),
    "l2f_kvidx": ((128, 4), U32),
    "l2a_dwT": (1024, 512),
    "l2a_dbc": (128, 4),
    "l2a_n1c": (128, 4),
    "l2a_n2c": (128, 8),
    "l2a_qkvT0": (256, 1280),
    "l2a_qkvT1": (256, 1280),
    "l2a_qbc0": (128, 10),
    "l2a_qbc1": (128, 10),
    "l2a_pwT0": (256, 256),
    "l2a_pwT1": (256, 256),
    "l2a_pbc0": (128, 2),
    "l2a_pbc1": (128, 2),
    "l2a_mt0": ((17, 8, 128, 384), F32),
    "l2a_mt1": ((17, 8, 128, 384), F32),
    "l2a_gidx0": ((128, 17), U32),
    "l2a_gidx1": ((128, 17), U32),
    "l2a_oidx0": ((128, 16), U32),
    "l2a_oidx1": ((128, 16), U32),
    "l2a_upT": (512, 1024),
    "l2a_ubc": (128, 8),
    "l2a_fw1": (1024, 2048),
    "l2a_fb1c": (128, 16),
    "l2a_fb1r": (1, 2048),
    "l2a_fw2": (2048, 1024),
    "l2a_fb2c": (128, 8),
}
ALL_BLOCKS = ["l0_attn", "l0_ffn", "l1_attn", "l1_ffn", "l2_attn",
              "l2_ffn", "l3_attn", "l3_ffn"]



def host_inputs(inputs):
    import ml_dtypes
    tr = lambda a: np.ascontiguousarray(np.asarray(a, np.float32).T)
    asf = lambda a: np.ascontiguousarray(np.asarray(a, np.float32))
    shared = {
        "L0f_wt": tr(inputs["L0_ffn_weight"]),
        "L0a_wqT": tr(inputs["L0_attn_w_q"]),
        "L0a_wkT": tr(inputs["L0_attn_w_k"]),
        "L0a_wvT": tr(inputs["L0_attn_w_v"]),
        "L0a_woT": tr(inputs["L0_attn_w_o"]),
        "id128": np.eye(128, dtype=ml_dtypes.bfloat16),
        "u128c": np.triu(np.ones((128, 128), dtype=ml_dtypes.bfloat16)),
        "l1a_wq": asf(inputs["L1_attn_Wq"]),
        "l1a_wk": asf(inputs["L1_attn_Wk"]),
        "l1a_wv": asf(inputs["L1_attn_Wv"]),
        "l1a_wg": asf(inputs["L1_attn_Wg"]),
        "l1a_wgk1": asf(inputs["L1_attn_Wgk1"]),
        "l1a_wgk2": asf(inputs["L1_attn_Wgk2"]),
        "l1a_bgk2": asf(inputs["L1_attn_bgk2"]).reshape(1, 512),
        "l1a_wout": asf(inputs["L1_attn_Wout"]),
        "l1a_rmsw": asf(inputs["L1_attn_rms_w"]).reshape(1, 256),
        "l1f_wr": asf(inputs["L1_ffn_w_router"]),
        "l3f_wr": np.pad(tr(inputs["L3_ffn_gate_w"]), ((0, 0), (0, 24))),
        "l3f_grb": np.pad(asf(inputs["L3_ffn_gate_b"]).reshape(1, 8),
                          ((0, 0), (0, 24))),
    }
    wqkvT = tr(inputs["L3_attn_Wqkv"])
    b3 = asf(inputs["L3_attn_bqkv"])
    shared.update({
        "l3a_wqT": np.ascontiguousarray(wqkvT[:, 0:1024]),
        "l3a_wkT": np.ascontiguousarray(wqkvT[:, 1024:2048]),
        "l3a_wvT": np.ascontiguousarray(wqkvT[:, 2048:3072]),
        "l3a_woT": tr(inputs["L3_attn_Wout"]),
        "l3a_bqc": np.ascontiguousarray(b3[0:1024].reshape(8, 128).T),
        "l3a_bkc": np.ascontiguousarray(b3[1024:2048].reshape(8, 128).T),
        "l3a_bkvr": np.ascontiguousarray(b3[1024:3072].reshape(1, 2048)),
        "l3a_boc": np.ascontiguousarray(
            asf(inputs["L3_attn_bout"]).reshape(8, 128).T),
        "l3a_gma": np.ascontiguousarray(
            asf(inputs["L3_attn_gamma"]).reshape(8, 128).T),
        "l3a_bta": np.ascontiguousarray(
            asf(inputs["L3_attn_beta"]).reshape(8, 128).T),
    })
    w1 = asf(inputs["L2_ffn_w1"])
    b1 = asf(inputs["L2_ffn_b1"])
    w2 = asf(inputs["L2_ffn_w2"])
    w1p = np.zeros((2, 1024, 5632), np.float32)
    b1p = np.zeros((2, 1, 5632), np.float32)
    ap_ = np.zeros((2, 1024, 2816), np.float32)
    gp_ = np.zeros((2, 1024, 2816), np.float32)
    ap_[:, :, 0:2730] = w1[:, :, 0:2730]
    gp_[:, :, 0:2730] = w1[:, :, 2730:5460]
    ab_ = np.zeros((2, 2816), np.float32)
    gb_ = np.zeros((2, 2816), np.float32)
    ab_[:, 0:2730] = b1[:, 0:2730]
    gb_[:, 0:2730] = b1[:, 2730:5460]
    for i in range(11):
        w1p[:, :, i * 512:i * 512 + 256] = ap_[:, :, i * 256:(i + 1) * 256]
        w1p[:, :, i * 512 + 256:(i + 1) * 512] = gp_[:, :, i * 256:(i + 1) * 256]
        b1p[:, 0, i * 512:i * 512 + 256] = ab_[:, i * 256:(i + 1) * 256]
        b1p[:, 0, i * 512 + 256:(i + 1) * 512] = gb_[:, i * 256:(i + 1) * 256]
    w2p = np.zeros((2, 2816, 1024), np.float32)
    w2p[:, 0:2730, :] = w2
    shared.update({
        "l2f_wqkv": asf(inputs["L2_ffn_wqkv"]),
        "l2f_qs": asf(inputs["L2_ffn_q_scale"]),
        "l2f_ks": asf(inputs["L2_ffn_k_scale"]),
        "l2f_wo": asf(inputs["L2_ffn_wo"]),
        "l2f_lgc": np.ascontiguousarray(
            asf(inputs["L2_ffn_ln_g"]).reshape(2, 8, 128).transpose(0, 2, 1)),
        "l2f_lbc": np.ascontiguousarray(
            asf(inputs["L2_ffn_ln_b"]).reshape(2, 8, 128).transpose(0, 2, 1)),
        "l2f_w1p": w1p, "l2f_w2p": w2p, "l2f_b1p": b1p,
        "l2f_b2c": np.ascontiguousarray(
            asf(inputs["L2_ffn_b2"]).reshape(2, 8, 128).transpose(0, 2, 1)),
    })
    colT = lambda v, n: np.ascontiguousarray(
        asf(v).reshape(n, 128).T)

    def padhd(m):
        out_ = np.zeros(m.shape[:-1] + (512,), np.float32)
        for h in range(8):
            out_[..., h * 64:h * 64 + 32] = m[..., h * 32:(h + 1) * 32]
        return out_

    def padqkv(wv):
        wv = asf(wv)
        return np.ascontiguousarray(np.concatenate(
            [padhd(wv[:, 0:256]), padhd(wv[:, 256:512]), wv[:, 512:768]],
            axis=1))

    def padqbc(bv):
        bv = asf(bv)
        full = np.concatenate([padhd(bv[0:256]), padhd(bv[256:512]),
                               bv[512:768]])
        return np.ascontiguousarray(full.reshape(10, 128).T)
    shared.update({
        "l2a_dwT": asf(inputs["L2_attn_down_w"]),
        "l2a_dbc": colT(inputs["L2_attn_down_b"], 4),
        "l2a_n1c": colT(inputs["L2_attn_norm1_w"], 4),
        "l2a_n2c": colT(inputs["L2_attn_norm2_w"], 8),
        "l2a_qkvT0": padqkv(inputs["L2_attn_qkv1_w"]),
        "l2a_qkvT1": padqkv(inputs["L2_attn_qkv2_w"]),
        "l2a_qbc0": padqbc(inputs["L2_attn_qkv1_b"]),
        "l2a_qbc1": padqbc(inputs["L2_attn_qkv2_b"]),
        "l2a_pwT0": asf(inputs["L2_attn_proj1_w"]),
        "l2a_pwT1": asf(inputs["L2_attn_proj2_w"]),
        "l2a_pbc0": colT(inputs["L2_attn_proj1_b"], 2),
        "l2a_pbc1": colT(inputs["L2_attn_proj2_b"], 2),
        "l2a_upT": asf(inputs["L2_attn_up_w"]),
        "l2a_ubc": colT(inputs["L2_attn_up_b"], 8),
        "l2a_fw1": asf(inputs["L2_attn_ffn_w1"]),
        "l2a_fb1c": colT(inputs["L2_attn_ffn_b1"], 16),
        "l2a_fb1r": asf(inputs["L2_attn_ffn_b1"]).reshape(1, 2048),
        "l2a_fw2": asf(inputs["L2_attn_ffn_w2"]),
        "l2a_fb2c": colT(inputs["L2_attn_ffn_b2"], 8),
    })
    ec1 = np.zeros((32, 1), np.float32)
    ec1[:32, 0] = np.arange(32) * 640.0
    shared["l1f_ecap"] = ec1
    ec3 = np.zeros((32, 1), np.float32)
    ec3[:8, 0] = np.arange(8) * 2560.0
    shared["l3f_ecap"] = ec3
    return shared


def percore_inputs(c, inputs):
    asf = lambda a: np.ascontiguousarray(np.asarray(a, np.float32))
    BIG = np.uint32(0x3FFFFFF)
    g = c // 4
    pidx = np.full((128, 3), BIG, np.uint32)
    for j in range(3):
        r = c - 1 - j
        if r >= g * 4:
            pidx[:, j] = (r * 128 + np.arange(128)).astype(np.uint32)
    mask = np.zeros((8, 1), np.float32)
    mask[:c, 0] = 1.0
    tok = (c * 2048 + np.arange(2048).reshape(16, 128)).astype(np.int32)
    tok2 = np.zeros((128, 32), np.int32)
    for tch in range(16):
        tok2[:, 2 * tch] = tok[tch]
        tok2[:, 2 * tch + 1] = tok[tch]
    out = {"l1a_pidx": pidx,
           "l1f_mask": mask, "l3f_mask": mask,
           "l1f_tok2": tok2, "l3f_tok2": tok2,
           "l1f_w1": asf(inputs["L1_ffn_w_gate"][4 * c:4 * c + 4]),
           "l1f_wu": asf(inputs["L1_ffn_w_up"][4 * c:4 * c + 4]),
           "l1f_w2": asf(inputs["L1_ffn_w_down"][4 * c:4 * c + 4]),
           "l3f_w1": asf(inputs["L3_ffn_w1"][c:c + 1]),
           "l3f_w2": asf(inputs["L3_ffn_w2"][c:c + 1]),
           "l3f_b1t": asf(inputs["L3_ffn_b1"][c]).reshape(32, 128).T.reshape(1, 128, 32),
           "l3f_b2t": asf(inputs["L3_ffn_b2"][c]).reshape(8, 128).T.reshape(1, 128, 8),
           }
    import ml_dtypes
    tl = ((c % 4) * 2048 + np.arange(2048)).astype(np.float64)
    half = np.arange(0, 64, 2, dtype=np.float64)
    inv_freq = 1.0 / 10000.0 ** (half / 64.0)
    fr = np.concatenate([tl[:, None] * inv_freq[None, :]] * 2, axis=1)
    base = (half + 0.4 * 64) / (1.4 * 64)
    power = (tl - 8192 // 2) / 256.0
    sc_ = base[None, :] ** power[:, None]
    sc_ = np.concatenate([sc_, sc_], axis=1)
    cos, sin = np.cos(fr), np.sin(fr)
    sgn = np.ones(64); sgn[0:32] = -1.0
    Aq, Bq = cos * sc_, sgn[None, :] * sin * sc_
    Ak, Bk = cos / sc_, sgn[None, :] * sin / sc_
    rotc = np.concatenate([np.tile(m.astype(np.float32), (1, 8))
                           for m in (Aq, Bq, Ak, Bk)], axis=1)
    out["l2f_rotc"] = rotc
    ii = tl[:, None]
    jj = (ii // 512) * 512 - 512 + np.arange(1024)[None, :]
    am = np.where((jj <= ii) & (jj >= ii - 512) & (jj >= 0), 0.0, -1e30)
    out["l2f_amask"] = am.astype(np.float32)
    bi = np.full((128, 1), BIG, np.uint32)
    kv = np.full((128, 4), BIG, np.uint32)
    if c % 4 > 0:
        bi[0, 0] = c - 1
        for j in range(4):
            kv[:, j] = (c - 1) * 512 + j * 128 + np.arange(128)
    out["l2f_bidx"] = bi
    out["l2f_kvidx"] = kv
    t0 = (c % 4) * 2048
    L = 8200
    for b, (w, s) in enumerate(((10, 5), (20, 10))):
        rpb = asf(inputs["L2_attn_rpb1" if b == 0 else "L2_attn_rpb2"])
        r0 = (t0 - s) % L
        a0 = (r0 // w) * w
        rg = (a0 + np.arange(2176)) % L
        tok = (rg + s) % L
        gi = np.where(tok < 8192, tok, BIG).astype(np.uint32)
        out[f"l2a_gidx{b}"] = np.ascontiguousarray(
            gi.reshape(17, 128).T)
        tt_ = t0 + np.arange(2048)
        ar = ((tt_ - s) % L - a0) % L
        out[f"l2a_oidx{b}"] = np.ascontiguousarray(
            ar.astype(np.uint32).reshape(16, 128).T)
        m_ = np.zeros(L)
        m_[L - w:L - s] = 1
        m_[L - s:] = 2
        mt = np.full((17, 8, 128, 384), -1e30, np.float32)
        for ci in range(17):
            qa = ci * 128 + np.arange(128)
            kb0_ = min(max(ci - 1, 0), 14)
            ka = kb0_ * 128 + np.arange(384)
            vk = (ka >= 0) & (ka < 2176)
            rq = rg[qa]
            rk = rg[np.clip(ka, 0, 2175)]
            same = ((rq[:, None] // w) == (rk[None, :] // w)) & vk[None, :]
            ridx = (rq[:, None] % w) - (rk[None, :] % w) + w - 1
            shiftm = np.where(m_[rq][:, None] != m_[rk][None, :], -100.0, 0.0)
            for h in range(8):
                bias = rpb[np.clip(ridx, 0, 2 * w - 2), h] + shiftm
                mt[ci, h] = np.where(same, bias, -1e30).astype(np.float32)
        out[f"l2a_mt{b}"] = mt
    return out


def make_in_maps(inputs, x_full, blocks):
    need = {"id128", "u128c"}
    for b in blocks:
        need |= BLOCK_INPUTS[b]
    shared = {k: v for k, v in host_inputs(inputs).items() if k in need}
    xt = np.ascontiguousarray(np.asarray(x_full, np.float32).reshape(16384, 1024))
    maps = []
    for c in range(NCORES):
        m = dict(shared)
        m.update({k: v for k, v in percore_inputs(c, inputs).items() if k in need})
        m["x_fm"] = np.ascontiguousarray(xt[c * T:(c + 1) * T].T)
        maps.append(m)
    return maps


def run_chain(inputs, x_full, blocks):
    nc = build_program(blocks)
    in_maps = make_in_maps(inputs, x_full, blocks)
    res = run_bass_kernel_spmd(nc, in_maps, core_ids=list(range(NCORES)))
    outs = [np.ascontiguousarray(np.asarray(r["y_fm"]).T) for r in res.results]
    return np.concatenate(outs, axis=0).reshape(2, 8192, 1024)


def kernel(**inputs):
    x = np.asarray(inputs["L0_attn_x"], np.float32)
    return run_chain(inputs, x, ALL_BLOCKS).astype(np.float32)
